# version 11
# baseline (speedup 1.0000x reference)
import numpy as np
import concourse.bass as bass
import concourse.mybir as mybir
from concourse.bass_utils import run_bass_kernel_spmd

F32 = mybir.dt.float32
F32R = mybir.dt.float32r
AL = mybir.AluOpType
AF = mybir.ActivationFunctionType

C = 1024
T = 2048
NCH = 8
G = 512
NG = T // G
SQRTC = float(np.sqrt(1024.0))
EPS_RMS = 1e-6
EPS_LN = 1e-5

ENGS = ("sync", "tensor", "vector", "scalar", "gpsimd")
SEM_OF = dict(sync="sq", tensor="ct", vector="cv", scalar="cs", gpsimd="gq")

DEBUG_TAPS = ()


class Res:
    __slots__ = ("w", "rd")

    def __init__(self):
        self.w = None
        self.rd = []


class Prog:
    def __init__(self):
        self.ops = {e: [] for e in ENGS}
        self.cnt = dict(sq=0, ct=0, cv=0, cs=0, gq=0, cc=0, aq=0,
                        sq0=0, sq1=0, sqw=0, jq=0)
        self.lastdma = {}
        self.lastw = {e: {} for e in ENGS}

    def _wait(self, eng, tok):
        if tok is None:
            return
        s, v = tok
        if v <= self.lastw[eng].get(s, 0):
            return
        self.lastw[eng][s] = v
        self.ops[eng].append(("w", s, v))

    def op(self, eng, fn, reads=(), writes=(), deps=(), sem=None, inc=1):
        if sem is None and inc == 16:
            sem = SEM_OF[eng]
        if inc == 16 and sem in ("gq", "sqw"):
            if sem in self.lastdma:
                deps = list(deps) + [self.lastdma[sem]]
        want = {}
        for r in reads:
            if r.w is not None:
                want[r.w[0]] = max(want.get(r.w[0], 0), r.w[1])
        for r in writes:
            if r.w is not None and not r.rd:
                want[r.w[0]] = max(want.get(r.w[0], 0), r.w[1])
            for t in r.rd:
                want[t[0]] = max(want.get(t[0], 0), t[1])
        for d in deps:
            if d is not None:
                want[d[0]] = max(want.get(d[0], 0), d[1])
        for s_, v_ in want.items():
            self._wait(eng, (s_, v_))
        if sem is None:
            sem = SEM_OF[eng]
        self.cnt[sem] += inc
        tok = (sem, self.cnt[sem])
        self.ops[eng].append(("o", fn, sem, inc))
        for r in reads:
            r.rd.append(tok)
        for r in writes:
            r.w = tok
            r.rd = []
        if inc == 16 and sem in ("gq", "sqw"):
            self.lastdma[sem] = tok
        return tok

    def dma(self, eng, out, in_, reads=(), writes=(), deps=()):
        return self.op(eng, lambda E, o=out, i=in_: E.dma_start(out=o, in_=i),
                       reads=reads, writes=writes, deps=deps, inc=16)


def build_nc(taps=()):
    nc = bass.Bass("TRN2", target_bir_lowering=False)
    P = Prog()

    dI = lambda n, s: nc.dram_tensor(n, s, F32, kind="ExternalInput").ap()
    x_d = dI("x", [C, T])
    wq_d = dI("wq", [C, C])
    wkv_d = dI("wkv", [C, 2 * C])
    wproj_d = dI("wproj", [C, C])
    wc1_d = dI("wc1", [C, 2048])
    wc2_d = dI("wc2", [2048, C])
    wr_d = dI("wr", [C, C])
    wvv_d = dI("wvv", [C, C])
    wo_d = dI("wo", [C, C])
    wst_d = dI("wst", [C, C])
    w2_d = dI("w2", [C, C])
    w1_d = dI("w1", [C, C])
    w3_d = dI("w3", [C, C])
    wk_d = [dI("wk0", [C, 4096]), dI("wk1", [C, 4096])]
    wv_d = [dI("wve0", [4096, C]), dI("wve1", [4096, C])]
    wsk_d = dI("wsk", [C, 8])
    vecs_d = dI("vecs", [128, 80])
    cb1_d = dI("cb1", [128, 16])
    caps_d = dI("caps", [1, 4])
    onesc_d = dI("onesc", [128, 1])
    onesr_d = dI("onesr", [1, 128])
    one1_d = dI("one1", [1, 1])
    sel2a_d = dI("sel2a", [1, 128])
    sel2b_d = dI("sel2b", [1, 128])
    i128_d = dI("i128", [128, 128])

    out_d = nc.dram_tensor("out", [C, T], F32, kind="ExternalOutput").ap()
    tap_d = {}
    for tname, shape in taps:
        tap_d[tname] = nc.dram_tensor("tap_" + tname, shape, F32, kind="ExternalOutput").ap()

    cc_in = nc.dram_tensor("cc_in", [130, 512], F32, kind="Internal").ap()
    cc_out = nc.dram_tensor("cc_out", [130, 512], F32, kind="Internal").ap()
    accA_d = nc.dram_tensor("accA_d", [C, T], F32, kind="Internal").ap()
    s_d = nc.dram_tensor("s_d", [C, T], F32, kind="Internal").ap()
    wsw2p_d = nc.dram_tensor("wsw2p", [C, 2 * C], F32R, kind="Internal").ap()

    A = lambda n, s: nc.alloc_sbuf_tensor(n, s, F32).ap()
    XN = A("XN", [128, 16384])
    BB = A("BB", [128, 16384])
    SCR = A("SCR", [128, 8192])
    WST = A("WST", [128, 1024])
    SC2 = A("SC2", [128, 2048])
    RB = A("RB", [128, 1024])
    KVB = A("KVB", [128, 1780])
    ROWX = A("ROWX", [1, 1536])
    AR = lambda n, s: nc.alloc_sbuf_tensor(n, s, F32R).ap()
    XSP = AR("XSP", [128, 2048])
    WSTR = AR("WSTR", [128, 2048])
    VEC = A("VEC", [128, 80])
    CB1 = A("CB1", [128, 16])
    CAPS = A("CAPS", [128, 4])
    ONESC = A("ONESC", [128, 1])
    ONESR = A("ONESR", [1, 128])
    ONE1 = A("ONE1", [1, 1])
    SELT2 = A("SELT2", [66, 128])
    I128 = A("I128", [128, 128])
    BCOL = A("BCOL", [128, 40])
    WSKT = A("WSKT", [128, 64])
    PS = nc.alloc_psum_tensor("PS", [128, 4096]).ap()

    KV_PHIK = KVB[:, 0:512]
    KV_V = KVB[:, 512:1024]
    KV_SB = KVB[:, 1024:1536]
    KSUM2 = KVB[:, 1536:1552]
    R1C = KVB[:, 1552:1568]
    R1SQC = KVB[:, 1568:1584]
    R1NC = KVB[:, 1584:1600]
    MEC = KVB[:, 1600:1648]
    ROUT = KVB[:, 1648:1776]
    CAPS_STG = KVB[0:1, 1776:1780]
    SIGS = KVB[:, 1024:1088]
    BIDS = [KVB[:, 1088 + 16 * e:1104 + 16 * e] for e in range(3)]
    TMP1 = KVB[:, 1136:1152]
    TMP2 = KVB[:, 1152:1168]
    SELS = [KVB[:, 1168 + 16 * e:1184 + 16 * e] for e in range(3)]
    WCONF = KVB[:, 1216:1232]
    SCL = KVB[:, 1232:1248]

    rXN = [[Res() for _ in range(NG)] for _ in range(NCH)]
    rBB = [[Res() for _ in range(NG)] for _ in range(NCH)]
    rSCR = [Res() for _ in range(16)]
    rWST = [Res() for _ in range(2)]
    rSC2 = [Res() for _ in range(4)]
    rRB = [Res() for _ in range(2)]
    rKV = dict(phik=Res(), v=Res(), kvsb=Res(), ksum2=Res(), r1c=Res(), r1sqc=Res(),
               r1nc=Res(), mec=Res(), rout=Res(), caps=Res(), tmp=Res(), wskt=Res())
    rROWX = [Res() for _ in range(3)]
    rXSP = [Res(), Res()]
    rWSRh = [Res(), Res()]
    rWSRl = [Res(), Res()]
    rBANK = [Res() for _ in range(8)]
    rCONST = Res()
    rBCOL = [Res() for _ in range(5)]
    rDRAM = {n: Res() for n in ("cc_in", "cc_out", "accA_d", "s_d", "wsw2h",
                                "khi0", "klo0", "vhi0", "vlo0", "khi1", "klo1", "vhi1", "vlo1")}

    def bank(b):
        return PS[:, 512 * b:512 * (b + 1)]

    VNAMES = ["s1", "s2", "gamma", "pb", "cb2", "l1w", "l1b", "l2w", "l2b", "pad"]
    VCOL = {n: 8 * i for i, n in enumerate(VNAMES)}

    def vcols(n, c0=0, n_=8):
        return VEC[:, VCOL[n] + c0:VCOL[n] + c0 + n_]

    def vcol(n, c):
        return VEC[:, VCOL[n] + c:VCOL[n] + c + 1]

    def chk(buf, c, g):
        return buf[:, 2048 * c + G * g: 2048 * c + G * g + G]

    def span3(buf, c0, n, g):
        return buf[:, :].rearrange("p (c t) -> p c t", t=2048)[:, c0:c0 + n, G * g:G * (g + 1)]

    def vbc(vap, n, w):
        return vap.unsqueeze(2).to_broadcast([128, n, w])

    def rbc(slot, n):
        return slot.unsqueeze(1).to_broadcast([128, n, G])

    RBS = [RB[:, 512 * i:512 * (i + 1)] for i in range(2)]
    RXs = [ROWX[0:1, 512 * i:512 * (i + 1)] for i in range(3)]

    ld = []
    for dst, src in ((VEC[:, :], vecs_d[:, :]), (CB1[:, :], cb1_d[:, :]),
                     (ONESC[:, :], onesc_d[:, :]), (ONESR[:, :], onesr_d[:, :]),
                     (ONE1[:, :], one1_d[:, :]),
                     (SELT2[0:1, :], sel2a_d[:, :]), (SELT2[1:2, :], sel2b_d[:, :]),
                     (SELT2[32:33, :], sel2a_d[:, :]), (SELT2[33:34, :], sel2b_d[:, :]),
                     (SELT2[64:65, :], sel2a_d[:, :]), (SELT2[65:66, :], sel2b_d[:, :]),
                     (I128[:, :], i128_d[:, :]), (CAPS_STG, caps_d[:, :])):
        ld.append(P.dma("gpsimd", dst, src))
    rCONST.w = ld[-1]
    for c in range(NCH):
        P.dma("gpsimd", XN[:, 2048 * c:2048 * (c + 1)], x_d[128 * c:128 * (c + 1), :],
              writes=[rXN[c][g] for g in range(NG)])
    P.op("tensor", lambda E: E.matmul(PS[:, 0:4], ONESR[0:1, :], CAPS_STG,
                                      start=True, stop=True),
         reads=[rCONST], writes=[rBANK[0]], deps=ld)
    P.op("vector", lambda E: E.tensor_copy(CAPS[:, :], PS[:, 0:4]),
         reads=[rBANK[0]], writes=[rKV["caps"]])

    for g in range(NG):
        for c in range(NCH):
            P.op("tensor", lambda E, c=c, g=g: E.matmul(
                PS[0:1, 0:512], ONESC[:, :], chk(XN, c, g),
                start=(c == 0), stop=(c == NCH - 1)),
                reads=[rXN[c][g], rCONST],
                writes=([rBANK[0]] if c in (0, NCH - 1) else []))
        for hf in range(2):
            P.op("scalar", lambda E, hf=hf, g=g: E.activation(
                SC2[:, :].rearrange("p (c t) -> p c t", t=G)[:, 0:4, :],
                span3(XN, 4 * hf, 4, g), AF.Square),
                reads=[rXN[c][g] for c in range(4 * hf, 4 * hf + 4)],
                writes=rSC2[0:4])
            for c in range(4 * hf, 4 * hf + 4):
                P.op("tensor", lambda E, c=c, g=g: E.matmul(
                    PS[0:1, 512:1024], ONESC[:, :], SC2[:, 512 * (c % 4):512 * (c % 4 + 1)],
                    start=(c == 0), stop=(c == NCH - 1)),
                    reads=[rSC2[c % 4], rCONST],
                    writes=([rBANK[1]] if c in (0, NCH - 1) else []))
        P.op("vector", lambda E: E.tensor_scalar(RXs[0], PS[0:1, 0:512], 1.0 / C, None, AL.mult),
             reads=[rBANK[0]], writes=[rROWX[0]])
        P.op("vector", lambda E: E.tensor_scalar(RXs[1], PS[0:1, 512:1024], 1.0 / C, None, AL.mult),
             reads=[rBANK[1]], writes=[rROWX[1]])
        P.op("vector", lambda E: E.tensor_tensor(RXs[2], RXs[0], RXs[0], AL.mult),
             reads=[rROWX[0]], writes=[rROWX[2]])
        P.op("tensor", lambda E: E.matmul(bank(2), ONESR[0:1, :], RXs[0], start=True, stop=True),
             reads=[rROWX[0], rCONST], writes=[rBANK[2]])
        P.op("vector", lambda E: E.tensor_copy(RBS[0], bank(2)), reads=[rBANK[2]], writes=[rRB[0]])
        P.op("vector", lambda E: E.tensor_tensor(RXs[1], RXs[1], RXs[2], AL.subtract),
             reads=[rROWX[2]], writes=[rROWX[1]])
        P.op("vector", lambda E: E.tensor_scalar(RXs[2], RXs[1], EPS_LN, None, AL.add),
             reads=[rROWX[1]], writes=[rROWX[2]])
        P.op("vector", lambda E: E.reciprocal(RXs[0], RXs[2]),
             reads=[rROWX[2]], writes=[rROWX[0]])
        P.op("scalar", lambda E: E.activation(RXs[2], RXs[0], AF.Sqrt),
             reads=[rROWX[0]], writes=[rROWX[2]])
        P.op("vector", lambda E: E.tensor_tensor(RXs[0], RXs[1], RXs[0], AL.mult),
             reads=[rROWX[1]], writes=[rROWX[0]])
        kq = ((SQRTC + EPS_RMS) / SQRTC) ** 2
        P.op("vector", lambda E: E.reciprocal(RXs[1], RXs[0]),
             reads=[rROWX[0]], writes=[rROWX[1]])
        t_r1row = P.op("scalar", lambda E: E.activation(RXs[1], RXs[1], AF.Sqrt, scale=kq),
                       reads=[], writes=[rROWX[1]])
        for u in range(4):
            P.op("tensor", lambda E, u=u: E.matmul(
                PS[0:128, 2048 + u:2048 + u + 1], RXs[1][0:1, 128 * u:128 * (u + 1)],
                ONE1[0:1, :], start=(u == 0), stop=(u == 3)),
                reads=[rROWX[1], rCONST],
                writes=([rBANK[4]] if u in (0, 3) else []))
        P.op("vector", lambda E, g=g: E.tensor_copy(R1C[:, 4 * g:4 * g + 4], PS[:, 2048:2052]),
             reads=[rBANK[4]], writes=([rKV["r1c"]] if g == NG - 1 else []))
        P.op("vector", lambda E: E.tensor_scalar(RXs[0], RXs[0], EPS_LN, None, AL.add),
             writes=[rROWX[0]])
        P.op("vector", lambda E: E.reciprocal(RXs[0], RXs[0]), writes=[rROWX[0]])
        P.op("scalar", lambda E: E.activation(RXs[0], RXs[0], AF.Sqrt), writes=[rROWX[0]])
        for u in range(4):
            P.op("tensor", lambda E, u=u: E.matmul(
                PS[0:128, 2560 + u:2560 + u + 1], RXs[0][0:1, 128 * u:128 * (u + 1)],
                ONE1[0:1, :], start=(u == 0), stop=(u == 3)),
                reads=[rROWX[0], rCONST],
                writes=([rBANK[5]] if u in (0, 3) else []))
        P.op("vector", lambda E, g=g: E.tensor_copy(R1NC[:, 4 * g:4 * g + 4], PS[:, 2560:2564]),
             reads=[rBANK[5]], writes=([rKV["r1nc"]] if g == NG - 1 else []))
        P.op("tensor", lambda E: E.matmul(bank(3), ONESR[0:1, :], RXs[2], start=True, stop=True),
             reads=[rROWX[2], rCONST], writes=[rBANK[3]])
        P.op("vector", lambda E: E.tensor_copy(RBS[1], bank(3)), reads=[rBANK[3]], writes=[rRB[1]])
        for hf in range(2):
            cres = [rXN[c][g] for c in range(4 * hf, 4 * hf + 4)]
            P.op("vector", lambda E, hf=hf, g=g: E.tensor_tensor(
                span3(XN, 4 * hf, 4, g), span3(XN, 4 * hf, 4, g), rbc(RBS[0], 4), AL.subtract),
                reads=[rRB[0]], writes=cres)
            P.op("vector", lambda E, hf=hf, g=g: E.tensor_tensor(
                span3(XN, 4 * hf, 4, g), span3(XN, 4 * hf, 4, g), rbc(RBS[1], 4), AL.mult),
                reads=[rRB[1]], writes=cres)
    P.op("vector", lambda E: E.tensor_tensor(R1SQC[:, :], R1C[:, :], R1C[:, :], AL.mult),
         reads=[rKV["r1c"]], writes=[rKV["r1sqc"]])

    WKH = SCR[:, 0:4096]
    WVH = SCR[:, 4096:8192]
    kv_mm_tok = []
    for p in range(2):
        t_wk = []
        for half2 in range(2):
            t_wk.append(P.op("sync", lambda E, half2=half2, p=p: E.dma_start(
                out=WKH[:, 2048 * half2:2048 * (half2 + 1)].rearrange("p (u f) -> p u f", f=512),
                in_=wkv_d[:, 512 * p:512 * (p + 1)]
                .rearrange("(u p) f -> p u f", p=128)[:, 4 * half2:4 * half2 + 4, :]),
                writes=rSCR[4 * half2:4 * half2 + 4], inc=16, sem="sqw"))
            t_wk.append(P.op("sync", lambda E, half2=half2, p=p: E.dma_start(
                out=WVH[:, 2048 * half2:2048 * (half2 + 1)].rearrange("p (u f) -> p u f", f=512),
                in_=wkv_d[:, 1024 + 512 * p:1024 + 512 * (p + 1)]
                .rearrange("(u p) f -> p u f", p=128)[:, 4 * half2:4 * half2 + 4, :]),
                writes=rSCR[8 + 4 * half2:8 + 4 * half2 + 4], inc=16, sem="sqw"))
        t_sk = P.op("vector", lambda E: E.tensor_tensor(
            WKH.rearrange("p (c f) -> p c f", f=512), WKH.rearrange("p (c f) -> p c f", f=512),
            vbc(vcols("s1"), 8, 512), AL.mult),
            reads=[rCONST], writes=rSCR[0:4], deps=t_wk)
        t_sv = P.op("vector", lambda E: E.tensor_tensor(
            WVH.rearrange("p (c f) -> p c f", f=512), WVH.rearrange("p (c f) -> p c f", f=512),
            vbc(vcols("s1"), 8, 512), AL.mult),
            reads=[rCONST], writes=rSCR[8:12], deps=t_wk)
        for tc in range(16):
            g, tsub = tc // 4, tc % 4
            xres = [rXN[c][g] for c in range(NCH)]
            for c in range(NCH):
                P.op("tensor", lambda E, c=c, tc=tc: E.matmul(
                    bank(0),
                    XN[:, 2048 * c + 128 * tc:2048 * c + 128 * tc + 128],
                    WKH[:, 512 * c:512 * (c + 1)],
                    start=(c == 0), stop=(c == NCH - 1)),
                    reads=[rXN[c][g], rSCR[c]],
                    writes=([rBANK[0]] if c in (0, NCH - 1) else []))
            P.op("scalar", lambda E, tc=tc: E.activation(
                SC2[:, 0:512], bank(0), AF.Square, scale=R1C[:, tc:tc + 1]),
                reads=[rBANK[0], rKV["r1c"]], writes=[rSC2[0]])
            P.op("scalar", lambda E: E.activation(KV_PHIK, SC2[:, 0:512], AF.Exp, scale=-0.5),
                 reads=[rSC2[0]], writes=[rKV["phik"]])
            for c in range(NCH):
                P.op("tensor", lambda E, c=c, tc=tc: E.matmul(
                    bank(1),
                    XN[:, 2048 * c + 128 * tc:2048 * c + 128 * tc + 128],
                    WVH[:, 512 * c:512 * (c + 1)],
                    start=(c == 0), stop=(c == NCH - 1)),
                    reads=[rXN[c][g], rSCR[8 + c]],
                    writes=([rBANK[1]] if c in (0, NCH - 1) else []))
            P.op("scalar", lambda E, tc=tc: E.activation(
                KV_V, bank(1), AF.Copy, scale=R1C[:, tc:tc + 1]),
                reads=[rBANK[1], rKV["r1c"]], writes=[rKV["v"]])
            for hl in range(8):
                P.op("tensor", lambda E, hl=hl, tc=tc: E.matmul(
                    PS[64 * (hl % 2):64 * (hl % 2) + 64,
                       512 * (4 + hl // 2):512 * (4 + hl // 2) + 64],
                    KV_PHIK[:, 64 * hl:64 * hl + 64],
                    KV_V[:, 64 * hl:64 * hl + 64],
                    start=(tc == 0), stop=(tc == 15)),
                    reads=[rKV["phik"], rKV["v"]],
                    writes=([rBANK[4 + hl // 2]] if tc in (0, 15) else []))
            P.op("tensor", lambda E, tc=tc: E.matmul(
                PS[0:1, 1024:1536], ONESC[:, :], KV_PHIK,
                start=(tc == 0), stop=(tc == 15)),
                reads=[rKV["phik"], rCONST],
                writes=([rBANK[2]] if tc in (0, 15) else []))
        for j in range(4):
            P.op("vector", lambda E, j=j, p=p: E.tensor_copy(
                KV_SB[:, 256 * p + 64 * j:256 * p + 64 * j + 64],
                PS[:, 512 * (4 + j):512 * (4 + j) + 64]),
                reads=[rBANK[4 + j]], writes=([rKV["kvsb"]] if p == 1 and j == 3 else []))
        P.op("vector", lambda E, p=p: E.tensor_copy(RXs[p], PS[0:1, 1024:1536]),
             reads=[rBANK[2]], writes=[rROWX[p]])
    t1 = P.dma("gpsimd", cc_in[0:128, :], KV_SB, reads=[rKV["kvsb"]], writes=[rDRAM["cc_in"]])
    t2 = P.dma("gpsimd", cc_in[128:129, :], RXs[0], reads=[rROWX[0]])
    t3 = P.dma("gpsimd", cc_in[129:130, :], RXs[1], reads=[rROWX[1]])
    t_cc = P.op("gpsimd", lambda E: E.collective_compute(
        "AllReduce", AL.add,
        replica_groups=[[0, 1, 2, 3], [4, 5, 6, 7]],
        ins=[cc_in[:, :]], outs=[cc_out[:, :]]),
        deps=[t1, t2, t3], sem="cc", inc=1)
    t_kvret = P.dma("gpsimd", KV_SB, cc_out[0:128, :], writes=[rKV["kvsb"]], deps=[t_cc])
    t_ks0 = P.dma("gpsimd", RXs[0], cc_out[128:129, :], writes=[rROWX[0]], deps=[t_cc])
    t_ks1 = P.dma("gpsimd", RXs[1], cc_out[129:130, :], writes=[rROWX[1]], deps=[t_cc])
    def wset_hi(s):
        return WSTR[:, 1024 * s:1024 * s + 512]

    def wset_lo(s):
        return WSTR[:, 1024 * s + 512:1024 * s + 1024]

    def xsp_pair(pr):
        return (XSP[:, 1024 * pr:1024 * pr + 512],
                XSP[:, 1024 * pr + 512:1024 * pr + 1024])

    def split_act(src_ap, src_res, pr, scale_col=None):
        XH, XL = xsp_pair(pr)
        if scale_col is None:
            P.op("scalar", lambda E, o=XH, i=src_ap: E.activation(o, i, AF.Copy),
                 reads=[src_res], writes=[rXSP[pr]])
            P.op("vector", lambda E, o=XL, i=src_ap, h=XH: E.tensor_tensor(o, i, h, AL.subtract),
                 reads=[src_res], writes=[rXSP[pr]])
        else:
            tb = SC2[:, 1536:2048]
            P.op("vector", lambda E, i=src_ap, sc=scale_col: E.tensor_scalar(
                tb, i, sc, None, AL.mult),
                reads=[src_res, rCONST], writes=[rSC2[3]])
            P.op("scalar", lambda E, o=XH: E.activation(o, tb, AF.Copy),
                 reads=[rSC2[3]], writes=[rXSP[pr]])
            P.op("vector", lambda E, o=XL, h=XH: E.tensor_tensor(o, tb, h, AL.subtract),
                 reads=[rSC2[3]], writes=[rXSP[pr]])
        return XH, XL

    def wdma_raw_h(wdram, r0, cols0, colh, ki):
        s = colh
        stg = WST[:, 512 * s:512 * s + 512]
        P.op("sync", lambda E, w=wdram, r0=r0, c0=cols0 + 512 * colh, stg=stg: E.dma_start(
            out=stg, in_=w[r0:r0 + 128, c0:c0 + 512]),
            writes=[rWST[s]], inc=16, sem="sqw")

    def wproc_raw_h(colh, ki):
        s = colh
        stg = WST[:, 512 * s:512 * s + 512]
        P.op("scalar", lambda E, s=s, stg=stg: E.activation(wset_hi(s), stg, AF.Copy),
             reads=[rWST[s]], writes=[rWSRh[s]])
        P.op("vector", lambda E, s=s, stg=stg: E.tensor_tensor(
            wset_lo(s), stg, wset_hi(s), AL.subtract),
            reads=[rWST[s], rWSRh[s]], writes=[rWSRl[s]])

    EXPSTG = [WST[:, 0:512], WST[:, 512:1024],
              SCR[:, 4096:4608], SCR[:, 4608:5120]]
    EXPRES_IDX = [("wst", 0), ("wst", 1), ("scr", 8), ("scr", 9)]

    def _expres(st):
        kind, i = EXPRES_IDX[st]
        return rWST[i] if kind == "wst" else rSCR[i]

    def wdma_exp(wdram, r0, cols0, colh, ki):
        st = 2 * (ki % 2) + colh
        P.op("sync", lambda E, w=wdram, r0=r0, c0=cols0 + 512 * colh, st=st: E.dma_start(
            out=EXPSTG[st], in_=w[r0:r0 + 128, c0:c0 + 512]),
            writes=[_expres(st)], inc=16, sem="sqw")

    def wproc_exp(colh, ki):
        st = 2 * (ki % 2) + colh
        s = colh
        P.op("scalar", lambda E, s=s, st=st: E.activation(wset_hi(s), EXPSTG[st], AF.Copy),
             reads=[_expres(st)], writes=[rWSRh[s]])
        P.op("vector", lambda E, s=s, st=st: E.tensor_tensor(
            wset_lo(s), EXPSTG[st], wset_hi(s), AL.subtract),
            reads=[_expres(st), rWSRh[s]], writes=[rWSRl[s]])

    def wload_presplit_h(pk_d, r0, cols0, colh, ki, rdep):
        s = colh
        pc0 = 2 * (cols0 + 512 * colh)
        P.op("sync", lambda E, d=pk_d, r0=r0, pc0=pc0, s=s: E.dma_start(
            out=WSTR[:, 1024 * s:1024 * (s + 1)], in_=d[r0:r0 + 128, pc0:pc0 + 1024]),
            reads=[rdep], writes=[rWSRh[s], rWSRl[s]], inc=16, sem="sq%d" % s)

    def mm3h(ki, nk, colh, XH, XL, rx):
        s = colh
        for j in range(4):
            b = 4 * colh + j
            for p_, (lhs, rhs) in enumerate((
                    (wset_hi(s)[:, 128 * j:128 * (j + 1)], XH),
                    (wset_lo(s)[:, 128 * j:128 * (j + 1)], XH),
                    (wset_hi(s)[:, 128 * j:128 * (j + 1)], XL))):
                P.op("tensor", lambda E, b=b, lhs=lhs, rhs=rhs, ki=ki, nk=nk,
                     p_=p_: E.matmul(
                         bank(b), lhs, rhs,
                         start=(ki == 0 and p_ == 0),
                         stop=(ki == nk - 1 and p_ == 2)),
                     reads=[rWSRh[s], rWSRl[s], rx],
                     writes=([rBANK[b]]
                             if (ki == 0 and p_ == 0) or (ki == nk - 1 and p_ == 2)
                             else []))

    def grp3p(nk, wdma_fn, wproc_fn, act_fn, evacq_fn, defer=False):
        wdma_fn(0, 0)
        wdma_fn(1, 0)
        if wproc_fn is not None:
            wproc_fn(0, 0)
        cur = act_fn(0)
        for ki in range(nk):
            if wproc_fn is not None:
                wproc_fn(1, ki)
            nxt = act_fn(ki + 1) if ki + 1 < nk else None
            pump_job(1)
            mm3h(ki, nk, 0, *cur)
            if ki + 1 < nk:
                wdma_fn(0, ki + 1)
            if ki == nk - 1 and not defer:
                evacq_fn(0)
            if wproc_fn is not None and ki + 1 < nk:
                wproc_fn(0, ki + 1)
            pump_job(1)
            mm3h(ki, nk, 1, *cur)
            if ki + 1 < nk:
                wdma_fn(1, ki + 1)
            cur = nxt
            if ki == nk - 1 and not defer:
                evacq_fn(1)
        if defer:
            evacq_fn(0)
            evacq_fn(1)

    job_items = []
    JSTG = KVB[:, 0:512]
    JHI = KVB[:, 512:1024]
    rJOB = Res()
    jst = {"done": 0, "in": 0}

    def pump_job(n=2):
        for _ in range(n):
            if jst["done"] < jst["in"]:
                src, hi, lo, dres = job_items[jst["done"]]
                t_hi = P.op("scalar", lambda E: E.activation(JHI.bitcast(F32R), JSTG, AF.Copy),
                            reads=[rJOB])
                P.op("gpsimd", lambda E, hi=hi: E.dma_start(out=hi, in_=JHI.bitcast(F32R)),
                     deps=[t_hi], reads=[rJOB], inc=16, sem="jq")
                t_lo = P.op("vector", lambda E: E.tensor_tensor(
                    JSTG.bitcast(F32R), JSTG, JHI.bitcast(F32R), AL.subtract),
                    deps=[t_hi], reads=[rJOB], writes=[rJOB])
                t_st = P.op("gpsimd", lambda E, lo=lo: E.dma_start(out=lo, in_=JSTG.bitcast(F32R)),
                            deps=[t_lo], reads=[rJOB], inc=16, sem="jq")
                if dres is not None:
                    dres.w = t_st
                jst["done"] += 1
            if jst["in"] < len(job_items):
                src, hi, lo, dres = job_items[jst["in"]]
                w = [rJOB] + ([rKV["phik"], rKV["v"]] if jst["in"] == 0 else [])
                P.op("gpsimd", lambda E, src=src: E.dma_start(out=JSTG, in_=src),
                     writes=w, inc=16, sem="jq")
                jst["in"] += 1

    def pump_rest():
        while jst["done"] < len(job_items):
            pump_job(2)

    def regen_bcast_from_cols(cols_ap, cols_res, g, rb_i, pbank):
        for u in range(4):
            P.op("tensor", lambda E, u=u, g=g, cols_ap=cols_ap, pbank=pbank: E.matmul(
                PS[0:1, 512 * pbank + 128 * u:512 * pbank + 128 * (u + 1)],
                cols_ap[:, 4 * g + u:4 * g + u + 1], I128[:, :],
                start=(u == 0), stop=(u == 3)),
                reads=[cols_res, rCONST],
                writes=([rBANK[pbank]] if u in (0, 3) else []))
        t_row = P.op("vector", lambda E, pbank=pbank: E.tensor_copy(
            RXs[2], PS[0:1, 512 * pbank:512 * pbank + 512]),
            reads=[rBANK[pbank]], writes=[rROWX[2]])
        P.op("tensor", lambda E, pbank=pbank: E.matmul(
            PS[:, 512 * pbank:512 * pbank + 512], ONESR[0:1, :], RXs[2],
            start=True, stop=True),
            reads=[rROWX[2], rCONST], writes=[rBANK[pbank]])
        return P.op("vector", lambda E, rb_i=rb_i, pbank=pbank: E.tensor_copy(
            RBS[rb_i], PS[:, 512 * pbank:512 * pbank + 512]),
            reads=[rBANK[pbank]], writes=[rRB[rb_i]])

    for fh in range(2):
        for k in range(NCH):
            t_a = P.op("sync", lambda E, k=k: E.dma_start(
                out=WST[:, 0:1024], in_=wst_d[128 * k:128 * (k + 1), :]),
                writes=[rWST[0], rWST[1]], inc=16, sem="sqw")
            t_b = P.op("sync", lambda E, k=k: E.dma_start(
                out=SCR[:, 0:1024], in_=w2_d[128 * k:128 * (k + 1), :]),
                writes=rSCR[0:2], inc=16, sem="sqw")
            for c in range(NCH):
                P.op("tensor", lambda E, c=c, fh=fh, k=k: E.matmul(
                    bank(c), WST[:, 128 * c:128 * (c + 1)],
                    SCR[:, 512 * fh:512 * fh + 512],
                    start=(k == 0), stop=(k == NCH - 1)),
                    reads=[rWST[0], rWST[1], rSCR[fh]],
                    writes=([rBANK[c]] if k in (0, NCH - 1) else []))
        for c in range(NCH):
            sa = SC2[:, 512 * (2 * (c % 2)):512 * (2 * (c % 2)) + 512]
            sb = SC2[:, 512 * (2 * (c % 2) + 1):512 * (2 * (c % 2) + 1) + 512]
            ra, rb_ = rSC2[2 * (c % 2)], rSC2[2 * (c % 2) + 1]
            P.op("vector", lambda E, c=c, sa=sa: E.tensor_copy(sa, bank(c)),
                 reads=[rBANK[c]], writes=[ra])
            P.op("scalar", lambda E, sa=sa, sb=sb: E.activation(sb.bitcast(F32R), sa, AF.Copy),
                 reads=[ra], writes=[rb_])
            P.dma("gpsimd", wsw2p_d[128 * c:128 * (c + 1), 1024 * fh:1024 * fh + 512],
                  sb.bitcast(F32R), reads=[rb_])
            P.op("vector", lambda E, sa=sa, sb=sb: E.tensor_tensor(
                sa.bitcast(F32R), sa, sb.bitcast(F32R), AL.subtract),
                reads=[rb_], writes=[ra])
            P.dma("gpsimd", wsw2p_d[128 * c:128 * (c + 1), 1024 * fh + 512:1024 * fh + 1024],
                  sa.bitcast(F32R), reads=[ra],
                  writes=([rDRAM["wsw2h"]] if fh == 1 and c == NCH - 1 else []))


    for g in range(NG):
        regen_bcast_from_cols(R1SQC, rKV["r1sqc"], g, 0, 0)

        def ev_q(colh, g=g):
            for f in range(4 * colh, 4 * colh + 4):
                st = SC2[:, 512 * (f % 3):512 * (f % 3) + 512]
                P.op("scalar", lambda E, f=f, st=st: E.activation(st, bank(f), AF.Square),
                     reads=[rBANK[f]], writes=[rSC2[f % 3]])
                P.op("vector", lambda E, st=st: E.tensor_tensor(st, st, RBS[0], AL.mult),
                     reads=[rRB[0]], writes=[rSC2[f % 3]])
                P.op("scalar", lambda E, f=f, g=g, st=st: E.activation(
                    chk(BB, f, g), st, AF.Exp, scale=-0.5),
                    reads=[rSC2[f % 3]], writes=[rBB[f][g]])
        grp3p(NCH,
              lambda colh, ki: wdma_raw_h(wq_d, 128 * ki, 0, colh, ki),
              wproc_raw_h,
              lambda ki, g=g: split_act(chk(XN, ki, g), rXN[ki][g], ki % 2,
                                        scale_col=vcol("s1", ki)) + (rXSP[ki % 2],),
              ev_q)

    for par in range(2):
        for j in range(8):
            h = 2 * j + par
            p_, hl = h // 8, h % 8
            P.op("tensor", lambda E, par=par, j=j, hl=hl, p_=p_: E.matmul(
                PS[64 * par:64 * par + 64, 1536 + 2 * j + par:1536 + 2 * j + par + 1],
                RXs[p_][0:1, 64 * hl:64 * hl + 64], ONE1[0:1, :],
                start=(j == 0), stop=(j == 7)),
                reads=[rROWX[p_], rCONST],
                deps=[t_ks0, t_ks1],
                writes=([rBANK[3]] if j in (0, 7) else []))
    P.op("gpsimd", lambda E: E.iota(KSUM2[:, :], [[0, 16]], base=0, channel_multiplier=0,
                                      allow_small_or_imprecise_dtypes=True),
         writes=[rKV["ksum2"]], sem="cc", inc=1)
    for h in range(16):
        P.op("vector", lambda E, h=h, pb=64 * (h % 2): E.tensor_copy(
            KSUM2[pb:pb + 64, h:h + 1], PS[pb:pb + 64, 1536 + h:1537 + h]),
            reads=[rBANK[3]], writes=[rKV["ksum2"]])

    for g in range(NG):
        SBK = [1, 1, 1, 5, 5, 5, 6, 6]
        PBS = [0, 32, 64, 0, 32, 64, 0, 32]
        for cq in range(NCH):
            sb = SBK[cq]
            pb = PBS[cq]
            P.op("tensor", lambda E, cq=cq, g=g, sb=sb, pb=pb: E.matmul(
                PS[pb:pb + 2, 512 * sb:512 * sb + 512], KSUM2[:, 2 * cq:2 * cq + 2],
                chk(BB, cq, g), start=True, stop=True),
                reads=[rKV["ksum2"], rBB[cq][g]], writes=[rBANK[sb]])
        ZB = [SC2[:, 512:1024], SC2[:, 1024:1536], SC2[:, 1536:2048]]
        for i, sb in enumerate((1, 5, 6)):
            P.op("vector", lambda E, i=i, sb=sb: E.tensor_scalar(
                ZB[i], PS[:, 512 * sb:512 * sb + 512], 1e-6, None, AL.add),
                reads=[rBANK[sb]], writes=[rSC2[1 + i]])
            P.op("vector", lambda E, i=i: E.reciprocal(ZB[i], ZB[i]),
                 writes=[rSC2[1 + i]])
        for cq in range(NCH):
            nb = 0 if cq % 2 == 0 else 2
            zb = 3 if cq % 2 == 0 else 4
            zi = cq // 3
            pb = PBS[cq]
            P.op("tensor", lambda E, cq=cq, g=g, nb=nb: E.matmul(
                PS[0:64, 512 * nb:512 * nb + 512], KV_SB[0:64, 64 * cq:64 * cq + 64],
                chk(BB, cq, g)[0:64, :], start=True, stop=True),
                reads=[rKV["kvsb"], rBB[cq][g]], writes=[rBANK[nb]])
            P.op("tensor", lambda E, cq=cq, g=g, nb=nb: E.matmul(
                PS[64:128, 512 * nb:512 * nb + 512], KV_SB[64:128, 64 * cq:64 * cq + 64],
                chk(BB, cq, g)[64:128, :], start=True, stop=True),
                reads=[rKV["kvsb"], rBB[cq][g]], writes=[rBANK[nb]])
            P.op("tensor", lambda E, zb=zb, zi=zi, pb=pb: E.matmul(
                PS[:, 512 * zb:512 * zb + 512], SELT2[pb:pb + 2, :], ZB[zi][pb:pb + 2, :],
                start=True, stop=True),
                reads=[rSC2[1 + zi], rCONST], writes=[rBANK[zb]])
            P.op("vector", lambda E, p=cq % 2, zb=zb: E.tensor_copy(
                RBS[p], PS[:, 512 * zb:512 * zb + 512]),
                reads=[rBANK[zb]], writes=[rRB[cq % 2]])
            P.op("vector", lambda E, cq=cq, g=g, nb=nb, p=cq % 2: E.tensor_tensor(
                chk(BB, cq, g), PS[:, 512 * nb:512 * nb + 512], RBS[p], AL.mult),
                reads=[rBANK[nb], rRB[cq % 2]], writes=[rBB[cq][g]])

    P.op("vector", lambda E: E.tensor_tensor(BCOL[:, 24:32], vcols("gamma"), vcols("pb"), AL.mult),
         reads=[rCONST], writes=[rBCOL[3]])
    for g in range(NG):
        def ev_proj(hf, g=g):
            P.op("vector", lambda E, hf=hf: E.tensor_tensor(
                SC2[:, :].rearrange("p (c t) -> p c t", t=G)[:, 0:4, :],
                PS[:, 2048 * hf:2048 * (hf + 1)].rearrange("p (c t) -> p c t", t=G)[:, 0:4, :],
                vbc(vcols("gamma", 4 * hf, 4), 4, G), AL.mult),
                reads=[rBANK[4 * hf + i] for i in range(4)] + [rCONST], writes=rSC2[0:4])
            P.op("vector", lambda E, hf=hf: E.tensor_tensor(
                SC2[:, :].rearrange("p (c t) -> p c t", t=G)[:, 0:4, :],
                SC2[:, :].rearrange("p (c t) -> p c t", t=G)[:, 0:4, :],
                vbc(BCOL[:, 24 + 4 * hf:24 + 4 * hf + 4], 4, G), AL.add),
                reads=[rBCOL[3]], writes=rSC2[0:4])
            for c in range(4 * hf, 4 * hf + 4):
                P.op("vector", lambda E, c=c, g=g: E.tensor_tensor(
                    chk(BB, c, g), chk(XN, c, g), SC2[:, 512 * (c % 4):512 * (c % 4) + 512],
                    AL.add),
                    reads=[rXN[c][g], rSC2[c % 4]], writes=[rBB[c][g]])
        grp3p(NCH,
              lambda colh, ki: wdma_raw_h(wproj_d, 128 * ki, 0, colh, ki),
              wproc_raw_h,
              lambda ki, g=g: split_act(chk(BB, ki, g), rBB[ki][g], ki % 2) + (rXSP[ki % 2],),
              ev_proj, defer=True)


    P.op("vector", lambda E: E.tensor_tensor(BCOL[:, 32:40], vcols("gamma"), vcols("cb2"), AL.mult),
         reads=[rCONST], writes=[rBCOL[4]])

    def ev_c2(hf, g):
        P.op("vector", lambda E, hf=hf: E.tensor_tensor(
            SC2[:, :].rearrange("p (c t) -> p c t", t=G)[:, 0:4, :],
            PS[:, 2048 * hf:2048 * (hf + 1)].rearrange("p (c t) -> p c t", t=G)[:, 0:4, :],
            vbc(vcols("gamma", 4 * hf, 4), 4, G), AL.mult),
            reads=[rBANK[4 * hf + i] for i in range(4)] + [rCONST], writes=rSC2[0:4])
        P.op("vector", lambda E, hf=hf: E.tensor_tensor(
            SC2[:, :].rearrange("p (c t) -> p c t", t=G)[:, 0:4, :],
            SC2[:, :].rearrange("p (c t) -> p c t", t=G)[:, 0:4, :],
            vbc(BCOL[:, 32 + 4 * hf:32 + 4 * hf + 4], 4, G), AL.add),
            reads=[rBCOL[4]], writes=rSC2[0:4])
        for c in range(4 * hf, 4 * hf + 4):
            P.op("vector", lambda E, c=c, g=g: E.tensor_tensor(
                chk(BB, c, g), chk(BB, c, g), SC2[:, 512 * (c % 4):512 * (c % 4) + 512],
                AL.add),
                reads=[rSC2[c % 4]], writes=[rBB[c][g]])

    for g in range(NG):
        for hf in range(2):
            P.op("scalar", lambda E, hf=hf, g=g: E.activation(
                SC2[:, :].rearrange("p (c t) -> p c t", t=G)[:, 0:4, :],
                span3(BB, 4 * hf, 4, g), AF.Square),
                reads=[rBB[c][g] for c in range(4 * hf, 4 * hf + 4)],
                writes=rSC2[0:4])
            for c in range(4 * hf, 4 * hf + 4):
                P.op("tensor", lambda E, c=c: E.matmul(
                    PS[0:1, 0:512], ONESC[:, :], SC2[:, 512 * (c % 4):512 * (c % 4 + 1)],
                    start=(c == 0), stop=(c == NCH - 1)),
                    reads=[rSC2[c % 4], rCONST],
                    writes=([rBANK[0]] if c in (0, NCH - 1) else []))
        kq2 = (SQRTC + EPS_RMS) ** 2
        P.op("vector", lambda E: E.reciprocal(RXs[0], PS[0:1, 0:512]),
             reads=[rBANK[0]], writes=[rROWX[0]])
        P.op("scalar", lambda E: E.activation(RXs[0], RXs[0], AF.Sqrt, scale=kq2),
             writes=[rROWX[0]])
        P.op("tensor", lambda E: E.matmul(PS[:, 0:512], ONESR[0:1, :], RXs[0],
                                          start=True, stop=True),
             reads=[rROWX[0], rCONST], writes=[rBANK[0]])
        P.op("vector", lambda E: E.tensor_copy(RBS[0], PS[:, 0:512]),
             reads=[rBANK[0]], writes=[rRB[0]])
        for hhalf in range(2):
            def ev_c1(colh, hhalf=hhalf, g=g):
                for f in range(4 * colh, 4 * colh + 4):
                    h = 8 * hhalf + f
                    st = SC2[:, 512 * (f % 3):512 * (f % 3) + 512]
                    P.op("vector", lambda E, f=f, st=st: E.tensor_tensor(
                        st, bank(f), RBS[0], AL.mult),
                        reads=[rBANK[f], rRB[0]], writes=[rSC2[f % 3]])
                    P.op("scalar", lambda E, h=h, st=st: E.activation(
                        SCR[:, 512 * h:512 * (h + 1)], st, AF.Gelu, bias=CB1[:, h:h + 1]),
                        reads=[rSC2[f % 3], rCONST], writes=[rSCR[h]])
            grp3p(NCH,
                  lambda colh, ki, hhalf=hhalf: wdma_raw_h(
                      wc1_d, 128 * ki, 1024 * hhalf, colh, ki),
                  wproc_raw_h,
                  lambda ki, g=g: split_act(chk(BB, ki, g), rBB[ki][g], ki % 2,
                                            scale_col=vcol("s2", ki)) + (rXSP[ki % 2],),
                  ev_c1)
        grp3p(16,
              lambda colh, h: wdma_raw_h(wc2_d, 128 * h, 0, colh, h),
              wproc_raw_h,
              lambda h: split_act(SCR[:, 512 * h:512 * (h + 1)], rSCR[h], h % 2)
              + (rXSP[h % 2],),
              lambda colh: ev_c2(colh, g))

    for c in range(NCH):
        P.dma("gpsimd", accA_d[128 * c:128 * (c + 1), :], BB[:, 2048 * c:2048 * (c + 1)],
              reads=[rBB[c][g] for g in range(NG)],
              writes=([rDRAM["accA_d"]] if c == NCH - 1 else []))


    for g in range(NG):
        regen_bcast_from_cols(R1NC, rKV["r1nc"], g, 1, 0)
        for hf in range(2):
            bres = [rBB[c][g] for c in range(4 * hf, 4 * hf + 4)]
            P.op("vector", lambda E, hf=hf, g=g: E.tensor_tensor(
                span3(BB, 4 * hf, 4, g), span3(XN, 4 * hf, 4, g), rbc(RBS[1], 4), AL.mult),
                reads=[rRB[1]] + [rXN[c][g] for c in range(4 * hf, 4 * hf + 4)],
                writes=bres)
            P.op("vector", lambda E, hf=hf, g=g: E.tensor_tensor(
                span3(BB, 4 * hf, 4, g), span3(BB, 4 * hf, 4, g),
                vbc(vcols("l1w", 4 * hf, 4), 4, G), AL.mult),
                reads=[rCONST], writes=bres)
            P.op("vector", lambda E, hf=hf, g=g: E.tensor_tensor(
                span3(BB, 4 * hf, 4, g), span3(BB, 4 * hf, 4, g),
                vbc(vcols("l1b", 4 * hf, 4), 4, G), AL.add),
                reads=[rCONST], writes=bres)

    for g in range(NG):
        def ev_s(colh, g=g):
            for f in range(4 * colh, 4 * colh + 4):
                st = SC2[:, 512 * (f % 3):512 * (f % 3) + 512]
                P.op("scalar", lambda E, f=f, st=st: E.activation(st, bank(f), AF.Sigmoid),
                     reads=[rBANK[f]], writes=[rSC2[f % 3]])
                P.dma("gpsimd", s_d[128 * f:128 * (f + 1), G * g:G * (g + 1)], st,
                      reads=[rSC2[f % 3]],
                      writes=([rDRAM["s_d"]] if f == NCH - 1 and g == NG - 1 else []))
        grp3p(NCH,
              lambda colh, ki: wload_presplit_h(wsw2p_d, 128 * ki, 0, colh, ki,
                                                rDRAM["wsw2h"]),
              None,
              lambda ki, g=g: split_act(chk(BB, ki, g), rBB[ki][g], ki % 2) + (rXSP[ki % 2],),
              ev_s)

    def ghalf(c, gg):
        return SCR[:, 1024 * c + 512 * gg:1024 * c + 512 * gg + 512]

    def rghalf(c, gg):
        return rSCR[2 * c + gg]

    for h2 in range(2):
        for wi, wdram in ((0, wr_d), (1, wvv_d)):
            for gg in range(2):
                g = 2 * h2 + gg

                def ev_rv(colh, wi=wi, gg=gg):
                    for f in range(4 * colh, 4 * colh + 4):
                        if wi == 0:
                            P.op("scalar", lambda E, f=f, gg=gg: E.activation(
                                ghalf(f, gg), bank(f), AF.Sigmoid),
                                reads=[rBANK[f]], writes=[rghalf(f, gg)])
                        else:
                            P.op("vector", lambda E, f=f, gg=gg: E.tensor_tensor(
                                ghalf(f, gg), ghalf(f, gg), bank(f), AL.mult),
                                reads=[rBANK[f]], writes=[rghalf(f, gg)])
                grp3p(NCH,
                      lambda colh, ki, wdram=wdram: wdma_raw_h(wdram, 128 * ki, 0, colh, ki),
                      wproc_raw_h,
                      lambda ki, g=g: split_act(chk(BB, ki, g), rBB[ki][g], ki % 2)
                      + (rXSP[ki % 2],),
                      ev_rv)
        for gg in range(2):
            g = 2 * h2 + gg

            def ev_wo(colh, g=g):
                for f in range(4 * colh, 4 * colh + 4):
                    P.op("vector", lambda E, f=f, g=g: E.tensor_tensor(
                        chk(XN, f, g), chk(XN, f, g), bank(f), AL.add),
                        reads=[rBANK[f]], writes=[rXN[f][g]])
            grp3p(NCH,
                  lambda colh, ki: wdma_raw_h(wo_d, 128 * ki, 0, colh, ki),
                  wproc_raw_h,
                  lambda ki, gg=gg: split_act(ghalf(ki, gg), rghalf(ki, gg), ki % 2)
                  + (rXSP[ki % 2],),
                  ev_wo)

    for g in range(NG):
        pump_job(4)
        for c in range(NCH):
            P.op("tensor", lambda E, c=c, g=g: E.matmul(
                PS[0:1, 0:512], ONESC[:, :], chk(XN, c, g),
                start=(c == 0), stop=(c == NCH - 1)),
                reads=[rXN[c][g], rCONST],
                writes=([rBANK[0]] if c in (0, NCH - 1) else []))
        for hf in range(2):
            P.op("scalar", lambda E, hf=hf, g=g: E.activation(
                SC2[:, :].rearrange("p (c t) -> p c t", t=G)[:, 0:4, :],
                span3(XN, 4 * hf, 4, g), AF.Square),
                reads=[rXN[c][g] for c in range(4 * hf, 4 * hf + 4)],
                writes=rSC2[0:4])
            for c in range(4 * hf, 4 * hf + 4):
                P.op("tensor", lambda E, c=c: E.matmul(
                    PS[0:1, 512:1024], ONESC[:, :], SC2[:, 512 * (c % 4):512 * (c % 4 + 1)],
                    start=(c == 0), stop=(c == NCH - 1)),
                    reads=[rSC2[c % 4], rCONST],
                    writes=([rBANK[1]] if c in (0, NCH - 1) else []))
        P.op("vector", lambda E: E.tensor_scalar(RXs[0], PS[0:1, 0:512], 1.0 / C, None, AL.mult),
             reads=[rBANK[0]], writes=[rROWX[0]])
        P.op("vector", lambda E: E.tensor_scalar(RXs[1], PS[0:1, 512:1024], 1.0 / C, None, AL.mult),
             reads=[rBANK[1]], writes=[rROWX[1]])
        P.op("vector", lambda E: E.tensor_tensor(RXs[2], RXs[0], RXs[0], AL.mult),
             reads=[rROWX[0]], writes=[rROWX[2]])
        P.op("vector", lambda E: E.tensor_tensor(RXs[1], RXs[1], RXs[2], AL.subtract),
             reads=[rROWX[2]], writes=[rROWX[1]])
        P.op("vector", lambda E: E.tensor_scalar(RXs[1], RXs[1], EPS_LN, None, AL.add),
             writes=[rROWX[1]])
        P.op("vector", lambda E: E.reciprocal(RXs[1], RXs[1]), writes=[rROWX[1]])
        P.op("scalar", lambda E: E.activation(RXs[1], RXs[1], AF.Sqrt), writes=[rROWX[1]])
        P.op("vector", lambda E: E.tensor_tensor(RXs[2], RXs[0], RXs[1], AL.mult),
             reads=[rROWX[0], rROWX[1]], writes=[rROWX[2]])
        P.op("vector", lambda E: E.tensor_scalar(RXs[2], RXs[2], -1.0, None, AL.mult),
             writes=[rROWX[2]])
        P.op("tensor", lambda E: E.matmul(PS[:, 0:512], ONESR[0:1, :], RXs[1],
                                          start=True, stop=True),
             reads=[rROWX[1], rCONST], writes=[rBANK[0]])
        P.op("vector", lambda E: E.tensor_copy(RBS[0], PS[:, 0:512]),
             reads=[rBANK[0]], writes=[rRB[0]])
        P.op("tensor", lambda E: E.matmul(PS[:, 512:1024], ONESR[0:1, :], RXs[2],
                                          start=True, stop=True),
             reads=[rROWX[2], rCONST], writes=[rBANK[1]])
        P.op("vector", lambda E: E.tensor_copy(RBS[1], PS[:, 512:1024]),
             reads=[rBANK[1]], writes=[rRB[1]])
        for hf in range(2):
            xres = [rXN[c][g] for c in range(4 * hf, 4 * hf + 4)]
            bres = [rBB[c][g] for c in range(4 * hf, 4 * hf + 4)]
            P.op("vector", lambda E, hf=hf, g=g: E.tensor_tensor(
                span3(BB, 4 * hf, 4, g), span3(XN, 4 * hf, 4, g), rbc(RBS[0], 4), AL.mult),
                reads=[rRB[0]] + xres, writes=bres)
            P.op("vector", lambda E, hf=hf, g=g: E.tensor_tensor(
                span3(BB, 4 * hf, 4, g), span3(BB, 4 * hf, 4, g), rbc(RBS[1], 4), AL.add),
                reads=[rRB[1]], writes=bres)
            P.op("vector", lambda E, hf=hf, g=g: E.tensor_tensor(
                span3(BB, 4 * hf, 4, g), span3(BB, 4 * hf, 4, g),
                vbc(vcols("l2w", 4 * hf, 4), 4, G), AL.mult),
                reads=[rCONST], writes=bres)
            P.op("vector", lambda E, hf=hf, g=g: E.tensor_tensor(
                span3(BB, 4 * hf, 4, g), span3(BB, 4 * hf, 4, g),
                vbc(vcols("l2b", 4 * hf, 4), 4, G), AL.add),
                reads=[rCONST], writes=bres)


    P.dma("gpsimd", WSKT[:, :].rearrange("p (u j) -> p u j", j=8),
          wsk_d[:, :].rearrange("(u p) j -> p u j", p=128), writes=[rKV["wskt"]])
    for tc in range(16):
        g = tc // 4
        pump_job(2)
        for c in range(NCH):
            P.op("tensor", lambda E, c=c, tc=tc: E.matmul(
                PS[:, 0:8], BB[:, 2048 * c + 128 * tc:2048 * c + 128 * tc + 128],
                WSKT[:, 8 * c:8 * c + 8],
                start=(c == 0), stop=(c == NCH - 1)),
                reads=[rBB[c][g], rKV["wskt"]],
                writes=([rBANK[0]] if c in (0, NCH - 1) else []))
        P.op("vector", lambda E, tc=tc: E.tensor_copy(ROUT[:, 8 * tc:8 * tc + 8], PS[:, 0:8]),
             reads=[rBANK[0]], writes=([rKV["rout"]] if tc == 15 else []))
    RV = ROUT.rearrange("p (u j) -> p u j", j=8)
    SGv = SIGS.rearrange("p (u j) -> p u j", j=4)

    def rv(j):
        return RV[:, :, j:j + 1].squeeze(2)

    def sg(j):
        return SGv[:, :, j:j + 1].squeeze(2)

    rT = rKV["tmp"]
    P.op("scalar", lambda E: E.activation(SGv, RV[:, :, 0:4], AF.Sigmoid),
         reads=[rKV["rout"]], writes=[rT, rKV["kvsb"]])
    for e in range(3):
        P.op("vector", lambda E, e=e: E.tensor_tensor(TMP1, sg(e), sg(3), AL.mult),
             reads=[rT], writes=[rT])
        P.op("vector", lambda E, e=e: E.tensor_scalar(TMP1, TMP1, CAPS[:, e:e + 1], None, AL.mult),
             reads=[rKV["caps"]], writes=[rT])
        P.op("vector", lambda E, e=e: E.tensor_tensor(BIDS[e], TMP1, rv(4 + e), AL.add),
             reads=[rKV["rout"]], writes=[rT])
    P.op("vector", lambda E: E.tensor_tensor(TMP1, BIDS[0], BIDS[1], AL.is_ge), writes=[rT])
    P.op("vector", lambda E: E.tensor_tensor(TMP2, BIDS[0], BIDS[2], AL.is_ge), writes=[rT])
    P.op("vector", lambda E: E.tensor_tensor(SELS[0], TMP1, TMP2, AL.mult), writes=[rT])
    P.op("vector", lambda E: E.tensor_tensor(TMP1, BIDS[1], BIDS[2], AL.is_ge), writes=[rT])
    P.op("vector", lambda E: E.tensor_scalar(TMP2, SELS[0], -1.0, 1.0, AL.mult, AL.add),
         writes=[rT])
    P.op("vector", lambda E: E.tensor_tensor(SELS[1], TMP2, TMP1, AL.mult), writes=[rT])
    P.op("vector", lambda E: E.tensor_scalar(TMP1, TMP1, -1.0, 1.0, AL.mult, AL.add),
         writes=[rT])
    P.op("vector", lambda E: E.tensor_tensor(SELS[2], TMP2, TMP1, AL.mult), writes=[rT])
    P.op("vector", lambda E: E.tensor_tensor(WCONF, SELS[0], sg(0), AL.mult), writes=[rT])
    P.op("vector", lambda E: E.tensor_tensor(TMP1, SELS[1], sg(1), AL.mult), writes=[rT])
    P.op("vector", lambda E: E.tensor_tensor(WCONF, WCONF, TMP1, AL.add), writes=[rT])
    P.op("vector", lambda E: E.tensor_tensor(TMP1, SELS[2], sg(2), AL.mult), writes=[rT])
    P.op("vector", lambda E: E.tensor_tensor(WCONF, WCONF, TMP1, AL.add), writes=[rT])
    P.op("vector", lambda E: E.tensor_scalar(TMP1, WCONF, 1e-6, None, AL.add), writes=[rT])
    P.op("vector", lambda E: E.reciprocal(TMP1, TMP1), writes=[rT])
    P.op("vector", lambda E: E.tensor_tensor(SCL, WCONF, TMP1, AL.mult), writes=[rT])
    for e in range(3):
        P.op("vector", lambda E, e=e: E.tensor_tensor(MEC[:, 16 * e:16 * e + 16], SELS[e],
                                                      SCL, AL.mult),
             reads=[rT], writes=([rKV["mec"]] if e == 2 else []))

    pump_rest()

    def wset_hi(s):
        return WSTR[:, 1024 * s:1024 * s + 512]

    def wset_lo(s):
        return WSTR[:, 1024 * s + 512:1024 * s + 1024]

    def xsp_pair(pr):
        return (XSP[:, 1024 * pr:1024 * pr + 512],
                XSP[:, 1024 * pr + 512:1024 * pr + 1024])

    def split_act(src_ap, src_res, pr):
        XH, XL = xsp_pair(pr)
        P.op("scalar", lambda E, o=XH, i=src_ap: E.activation(o, i, AF.Copy),
             reads=[src_res], writes=[rXSP[pr]])
        P.op("vector", lambda E, o=XL, i=src_ap, h=XH: E.tensor_tensor(o, i, h, AL.subtract),
             reads=[src_res], writes=[rXSP[pr]])
        return XH, XL

    HRF = [SCR[:, 512 * hh:512 * (hh + 1)] for hh in range(8)]
    for e in range(2):
        for g in range(NG):
            regen_bcast_from_cols(MEC[:, 16 * e:16 * e + 16], rKV["mec"], g, 1, 0)
            for q4 in range(4):
                def ev_k(colh):
                    for j in range(4):
                        hh = 4 * colh + j
                        a = SC2[:, 512 * (2 * (j % 2)):512 * (2 * (j % 2)) + 512]
                        bsc = SC2[:, 512 * (2 * (j % 2) + 1):512 * (2 * (j % 2) + 1) + 512]
                        ra, rb_ = rSC2[2 * (j % 2)], rSC2[2 * (j % 2) + 1]
                        P.op("scalar", lambda E, hh=hh, a=a: E.activation(a, bank(hh), AF.Relu),
                             reads=[rBANK[hh]], writes=[ra])
                        P.op("vector", lambda E, a=a, bsc=bsc: E.tensor_tensor(
                            bsc, a, a, AL.mult),
                            reads=[ra], writes=[rb_])
                        P.op("vector", lambda E, hh=hh, bsc=bsc: E.tensor_tensor(
                            HRF[hh], bsc, RBS[1], AL.mult),
                            reads=[rb_, rRB[1]], writes=[rSCR[hh]])
                grp3p(NCH,
                      lambda colh, ki, e=e, q4=q4: wdma_exp(
                          wk_d[e], 128 * ki, 1024 * q4, colh, ki),
                      wproc_exp,
                      lambda ki, g=g: split_act(chk(BB, ki, g), rBB[ki][g], ki % 2)
                      + (rXSP[ki % 2],),
                      ev_k)

                def ev_v(colh, g=g):
                    for j in range(4):
                        co = 4 * colh + j
                        P.op("vector", lambda E, co=co, g=g: E.tensor_tensor(
                            chk(XN, co, g), chk(XN, co, g), bank(co), AL.add),
                            reads=[rBANK[co]], writes=[rXN[co][g]])
                grp3p(NCH,
                      lambda colh, ki, e=e, q4=q4: wdma_exp(
                          wv_d[e], 128 * (8 * q4 + ki), 0, colh, ki),
                      wproc_exp,
                      lambda ki: split_act(HRF[ki], rSCR[ki], ki % 2) + (rXSP[ki % 2],),
                      ev_v)
    for h2 in range(2):
        for gg in range(2):
            g = 2 * h2 + gg

            def ev_w1(colh, g=g, gg=gg):
                for f in range(4 * colh, 4 * colh + 4):
                    st = SC2[:, 512 * (f % 3):512 * (f % 3) + 512]
                    P.dma("gpsimd", st, s_d[128 * f:128 * (f + 1), G * g:G * (g + 1)],
                          reads=[rDRAM["s_d"]], writes=[rSC2[f % 3]])
                    P.op("vector", lambda E, f=f, gg=gg, st=st: E.tensor_tensor(
                        ghalf(f, gg), bank(f), st, AL.mult),
                        reads=[rBANK[f], rSC2[f % 3]], writes=[rghalf(f, gg)])
            grp3p(NCH,
                  lambda colh, ki: wdma_raw_h(w1_d, 128 * ki, 0, colh, ki),
                  wproc_raw_h,
                  lambda ki, g=g: split_act(chk(BB, ki, g), rBB[ki][g], ki % 2)
                  + (rXSP[ki % 2],),
                  ev_w1)
        for gg in range(2):
            g = 2 * h2 + gg
            regen_bcast_from_cols(MEC[:, 32:48], rKV["mec"], g, 1, 0)

            def ev_w3(colh, g=g):
                for f in range(4 * colh, 4 * colh + 4):
                    st = SC2[:, 512 * (f % 3):512 * (f % 3) + 512]
                    P.op("vector", lambda E, f=f, st=st: E.tensor_tensor(
                        st, bank(f), RBS[1], AL.mult),
                        reads=[rBANK[f], rRB[1]], writes=[rSC2[f % 3]])
                    P.op("vector", lambda E, f=f, g=g, st=st: E.tensor_tensor(
                        chk(XN, f, g), chk(XN, f, g), st, AL.add),
                        reads=[rSC2[f % 3]], writes=[rXN[f][g]])
            grp3p(NCH,
                  lambda colh, ki: wdma_raw_h(w3_d, 128 * ki, 0, colh, ki),
                  wproc_raw_h,
                  lambda ki, gg=gg: split_act(ghalf(ki, gg), rghalf(ki, gg), ki % 2)
                  + (rXSP[ki % 2],),
                  ev_w3)

    if "xpost2" in tap_d:
        for c in range(NCH):
            P.dma("gpsimd", tap_d["xpost2"][128 * c:128 * (c + 1), :],
                  XN[:, 2048 * c:2048 * (c + 1)],
                  reads=[rXN[c][g] for g in range(NG)])
    for c in range(NCH):
        t_a = P.dma("gpsimd", SC2[:, :], accA_d[128 * c:128 * (c + 1), :],
                    reads=[rDRAM["accA_d"]], writes=rSC2[0:4])
        t_x = P.dma("gpsimd", SCR[:, 0:2048], x_d[128 * c:128 * (c + 1), :], writes=rSCR[0:4])
        P.op("vector", lambda E, c=c: E.tensor_tensor(
            XN[:, 2048 * c:2048 * (c + 1)], XN[:, 2048 * c:2048 * (c + 1)], SC2[:, :], AL.add),
            reads=rSC2[0:4], deps=[t_a], writes=[rXN[c][g] for g in range(NG)])
        P.op("vector", lambda E, c=c: E.tensor_tensor(
            XN[:, 2048 * c:2048 * (c + 1)], XN[:, 2048 * c:2048 * (c + 1)],
            SCR[:, 0:2048], AL.add),
            reads=rSCR[0:4], deps=[t_x], writes=[rXN[c][g] for g in range(NG)])
        P.dma("gpsimd", out_d[128 * c:128 * (c + 1), :], XN[:, 2048 * c:2048 * (c + 1)],
              reads=[rXN[c][g] for g in range(NG)])

    def tap_fm(name, buf, rres):
        if name not in tap_d:
            return
        for c in range(NCH):
            P.dma("gpsimd", tap_d[name][128 * c:128 * (c + 1), :],
                  buf[:, 2048 * c:2048 * (c + 1)],
                  reads=[rres[c][g] for g in range(NG)])

    tap_fm("xn", XN, rXN)
    tap_fm("y", BB, rBB)
    tap_fm("x2", XN, rXN)
    tap_fm("h", BB, rBB)
    if "kv" in tap_d:
        P.dma("gpsimd", tap_d["kv"][:, :], KV_SB, reads=[rKV["kvsb"]])
    if "ksum2" in tap_d:
        P.dma("gpsimd", tap_d["ksum2"][:, :], KSUM2[:, :], reads=[rKV["ksum2"]])

    for e, s in (("tensor", "ct"), ("vector", "cv"), ("scalar", "cs"), ("sync", "sq")):
        pass
    for s in ("ct", "cv", "cs", "sq", "aq", "sq0", "sq1", "sqw", "jq"):
        if P.cnt[s]:
            P._wait("gpsimd", (s, P.cnt[s]))
    P._wait("gpsimd", ("gq", P.cnt["gq"]))

    sem_handles = {}
    for sname in ("sq", "ct", "cv", "cs", "gq", "cc", "aq", "sq0", "sq1", "sqw", "jq"):
        sem_handles[sname] = nc.semaphore(sname).__enter__()
    with nc.Block() as block:
        def mk(name):
            def run(Eobj):
                for op in P.ops[name]:
                    if op[0] == "w":
                        Eobj.wait_ge(sem_handles[op[1]], op[2])
                    else:
                        _, fn, sem, inc = op
                        fn(Eobj).then_inc(sem_handles[sem], inc)
            return run
        block.sync(mk("sync"))
        block.tensor(mk("tensor"))
        block.vector(mk("vector"))
        block.scalar(mk("scalar"))
        block.gpsimd(mk("gpsimd"))
    return nc


def prep_inputs(inp, core):
    b, q = core // 4, core % 4
    x = np.asarray(inp["L0_attn_x"])
    xs = np.ascontiguousarray(x[b, 2048 * q:2048 * (q + 1), :].T)
    qkv = np.asarray(inp["L0_attn_qkv_w"])
    d = {
        "x": xs,
        "wq": np.ascontiguousarray(qkv[0:1024].T),
        "wkv": np.ascontiguousarray(qkv[1024:3072].T),
        "wproj": np.ascontiguousarray(np.asarray(inp["L0_attn_proj_w"]).T),
        "wc1": np.ascontiguousarray(np.asarray(inp["L0_attn_conv1_w"]).T),
        "wc2": np.ascontiguousarray(np.asarray(inp["L0_attn_conv2_w"]).T),
        "wr": np.ascontiguousarray(np.asarray(inp["L0_ffn_Wr"])),
        "wvv": np.ascontiguousarray(np.asarray(inp["L0_ffn_Wv"])),
        "wo": np.ascontiguousarray(np.asarray(inp["L0_ffn_Wo"])),
        "wst": np.ascontiguousarray(np.asarray(inp["L0_ffn_Ws"]).T),
        "w2": np.ascontiguousarray(np.asarray(inp["L0_ffn_W2"])),
        "w1": np.ascontiguousarray(np.asarray(inp["L0_ffn_W1"])),
        "w3": np.ascontiguousarray(np.asarray(inp["L0_ffn_W3"])),
        "wk0": np.ascontiguousarray(np.asarray(inp["L0_ffn_K_rwkv"])[0]),
        "wk1": np.ascontiguousarray(np.asarray(inp["L0_ffn_K_rwkv"])[1]),
        "wve0": np.ascontiguousarray(np.asarray(inp["L0_ffn_V_rwkv"])[0]),
        "wve1": np.ascontiguousarray(np.asarray(inp["L0_ffn_V_rwkv"])[1]),
    }
    wsk = np.zeros((1024, 8), np.float32)
    wsk[:, 0:2] = np.asarray(inp["L0_ffn_conf_rwkv"]).T
    wsk[:, 2] = np.asarray(inp["L0_ffn_conf_trans"])
    wsk[:, 3] = np.asarray(inp["L0_ffn_w_diff"])
    wsk[:, 4:7] = np.asarray(inp["L0_ffn_W_aff"])
    d["wsk"] = np.ascontiguousarray(wsk)
    vecs = np.zeros((128, 80), np.float32)
    vsrc = {
        "s1": inp["L0_attn_scale1"], "s2": inp["L0_attn_scale2"], "gamma": inp["L0_attn_gamma"],
        "pb": inp["L0_attn_proj_b"], "cb2": inp["L0_attn_conv2_b"],
        "l1w": inp["L0_ffn_ln1_w"], "l1b": inp["L0_ffn_ln1_b"],
        "l2w": inp["L0_ffn_ln2_w"], "l2b": inp["L0_ffn_ln2_b"],
    }
    VNAMES = ["s1", "s2", "gamma", "pb", "cb2", "l1w", "l1b", "l2w", "l2b", "pad"]
    for i, n in enumerate(VNAMES):
        if n == "pad":
            continue
        vecs[:, 8 * i:8 * i + 8] = np.asarray(vsrc[n]).reshape(8, 128).T
    d["vecs"] = vecs
    d["cb1"] = np.ascontiguousarray(np.asarray(inp["L0_attn_conv1_b"]).reshape(16, 128).T)
    caps = np.zeros((1, 4), np.float32)
    caps[0, 0:3] = np.asarray(inp["L0_ffn_capital_shares"])
    d["caps"] = caps
    d["onesc"] = np.ones((128, 1), np.float32)
    d["onesr"] = np.ones((1, 128), np.float32)
    d["one1"] = np.ones((1, 1), np.float32)
    sel2a = np.zeros((1, 128), np.float32); sel2a[0, 0:64] = 1.0
    sel2b = np.zeros((1, 128), np.float32); sel2b[0, 64:128] = 1.0
    d["sel2a"] = sel2a
    d["sel2b"] = sel2b
    d["i128"] = np.eye(128, dtype=np.float32)
    return d


def kernel(**inputs):
    nc = build_nc(taps=DEBUG_TAPS)
    in_maps = [prep_inputs(inputs, i) for i in range(8)]
    res = run_bass_kernel_spmd(nc, in_maps, core_ids=list(range(8)))
    x = np.asarray(inputs["L0_attn_x"])
    y = np.zeros_like(x)
    for i in range(8):
        b, q = i // 4, i % 4
        y[b, 2048 * q:2048 * (q + 1), :] = res.results[i]["out"].T
    if DEBUG_TAPS:
        return y, res
    return y


# version 12
# speedup vs baseline: 1.0029x; 1.0029x over previous
import numpy as np
import concourse.bass as bass
import concourse.mybir as mybir
from concourse.bass_utils import run_bass_kernel_spmd

F32 = mybir.dt.float32
F32R = mybir.dt.float32r
AL = mybir.AluOpType
AF = mybir.ActivationFunctionType

C = 1024
T = 2048
NCH = 8
G = 512
NG = T // G
SQRTC = float(np.sqrt(1024.0))
EPS_RMS = 1e-6
EPS_LN = 1e-5

ENGS = ("sync", "tensor", "vector", "scalar", "gpsimd")
SEM_OF = dict(sync="sq", tensor="ct", vector="cv", scalar="cs", gpsimd="gq")

DEBUG_TAPS = ()


class Res:
    __slots__ = ("w", "rd")

    def __init__(self):
        self.w = None
        self.rd = []


class Prog:
    def __init__(self):
        self.ops = {e: [] for e in ENGS}
        self.cnt = dict(sq=0, ct=0, cv=0, cs=0, gq=0, cc=0, aq=0,
                        sq0=0, sq1=0, sqw=0, jq=0)
        self.lastdma = {}
        self.lastw = {e: {} for e in ENGS}

    def _wait(self, eng, tok):
        if tok is None:
            return
        s, v = tok
        if v <= self.lastw[eng].get(s, 0):
            return
        self.lastw[eng][s] = v
        self.ops[eng].append(("w", s, v))

    def op(self, eng, fn, reads=(), writes=(), deps=(), sem=None, inc=1):
        if sem is None and inc == 16:
            sem = SEM_OF[eng]
        if inc == 16 and sem in ("gq", "sqw"):
            if sem in self.lastdma:
                deps = list(deps) + [self.lastdma[sem]]
        want = {}
        for r in reads:
            if r.w is not None:
                want[r.w[0]] = max(want.get(r.w[0], 0), r.w[1])
        for r in writes:
            if r.w is not None and not r.rd:
                want[r.w[0]] = max(want.get(r.w[0], 0), r.w[1])
            for t in r.rd:
                want[t[0]] = max(want.get(t[0], 0), t[1])
        for d in deps:
            if d is not None:
                want[d[0]] = max(want.get(d[0], 0), d[1])
        for s_, v_ in want.items():
            self._wait(eng, (s_, v_))
        if sem is None:
            sem = SEM_OF[eng]
        self.cnt[sem] += inc
        tok = (sem, self.cnt[sem])
        self.ops[eng].append(("o", fn, sem, inc))
        for r in reads:
            r.rd.append(tok)
        for r in writes:
            r.w = tok
            r.rd = []
        if inc == 16 and sem in ("gq", "sqw"):
            self.lastdma[sem] = tok
        return tok

    def dma(self, eng, out, in_, reads=(), writes=(), deps=()):
        return self.op(eng, lambda E, o=out, i=in_: E.dma_start(out=o, in_=i),
                       reads=reads, writes=writes, deps=deps, inc=16)


def build_nc(taps=()):
    nc = bass.Bass("TRN2", target_bir_lowering=False)
    P = Prog()

    dI = lambda n, s: nc.dram_tensor(n, s, F32, kind="ExternalInput").ap()
    x_d = dI("x", [C, T])
    wq_d = dI("wq", [C, C])
    wkv_d = dI("wkv", [C, 2 * C])
    wproj_d = dI("wproj", [C, C])
    wc1_d = dI("wc1", [C, 2048])
    wc2_d = dI("wc2", [2048, C])
    wr_d = dI("wr", [C, C])
    wvv_d = dI("wvv", [C, C])
    wo_d = dI("wo", [C, C])
    wst_d = dI("wst", [C, C])
    w2_d = dI("w2", [C, C])
    w1_d = dI("w1", [C, C])
    w3_d = dI("w3", [C, C])
    wk_d = [dI("wk0", [C, 4096]), dI("wk1", [C, 4096])]
    wv_d = [dI("wve0", [4096, C]), dI("wve1", [4096, C])]
    wsk_d = dI("wsk", [C, 8])
    vecs_d = dI("vecs", [128, 80])
    cb1_d = dI("cb1", [128, 16])
    caps_d = dI("caps", [1, 4])
    onesc_d = dI("onesc", [128, 1])
    onesr_d = dI("onesr", [1, 128])
    one1_d = dI("one1", [1, 1])
    sel2a_d = dI("sel2a", [1, 128])
    sel2b_d = dI("sel2b", [1, 128])
    i128_d = dI("i128", [128, 128])

    out_d = nc.dram_tensor("out", [C, T], F32, kind="ExternalOutput").ap()
    tap_d = {}
    for tname, shape in taps:
        tap_d[tname] = nc.dram_tensor("tap_" + tname, shape, F32, kind="ExternalOutput").ap()

    cc_in = nc.dram_tensor("cc_in", [130, 512], F32, kind="Internal").ap()
    cc_out = nc.dram_tensor("cc_out", [130, 512], F32, kind="Internal").ap()
    accA_d = nc.dram_tensor("accA_d", [C, T], F32, kind="Internal").ap()
    s_d = nc.dram_tensor("s_d", [C, T], F32, kind="Internal").ap()
    wsw2p_d = nc.dram_tensor("wsw2p", [C, 2 * C], F32R, kind="Internal").ap()

    A = lambda n, s: nc.alloc_sbuf_tensor(n, s, F32).ap()
    XN = A("XN", [128, 16384])
    BB = A("BB", [128, 16384])
    SCR = A("SCR", [128, 8192])
    WST = A("WST", [128, 1024])
    SC2 = A("SC2", [128, 2048])
    RB = A("RB", [128, 1024])
    KVB = A("KVB", [128, 1780])
    ROWX = A("ROWX", [1, 1536])
    AR = lambda n, s: nc.alloc_sbuf_tensor(n, s, F32R).ap()
    XSP = AR("XSP", [128, 2048])
    WSTR = AR("WSTR", [128, 2048])
    VEC = A("VEC", [128, 80])
    CB1 = A("CB1", [128, 16])
    CAPS = A("CAPS", [128, 4])
    ONESC = A("ONESC", [128, 1])
    ONESR = A("ONESR", [1, 128])
    ONE1 = A("ONE1", [1, 1])
    SELT2 = A("SELT2", [66, 128])
    I128 = A("I128", [128, 128])
    BCOL = A("BCOL", [128, 40])
    WSKT = A("WSKT", [128, 64])
    PS = nc.alloc_psum_tensor("PS", [128, 4096]).ap()

    KV_PHIK = KVB[:, 0:512]
    KV_V = KVB[:, 512:1024]
    KV_SB = KVB[:, 1024:1536]
    KSUM2 = KVB[:, 1536:1552]
    R1C = KVB[:, 1552:1568]
    R1SQC = KVB[:, 1568:1584]
    R1NC = KVB[:, 1584:1600]
    MEC = KVB[:, 1600:1648]
    ROUT = KVB[:, 1648:1776]
    CAPS_STG = KVB[0:1, 1776:1780]
    SIGS = KVB[:, 1024:1088]
    BIDS = [KVB[:, 1088 + 16 * e:1104 + 16 * e] for e in range(3)]
    TMP1 = KVB[:, 1136:1152]
    TMP2 = KVB[:, 1152:1168]
    SELS = [KVB[:, 1168 + 16 * e:1184 + 16 * e] for e in range(3)]
    WCONF = KVB[:, 1216:1232]
    SCL = KVB[:, 1232:1248]

    rXN = [[Res() for _ in range(NG)] for _ in range(NCH)]
    rBB = [[Res() for _ in range(NG)] for _ in range(NCH)]
    rSCR = [Res() for _ in range(16)]
    rWST = [Res() for _ in range(2)]
    rSC2 = [Res() for _ in range(4)]
    rRB = [Res() for _ in range(2)]
    rKV = dict(phik=Res(), v=Res(), kvsb=Res(), ksum2=Res(), r1c=Res(), r1sqc=Res(),
               r1nc=Res(), mec=Res(), rout=Res(), caps=Res(), tmp=Res(), wskt=Res())
    rROWX = [Res() for _ in range(3)]
    rXSP = [Res(), Res()]
    rWSRh = [Res(), Res()]
    rWSRl = [Res(), Res()]
    rBANK = [Res() for _ in range(8)]
    rCONST = Res()
    rBCOL = [Res() for _ in range(5)]
    rDRAM = {n: Res() for n in ("cc_in", "cc_out", "accA_d", "s_d", "wsw2h",
                                "khi0", "klo0", "vhi0", "vlo0", "khi1", "klo1", "vhi1", "vlo1")}

    def bank(b):
        return PS[:, 512 * b:512 * (b + 1)]

    VNAMES = ["s1", "s2", "gamma", "pb", "cb2", "l1w", "l1b", "l2w", "l2b", "pad"]
    VCOL = {n: 8 * i for i, n in enumerate(VNAMES)}

    def vcols(n, c0=0, n_=8):
        return VEC[:, VCOL[n] + c0:VCOL[n] + c0 + n_]

    def vcol(n, c):
        return VEC[:, VCOL[n] + c:VCOL[n] + c + 1]

    def chk(buf, c, g):
        return buf[:, 2048 * c + G * g: 2048 * c + G * g + G]

    def span3(buf, c0, n, g):
        return buf[:, :].rearrange("p (c t) -> p c t", t=2048)[:, c0:c0 + n, G * g:G * (g + 1)]

    def vbc(vap, n, w):
        return vap.unsqueeze(2).to_broadcast([128, n, w])

    def rbc(slot, n):
        return slot.unsqueeze(1).to_broadcast([128, n, G])

    RBS = [RB[:, 512 * i:512 * (i + 1)] for i in range(2)]
    RXs = [ROWX[0:1, 512 * i:512 * (i + 1)] for i in range(3)]

    ld = []
    for dst, src in ((VEC[:, :], vecs_d[:, :]), (CB1[:, :], cb1_d[:, :]),
                     (ONESC[:, :], onesc_d[:, :]), (ONESR[:, :], onesr_d[:, :]),
                     (ONE1[:, :], one1_d[:, :]),
                     (SELT2[0:1, :], sel2a_d[:, :]), (SELT2[1:2, :], sel2b_d[:, :]),
                     (SELT2[32:33, :], sel2a_d[:, :]), (SELT2[33:34, :], sel2b_d[:, :]),
                     (SELT2[64:65, :], sel2a_d[:, :]), (SELT2[65:66, :], sel2b_d[:, :]),
                     (I128[:, :], i128_d[:, :]), (CAPS_STG, caps_d[:, :])):
        ld.append(P.dma("gpsimd", dst, src))
    rCONST.w = ld[-1]
    for c in range(NCH):
        P.dma("gpsimd", XN[:, 2048 * c:2048 * (c + 1)], x_d[128 * c:128 * (c + 1), :],
              writes=[rXN[c][g] for g in range(NG)])
    P.op("tensor", lambda E: E.matmul(PS[:, 0:4], ONESR[0:1, :], CAPS_STG,
                                      start=True, stop=True),
         reads=[rCONST], writes=[rBANK[0]], deps=ld)
    P.op("vector", lambda E: E.tensor_copy(CAPS[:, :], PS[:, 0:4]),
         reads=[rBANK[0]], writes=[rKV["caps"]])

    for g in range(NG):
        for c in range(NCH):
            P.op("tensor", lambda E, c=c, g=g: E.matmul(
                PS[0:1, 0:512], ONESC[:, :], chk(XN, c, g),
                start=(c == 0), stop=(c == NCH - 1)),
                reads=[rXN[c][g], rCONST],
                writes=([rBANK[0]] if c in (0, NCH - 1) else []))
        for hf in range(2):
            P.op("scalar", lambda E, hf=hf, g=g: E.activation(
                SC2[:, :].rearrange("p (c t) -> p c t", t=G)[:, 0:4, :],
                span3(XN, 4 * hf, 4, g), AF.Square),
                reads=[rXN[c][g] for c in range(4 * hf, 4 * hf + 4)],
                writes=rSC2[0:4])
            for c in range(4 * hf, 4 * hf + 4):
                P.op("tensor", lambda E, c=c, g=g: E.matmul(
                    PS[0:1, 512:1024], ONESC[:, :], SC2[:, 512 * (c % 4):512 * (c % 4 + 1)],
                    start=(c == 0), stop=(c == NCH - 1)),
                    reads=[rSC2[c % 4], rCONST],
                    writes=([rBANK[1]] if c in (0, NCH - 1) else []))
        P.op("vector", lambda E: E.tensor_scalar(RXs[0], PS[0:1, 0:512], 1.0 / C, None, AL.mult),
             reads=[rBANK[0]], writes=[rROWX[0]])
        P.op("vector", lambda E: E.tensor_scalar(RXs[1], PS[0:1, 512:1024], 1.0 / C, None, AL.mult),
             reads=[rBANK[1]], writes=[rROWX[1]])
        P.op("vector", lambda E: E.tensor_tensor(RXs[2], RXs[0], RXs[0], AL.mult),
             reads=[rROWX[0]], writes=[rROWX[2]])
        P.op("tensor", lambda E: E.matmul(bank(2), ONESR[0:1, :], RXs[0], start=True, stop=True),
             reads=[rROWX[0], rCONST], writes=[rBANK[2]])
        P.op("vector", lambda E: E.tensor_copy(RBS[0], bank(2)), reads=[rBANK[2]], writes=[rRB[0]])
        P.op("vector", lambda E: E.tensor_tensor(RXs[1], RXs[1], RXs[2], AL.subtract),
             reads=[rROWX[2]], writes=[rROWX[1]])
        P.op("vector", lambda E: E.tensor_scalar(RXs[2], RXs[1], EPS_LN, None, AL.add),
             reads=[rROWX[1]], writes=[rROWX[2]])
        P.op("vector", lambda E: E.reciprocal(RXs[0], RXs[2]),
             reads=[rROWX[2]], writes=[rROWX[0]])
        P.op("scalar", lambda E: E.activation(RXs[2], RXs[0], AF.Sqrt),
             reads=[rROWX[0]], writes=[rROWX[2]])
        P.op("vector", lambda E: E.tensor_tensor(RXs[0], RXs[1], RXs[0], AL.mult),
             reads=[rROWX[1]], writes=[rROWX[0]])
        kq = ((SQRTC + EPS_RMS) / SQRTC) ** 2
        P.op("vector", lambda E: E.reciprocal(RXs[1], RXs[0]),
             reads=[rROWX[0]], writes=[rROWX[1]])
        t_r1row = P.op("scalar", lambda E: E.activation(RXs[1], RXs[1], AF.Sqrt, scale=kq),
                       reads=[], writes=[rROWX[1]])
        for u in range(4):
            P.op("tensor", lambda E, u=u: E.matmul(
                PS[0:128, 2048 + u:2048 + u + 1], RXs[1][0:1, 128 * u:128 * (u + 1)],
                ONE1[0:1, :], start=(u == 0), stop=(u == 3)),
                reads=[rROWX[1], rCONST],
                writes=([rBANK[4]] if u in (0, 3) else []))
        P.op("vector", lambda E, g=g: E.tensor_copy(R1C[:, 4 * g:4 * g + 4], PS[:, 2048:2052]),
             reads=[rBANK[4]], writes=([rKV["r1c"]] if g == NG - 1 else []))
        P.op("vector", lambda E: E.tensor_scalar(RXs[0], RXs[0], EPS_LN, None, AL.add),
             writes=[rROWX[0]])
        P.op("vector", lambda E: E.reciprocal(RXs[0], RXs[0]), writes=[rROWX[0]])
        P.op("scalar", lambda E: E.activation(RXs[0], RXs[0], AF.Sqrt), writes=[rROWX[0]])
        for u in range(4):
            P.op("tensor", lambda E, u=u: E.matmul(
                PS[0:128, 2560 + u:2560 + u + 1], RXs[0][0:1, 128 * u:128 * (u + 1)],
                ONE1[0:1, :], start=(u == 0), stop=(u == 3)),
                reads=[rROWX[0], rCONST],
                writes=([rBANK[5]] if u in (0, 3) else []))
        P.op("vector", lambda E, g=g: E.tensor_copy(R1NC[:, 4 * g:4 * g + 4], PS[:, 2560:2564]),
             reads=[rBANK[5]], writes=([rKV["r1nc"]] if g == NG - 1 else []))
        P.op("tensor", lambda E: E.matmul(bank(3), ONESR[0:1, :], RXs[2], start=True, stop=True),
             reads=[rROWX[2], rCONST], writes=[rBANK[3]])
        P.op("vector", lambda E: E.tensor_copy(RBS[1], bank(3)), reads=[rBANK[3]], writes=[rRB[1]])
        for hf in range(2):
            cres = [rXN[c][g] for c in range(4 * hf, 4 * hf + 4)]
            P.op("vector", lambda E, hf=hf, g=g: E.tensor_tensor(
                span3(XN, 4 * hf, 4, g), span3(XN, 4 * hf, 4, g), rbc(RBS[0], 4), AL.subtract),
                reads=[rRB[0]], writes=cres)
            P.op("vector", lambda E, hf=hf, g=g: E.tensor_tensor(
                span3(XN, 4 * hf, 4, g), span3(XN, 4 * hf, 4, g), rbc(RBS[1], 4), AL.mult),
                reads=[rRB[1]], writes=cres)
    P.op("vector", lambda E: E.tensor_tensor(R1SQC[:, :], R1C[:, :], R1C[:, :], AL.mult),
         reads=[rKV["r1c"]], writes=[rKV["r1sqc"]])

    WKH = SCR[:, 0:4096]
    WVH = SCR[:, 4096:8192]
    kv_mm_tok = []
    for p in range(2):
        t_wk = []
        for half2 in range(2):
            t_wk.append(P.op("sync", lambda E, half2=half2, p=p: E.dma_start(
                out=WKH[:, 2048 * half2:2048 * (half2 + 1)].rearrange("p (u f) -> p u f", f=512),
                in_=wkv_d[:, 512 * p:512 * (p + 1)]
                .rearrange("(u p) f -> p u f", p=128)[:, 4 * half2:4 * half2 + 4, :]),
                writes=rSCR[4 * half2:4 * half2 + 4], inc=16, sem="sqw"))
            t_wk.append(P.op("sync", lambda E, half2=half2, p=p: E.dma_start(
                out=WVH[:, 2048 * half2:2048 * (half2 + 1)].rearrange("p (u f) -> p u f", f=512),
                in_=wkv_d[:, 1024 + 512 * p:1024 + 512 * (p + 1)]
                .rearrange("(u p) f -> p u f", p=128)[:, 4 * half2:4 * half2 + 4, :]),
                writes=rSCR[8 + 4 * half2:8 + 4 * half2 + 4], inc=16, sem="sqw"))
        t_sk = P.op("vector", lambda E: E.tensor_tensor(
            WKH.rearrange("p (c f) -> p c f", f=512), WKH.rearrange("p (c f) -> p c f", f=512),
            vbc(vcols("s1"), 8, 512), AL.mult),
            reads=[rCONST], writes=rSCR[0:4], deps=t_wk)
        t_sv = P.op("vector", lambda E: E.tensor_tensor(
            WVH.rearrange("p (c f) -> p c f", f=512), WVH.rearrange("p (c f) -> p c f", f=512),
            vbc(vcols("s1"), 8, 512), AL.mult),
            reads=[rCONST], writes=rSCR[8:12], deps=t_wk)
        for tc in range(16):
            g, tsub = tc // 4, tc % 4
            xres = [rXN[c][g] for c in range(NCH)]
            for c in range(NCH):
                P.op("tensor", lambda E, c=c, tc=tc: E.matmul(
                    bank(0),
                    XN[:, 2048 * c + 128 * tc:2048 * c + 128 * tc + 128],
                    WKH[:, 512 * c:512 * (c + 1)],
                    start=(c == 0), stop=(c == NCH - 1)),
                    reads=[rXN[c][g], rSCR[c]],
                    writes=([rBANK[0]] if c in (0, NCH - 1) else []))
            P.op("scalar", lambda E, tc=tc: E.activation(
                SC2[:, 0:512], bank(0), AF.Square, scale=R1C[:, tc:tc + 1]),
                reads=[rBANK[0], rKV["r1c"]], writes=[rSC2[0]])
            P.op("scalar", lambda E: E.activation(KV_PHIK, SC2[:, 0:512], AF.Exp, scale=-0.5),
                 reads=[rSC2[0]], writes=[rKV["phik"]])
            for c in range(NCH):
                P.op("tensor", lambda E, c=c, tc=tc: E.matmul(
                    bank(1),
                    XN[:, 2048 * c + 128 * tc:2048 * c + 128 * tc + 128],
                    WVH[:, 512 * c:512 * (c + 1)],
                    start=(c == 0), stop=(c == NCH - 1)),
                    reads=[rXN[c][g], rSCR[8 + c]],
                    writes=([rBANK[1]] if c in (0, NCH - 1) else []))
            P.op("scalar", lambda E, tc=tc: E.activation(
                KV_V, bank(1), AF.Copy, scale=R1C[:, tc:tc + 1]),
                reads=[rBANK[1], rKV["r1c"]], writes=[rKV["v"]])
            for hl in range(8):
                P.op("tensor", lambda E, hl=hl, tc=tc: E.matmul(
                    PS[64 * (hl % 2):64 * (hl % 2) + 64,
                       512 * (4 + hl // 2):512 * (4 + hl // 2) + 64],
                    KV_PHIK[:, 64 * hl:64 * hl + 64],
                    KV_V[:, 64 * hl:64 * hl + 64],
                    start=(tc == 0), stop=(tc == 15)),
                    reads=[rKV["phik"], rKV["v"]],
                    writes=([rBANK[4 + hl // 2]] if tc in (0, 15) else []))
            P.op("tensor", lambda E, tc=tc: E.matmul(
                PS[0:1, 1024:1536], ONESC[:, :], KV_PHIK,
                start=(tc == 0), stop=(tc == 15)),
                reads=[rKV["phik"], rCONST],
                writes=([rBANK[2]] if tc in (0, 15) else []))
        for j in range(4):
            P.op("vector", lambda E, j=j, p=p: E.tensor_copy(
                KV_SB[:, 256 * p + 64 * j:256 * p + 64 * j + 64],
                PS[:, 512 * (4 + j):512 * (4 + j) + 64]),
                reads=[rBANK[4 + j]], writes=([rKV["kvsb"]] if p == 1 and j == 3 else []))
        P.op("vector", lambda E, p=p: E.tensor_copy(RXs[p], PS[0:1, 1024:1536]),
             reads=[rBANK[2]], writes=[rROWX[p]])
    t1 = P.dma("gpsimd", cc_in[0:128, :], KV_SB, reads=[rKV["kvsb"]], writes=[rDRAM["cc_in"]])
    t2 = P.dma("gpsimd", cc_in[128:129, :], RXs[0], reads=[rROWX[0]])
    t3 = P.dma("gpsimd", cc_in[129:130, :], RXs[1], reads=[rROWX[1]])
    t_cc = P.op("gpsimd", lambda E: E.collective_compute(
        "AllReduce", AL.add,
        replica_groups=[[0, 1, 2, 3], [4, 5, 6, 7]],
        ins=[cc_in[:, :]], outs=[cc_out[:, :]]),
        deps=[t1, t2, t3], sem="cc", inc=1)
    t_kvret = P.dma("gpsimd", KV_SB, cc_out[0:128, :], writes=[rKV["kvsb"]], deps=[t_cc])
    t_ks0 = P.dma("gpsimd", RXs[0], cc_out[128:129, :], writes=[rROWX[0]], deps=[t_cc])
    t_ks1 = P.dma("gpsimd", RXs[1], cc_out[129:130, :], writes=[rROWX[1]], deps=[t_cc])
    def wset_hi(s):
        return WSTR[:, 1024 * s:1024 * s + 512]

    def wset_lo(s):
        return WSTR[:, 1024 * s + 512:1024 * s + 1024]

    def xsp_pair(pr):
        return (XSP[:, 1024 * pr:1024 * pr + 512],
                XSP[:, 1024 * pr + 512:1024 * pr + 1024])

    def split_act(src_ap, src_res, pr, scale_col=None):
        XH, XL = xsp_pair(pr)
        if scale_col is None:
            P.op("scalar", lambda E, o=XH, i=src_ap: E.activation(o, i, AF.Copy),
                 reads=[src_res], writes=[rXSP[pr]])
            P.op("vector", lambda E, o=XL, i=src_ap, h=XH: E.tensor_tensor(o, i, h, AL.subtract),
                 reads=[src_res], writes=[rXSP[pr]])
        else:
            tb = SC2[:, 1536:2048]
            P.op("vector", lambda E, i=src_ap, sc=scale_col: E.tensor_scalar(
                tb, i, sc, None, AL.mult),
                reads=[src_res, rCONST], writes=[rSC2[3]])
            P.op("scalar", lambda E, o=XH: E.activation(o, tb, AF.Copy),
                 reads=[rSC2[3]], writes=[rXSP[pr]])
            P.op("vector", lambda E, o=XL, h=XH: E.tensor_tensor(o, tb, h, AL.subtract),
                 reads=[rSC2[3]], writes=[rXSP[pr]])
        return XH, XL

    def wdma_raw_h(wdram, r0, cols0, colh, ki):
        s = colh
        stg = WST[:, 512 * s:512 * s + 512]
        P.op("sync", lambda E, w=wdram, r0=r0, c0=cols0 + 512 * colh, stg=stg: E.dma_start(
            out=stg, in_=w[r0:r0 + 128, c0:c0 + 512]),
            writes=[rWST[s]], inc=16, sem="sqw")

    def wproc_raw_h(colh, ki):
        s = colh
        stg = WST[:, 512 * s:512 * s + 512]
        P.op("scalar", lambda E, s=s, stg=stg: E.activation(wset_hi(s), stg, AF.Copy),
             reads=[rWST[s]], writes=[rWSRh[s]])
        P.op("vector", lambda E, s=s, stg=stg: E.tensor_tensor(
            wset_lo(s), stg, wset_hi(s), AL.subtract),
            reads=[rWST[s], rWSRh[s]], writes=[rWSRl[s]])

    EXPSTG = [WST[:, 0:512], WST[:, 512:1024],
              SCR[:, 4096:4608], SCR[:, 4608:5120]]
    EXPRES_IDX = [("wst", 0), ("wst", 1), ("scr", 8), ("scr", 9)]

    def _expres(st):
        kind, i = EXPRES_IDX[st]
        return rWST[i] if kind == "wst" else rSCR[i]

    def wdma_exp(wdram, r0, cols0, colh, ki):
        st = 2 * (ki % 2) + colh
        eng, sm = ("sync", "sqw") if colh == 0 else ("gpsimd", "gq")
        P.op(eng, lambda E, w=wdram, r0=r0, c0=cols0 + 512 * colh, st=st: E.dma_start(
            out=EXPSTG[st], in_=w[r0:r0 + 128, c0:c0 + 512]),
            writes=[_expres(st)], inc=16, sem=sm)

    def wproc_exp(colh, ki):
        st = 2 * (ki % 2) + colh
        s = colh
        P.op("scalar", lambda E, s=s, st=st: E.activation(wset_hi(s), EXPSTG[st], AF.Copy),
             reads=[_expres(st)], writes=[rWSRh[s]])
        P.op("vector", lambda E, s=s, st=st: E.tensor_tensor(
            wset_lo(s), EXPSTG[st], wset_hi(s), AL.subtract),
            reads=[_expres(st), rWSRh[s]], writes=[rWSRl[s]])

    def wload_presplit_h(pk_d, r0, cols0, colh, ki, rdep):
        s = colh
        pc0 = 2 * (cols0 + 512 * colh)
        P.op("sync", lambda E, d=pk_d, r0=r0, pc0=pc0, s=s: E.dma_start(
            out=WSTR[:, 1024 * s:1024 * (s + 1)], in_=d[r0:r0 + 128, pc0:pc0 + 1024]),
            reads=[rdep], writes=[rWSRh[s], rWSRl[s]], inc=16, sem="sq%d" % s)

    def mm3h(ki, nk, colh, XH, XL, rx):
        s = colh
        for j in range(4):
            b = 4 * colh + j
            for p_, (lhs, rhs) in enumerate((
                    (wset_hi(s)[:, 128 * j:128 * (j + 1)], XH),
                    (wset_lo(s)[:, 128 * j:128 * (j + 1)], XH),
                    (wset_hi(s)[:, 128 * j:128 * (j + 1)], XL))):
                P.op("tensor", lambda E, b=b, lhs=lhs, rhs=rhs, ki=ki, nk=nk,
                     p_=p_: E.matmul(
                         bank(b), lhs, rhs,
                         start=(ki == 0 and p_ == 0),
                         stop=(ki == nk - 1 and p_ == 2)),
                     reads=[rWSRh[s], rWSRl[s], rx],
                     writes=([rBANK[b]]
                             if (ki == 0 and p_ == 0) or (ki == nk - 1 and p_ == 2)
                             else []))

    def grp3p(nk, wdma_fn, wproc_fn, act_fn, evacq_fn, defer=False):
        wdma_fn(0, 0)
        wdma_fn(1, 0)
        if wproc_fn is not None:
            wproc_fn(0, 0)
        cur = act_fn(0)
        for ki in range(nk):
            if wproc_fn is not None:
                wproc_fn(1, ki)
            nxt = act_fn(ki + 1) if ki + 1 < nk else None
            pump_job(1)
            mm3h(ki, nk, 0, *cur)
            if ki + 1 < nk:
                wdma_fn(0, ki + 1)
            if ki == nk - 1 and not defer:
                evacq_fn(0)
            if wproc_fn is not None and ki + 1 < nk:
                wproc_fn(0, ki + 1)
            pump_job(1)
            mm3h(ki, nk, 1, *cur)
            if ki + 1 < nk:
                wdma_fn(1, ki + 1)
            cur = nxt
            if ki == nk - 1 and not defer:
                evacq_fn(1)
        if defer:
            evacq_fn(0)
            evacq_fn(1)

    job_items = []
    JSTG = KVB[:, 0:512]
    JHI = KVB[:, 512:1024]
    rJOB = Res()
    jst = {"done": 0, "in": 0}

    def pump_job(n=2):
        for _ in range(n):
            if jst["done"] < jst["in"]:
                src, hi, lo, dres = job_items[jst["done"]]
                t_hi = P.op("scalar", lambda E: E.activation(JHI.bitcast(F32R), JSTG, AF.Copy),
                            reads=[rJOB])
                P.op("gpsimd", lambda E, hi=hi: E.dma_start(out=hi, in_=JHI.bitcast(F32R)),
                     deps=[t_hi], reads=[rJOB], inc=16, sem="jq")
                t_lo = P.op("vector", lambda E: E.tensor_tensor(
                    JSTG.bitcast(F32R), JSTG, JHI.bitcast(F32R), AL.subtract),
                    deps=[t_hi], reads=[rJOB], writes=[rJOB])
                t_st = P.op("gpsimd", lambda E, lo=lo: E.dma_start(out=lo, in_=JSTG.bitcast(F32R)),
                            deps=[t_lo], reads=[rJOB], inc=16, sem="jq")
                if dres is not None:
                    dres.w = t_st
                jst["done"] += 1
            if jst["in"] < len(job_items):
                src, hi, lo, dres = job_items[jst["in"]]
                w = [rJOB] + ([rKV["phik"], rKV["v"]] if jst["in"] == 0 else [])
                P.op("gpsimd", lambda E, src=src: E.dma_start(out=JSTG, in_=src),
                     writes=w, inc=16, sem="jq")
                jst["in"] += 1

    def pump_rest():
        while jst["done"] < len(job_items):
            pump_job(2)

    def regen_bcast_from_cols(cols_ap, cols_res, g, rb_i, pbank):
        for u in range(4):
            P.op("tensor", lambda E, u=u, g=g, cols_ap=cols_ap, pbank=pbank: E.matmul(
                PS[0:1, 512 * pbank + 128 * u:512 * pbank + 128 * (u + 1)],
                cols_ap[:, 4 * g + u:4 * g + u + 1], I128[:, :],
                start=(u == 0), stop=(u == 3)),
                reads=[cols_res, rCONST],
                writes=([rBANK[pbank]] if u in (0, 3) else []))
        t_row = P.op("vector", lambda E, pbank=pbank: E.tensor_copy(
            RXs[2], PS[0:1, 512 * pbank:512 * pbank + 512]),
            reads=[rBANK[pbank]], writes=[rROWX[2]])
        P.op("tensor", lambda E, pbank=pbank: E.matmul(
            PS[:, 512 * pbank:512 * pbank + 512], ONESR[0:1, :], RXs[2],
            start=True, stop=True),
            reads=[rROWX[2], rCONST], writes=[rBANK[pbank]])
        return P.op("vector", lambda E, rb_i=rb_i, pbank=pbank: E.tensor_copy(
            RBS[rb_i], PS[:, 512 * pbank:512 * pbank + 512]),
            reads=[rBANK[pbank]], writes=[rRB[rb_i]])

    for fh in range(2):
        for k in range(NCH):
            t_a = P.op("sync", lambda E, k=k: E.dma_start(
                out=WST[:, 0:1024], in_=wst_d[128 * k:128 * (k + 1), :]),
                writes=[rWST[0], rWST[1]], inc=16, sem="sqw")
            t_b = P.op("sync", lambda E, k=k: E.dma_start(
                out=SCR[:, 0:1024], in_=w2_d[128 * k:128 * (k + 1), :]),
                writes=rSCR[0:2], inc=16, sem="sqw")
            for c in range(NCH):
                P.op("tensor", lambda E, c=c, fh=fh, k=k: E.matmul(
                    bank(c), WST[:, 128 * c:128 * (c + 1)],
                    SCR[:, 512 * fh:512 * fh + 512],
                    start=(k == 0), stop=(k == NCH - 1)),
                    reads=[rWST[0], rWST[1], rSCR[fh]],
                    writes=([rBANK[c]] if k in (0, NCH - 1) else []))
        for c in range(NCH):
            sa = SC2[:, 512 * (2 * (c % 2)):512 * (2 * (c % 2)) + 512]
            sb = SC2[:, 512 * (2 * (c % 2) + 1):512 * (2 * (c % 2) + 1) + 512]
            ra, rb_ = rSC2[2 * (c % 2)], rSC2[2 * (c % 2) + 1]
            P.op("vector", lambda E, c=c, sa=sa: E.tensor_copy(sa, bank(c)),
                 reads=[rBANK[c]], writes=[ra])
            P.op("scalar", lambda E, sa=sa, sb=sb: E.activation(sb.bitcast(F32R), sa, AF.Copy),
                 reads=[ra], writes=[rb_])
            P.dma("gpsimd", wsw2p_d[128 * c:128 * (c + 1), 1024 * fh:1024 * fh + 512],
                  sb.bitcast(F32R), reads=[rb_])
            P.op("vector", lambda E, sa=sa, sb=sb: E.tensor_tensor(
                sa.bitcast(F32R), sa, sb.bitcast(F32R), AL.subtract),
                reads=[rb_], writes=[ra])
            P.dma("gpsimd", wsw2p_d[128 * c:128 * (c + 1), 1024 * fh + 512:1024 * fh + 1024],
                  sa.bitcast(F32R), reads=[ra],
                  writes=([rDRAM["wsw2h"]] if fh == 1 and c == NCH - 1 else []))


    for g in range(NG):
        regen_bcast_from_cols(R1SQC, rKV["r1sqc"], g, 0, 0)

        def ev_q(colh, g=g):
            for f in range(4 * colh, 4 * colh + 4):
                st = SC2[:, 512 * (f % 3):512 * (f % 3) + 512]
                P.op("scalar", lambda E, f=f, st=st: E.activation(st, bank(f), AF.Square),
                     reads=[rBANK[f]], writes=[rSC2[f % 3]])
                P.op("vector", lambda E, st=st: E.tensor_tensor(st, st, RBS[0], AL.mult),
                     reads=[rRB[0]], writes=[rSC2[f % 3]])
                P.op("scalar", lambda E, f=f, g=g, st=st: E.activation(
                    chk(BB, f, g), st, AF.Exp, scale=-0.5),
                    reads=[rSC2[f % 3]], writes=[rBB[f][g]])
        grp3p(NCH,
              lambda colh, ki: wdma_raw_h(wq_d, 128 * ki, 0, colh, ki),
              wproc_raw_h,
              lambda ki, g=g: split_act(chk(XN, ki, g), rXN[ki][g], ki % 2,
                                        scale_col=vcol("s1", ki)) + (rXSP[ki % 2],),
              ev_q)

    for par in range(2):
        for j in range(8):
            h = 2 * j + par
            p_, hl = h // 8, h % 8
            P.op("tensor", lambda E, par=par, j=j, hl=hl, p_=p_: E.matmul(
                PS[64 * par:64 * par + 64, 1536 + 2 * j + par:1536 + 2 * j + par + 1],
                RXs[p_][0:1, 64 * hl:64 * hl + 64], ONE1[0:1, :],
                start=(j == 0), stop=(j == 7)),
                reads=[rROWX[p_], rCONST],
                deps=[t_ks0, t_ks1],
                writes=([rBANK[3]] if j in (0, 7) else []))
    P.op("gpsimd", lambda E: E.iota(KSUM2[:, :], [[0, 16]], base=0, channel_multiplier=0,
                                      allow_small_or_imprecise_dtypes=True),
         writes=[rKV["ksum2"]], sem="cc", inc=1)
    for h in range(16):
        P.op("vector", lambda E, h=h, pb=64 * (h % 2): E.tensor_copy(
            KSUM2[pb:pb + 64, h:h + 1], PS[pb:pb + 64, 1536 + h:1537 + h]),
            reads=[rBANK[3]], writes=[rKV["ksum2"]])

    for g in range(NG):
        SBK = [1, 1, 1, 5, 5, 5, 6, 6]
        PBS = [0, 32, 64, 0, 32, 64, 0, 32]
        for cq in range(NCH):
            sb = SBK[cq]
            pb = PBS[cq]
            P.op("tensor", lambda E, cq=cq, g=g, sb=sb, pb=pb: E.matmul(
                PS[pb:pb + 2, 512 * sb:512 * sb + 512], KSUM2[:, 2 * cq:2 * cq + 2],
                chk(BB, cq, g), start=True, stop=True),
                reads=[rKV["ksum2"], rBB[cq][g]], writes=[rBANK[sb]])
        ZB = [SC2[:, 512:1024], SC2[:, 1024:1536], SC2[:, 1536:2048]]
        for i, sb in enumerate((1, 5, 6)):
            P.op("vector", lambda E, i=i, sb=sb: E.tensor_scalar(
                ZB[i], PS[:, 512 * sb:512 * sb + 512], 1e-6, None, AL.add),
                reads=[rBANK[sb]], writes=[rSC2[1 + i]])
            P.op("vector", lambda E, i=i: E.reciprocal(ZB[i], ZB[i]),
                 writes=[rSC2[1 + i]])
        for cq in range(NCH):
            nb = 0 if cq % 2 == 0 else 2
            zb = 3 if cq % 2 == 0 else 4
            zi = cq // 3
            pb = PBS[cq]
            P.op("tensor", lambda E, cq=cq, g=g, nb=nb: E.matmul(
                PS[0:64, 512 * nb:512 * nb + 512], KV_SB[0:64, 64 * cq:64 * cq + 64],
                chk(BB, cq, g)[0:64, :], start=True, stop=True),
                reads=[rKV["kvsb"], rBB[cq][g]], writes=[rBANK[nb]])
            P.op("tensor", lambda E, cq=cq, g=g, nb=nb: E.matmul(
                PS[64:128, 512 * nb:512 * nb + 512], KV_SB[64:128, 64 * cq:64 * cq + 64],
                chk(BB, cq, g)[64:128, :], start=True, stop=True),
                reads=[rKV["kvsb"], rBB[cq][g]], writes=[rBANK[nb]])
            P.op("tensor", lambda E, zb=zb, zi=zi, pb=pb: E.matmul(
                PS[:, 512 * zb:512 * zb + 512], SELT2[pb:pb + 2, :], ZB[zi][pb:pb + 2, :],
                start=True, stop=True),
                reads=[rSC2[1 + zi], rCONST], writes=[rBANK[zb]])
            P.op("vector", lambda E, p=cq % 2, zb=zb: E.tensor_copy(
                RBS[p], PS[:, 512 * zb:512 * zb + 512]),
                reads=[rBANK[zb]], writes=[rRB[cq % 2]])
            P.op("vector", lambda E, cq=cq, g=g, nb=nb, p=cq % 2: E.tensor_tensor(
                chk(BB, cq, g), PS[:, 512 * nb:512 * nb + 512], RBS[p], AL.mult),
                reads=[rBANK[nb], rRB[cq % 2]], writes=[rBB[cq][g]])

    P.op("vector", lambda E: E.tensor_tensor(BCOL[:, 24:32], vcols("gamma"), vcols("pb"), AL.mult),
         reads=[rCONST], writes=[rBCOL[3]])
    for g in range(NG):
        def ev_proj(hf, g=g):
            P.op("vector", lambda E, hf=hf: E.tensor_tensor(
                SC2[:, :].rearrange("p (c t) -> p c t", t=G)[:, 0:4, :],
                PS[:, 2048 * hf:2048 * (hf + 1)].rearrange("p (c t) -> p c t", t=G)[:, 0:4, :],
                vbc(vcols("gamma", 4 * hf, 4), 4, G), AL.mult),
                reads=[rBANK[4 * hf + i] for i in range(4)] + [rCONST], writes=rSC2[0:4])
            P.op("vector", lambda E, hf=hf: E.tensor_tensor(
                SC2[:, :].rearrange("p (c t) -> p c t", t=G)[:, 0:4, :],
                SC2[:, :].rearrange("p (c t) -> p c t", t=G)[:, 0:4, :],
                vbc(BCOL[:, 24 + 4 * hf:24 + 4 * hf + 4], 4, G), AL.add),
                reads=[rBCOL[3]], writes=rSC2[0:4])
            for c in range(4 * hf, 4 * hf + 4):
                P.op("vector", lambda E, c=c, g=g: E.tensor_tensor(
                    chk(BB, c, g), chk(XN, c, g), SC2[:, 512 * (c % 4):512 * (c % 4) + 512],
                    AL.add),
                    reads=[rXN[c][g], rSC2[c % 4]], writes=[rBB[c][g]])
        grp3p(NCH,
              lambda colh, ki: wdma_raw_h(wproj_d, 128 * ki, 0, colh, ki),
              wproc_raw_h,
              lambda ki, g=g: split_act(chk(BB, ki, g), rBB[ki][g], ki % 2) + (rXSP[ki % 2],),
              ev_proj, defer=True)


    P.op("vector", lambda E: E.tensor_tensor(BCOL[:, 32:40], vcols("gamma"), vcols("cb2"), AL.mult),
         reads=[rCONST], writes=[rBCOL[4]])

    def ev_c2(hf, g):
        P.op("vector", lambda E, hf=hf: E.tensor_tensor(
            SC2[:, :].rearrange("p (c t) -> p c t", t=G)[:, 0:4, :],
            PS[:, 2048 * hf:2048 * (hf + 1)].rearrange("p (c t) -> p c t", t=G)[:, 0:4, :],
            vbc(vcols("gamma", 4 * hf, 4), 4, G), AL.mult),
            reads=[rBANK[4 * hf + i] for i in range(4)] + [rCONST], writes=rSC2[0:4])
        P.op("vector", lambda E, hf=hf: E.tensor_tensor(
            SC2[:, :].rearrange("p (c t) -> p c t", t=G)[:, 0:4, :],
            SC2[:, :].rearrange("p (c t) -> p c t", t=G)[:, 0:4, :],
            vbc(BCOL[:, 32 + 4 * hf:32 + 4 * hf + 4], 4, G), AL.add),
            reads=[rBCOL[4]], writes=rSC2[0:4])
        for c in range(4 * hf, 4 * hf + 4):
            P.op("vector", lambda E, c=c, g=g: E.tensor_tensor(
                chk(BB, c, g), chk(BB, c, g), SC2[:, 512 * (c % 4):512 * (c % 4) + 512],
                AL.add),
                reads=[rSC2[c % 4]], writes=[rBB[c][g]])

    for g in range(NG):
        for hf in range(2):
            P.op("scalar", lambda E, hf=hf, g=g: E.activation(
                SC2[:, :].rearrange("p (c t) -> p c t", t=G)[:, 0:4, :],
                span3(BB, 4 * hf, 4, g), AF.Square),
                reads=[rBB[c][g] for c in range(4 * hf, 4 * hf + 4)],
                writes=rSC2[0:4])
            for c in range(4 * hf, 4 * hf + 4):
                P.op("tensor", lambda E, c=c: E.matmul(
                    PS[0:1, 0:512], ONESC[:, :], SC2[:, 512 * (c % 4):512 * (c % 4 + 1)],
                    start=(c == 0), stop=(c == NCH - 1)),
                    reads=[rSC2[c % 4], rCONST],
                    writes=([rBANK[0]] if c in (0, NCH - 1) else []))
        kq2 = (SQRTC + EPS_RMS) ** 2
        P.op("vector", lambda E: E.reciprocal(RXs[0], PS[0:1, 0:512]),
             reads=[rBANK[0]], writes=[rROWX[0]])
        P.op("scalar", lambda E: E.activation(RXs[0], RXs[0], AF.Sqrt, scale=kq2),
             writes=[rROWX[0]])
        P.op("tensor", lambda E: E.matmul(PS[:, 0:512], ONESR[0:1, :], RXs[0],
                                          start=True, stop=True),
             reads=[rROWX[0], rCONST], writes=[rBANK[0]])
        P.op("vector", lambda E: E.tensor_copy(RBS[0], PS[:, 0:512]),
             reads=[rBANK[0]], writes=[rRB[0]])
        for hhalf in range(2):
            def ev_c1(colh, hhalf=hhalf, g=g):
                for f in range(4 * colh, 4 * colh + 4):
                    h = 8 * hhalf + f
                    st = SC2[:, 512 * (f % 3):512 * (f % 3) + 512]
                    P.op("vector", lambda E, f=f, st=st: E.tensor_tensor(
                        st, bank(f), RBS[0], AL.mult),
                        reads=[rBANK[f], rRB[0]], writes=[rSC2[f % 3]])
                    P.op("scalar", lambda E, h=h, st=st: E.activation(
                        SCR[:, 512 * h:512 * (h + 1)], st, AF.Gelu, bias=CB1[:, h:h + 1]),
                        reads=[rSC2[f % 3], rCONST], writes=[rSCR[h]])
            grp3p(NCH,
                  lambda colh, ki, hhalf=hhalf: wdma_raw_h(
                      wc1_d, 128 * ki, 1024 * hhalf, colh, ki),
                  wproc_raw_h,
                  lambda ki, g=g: split_act(chk(BB, ki, g), rBB[ki][g], ki % 2,
                                            scale_col=vcol("s2", ki)) + (rXSP[ki % 2],),
                  ev_c1)
        grp3p(16,
              lambda colh, h: wdma_raw_h(wc2_d, 128 * h, 0, colh, h),
              wproc_raw_h,
              lambda h: split_act(SCR[:, 512 * h:512 * (h + 1)], rSCR[h], h % 2)
              + (rXSP[h % 2],),
              lambda colh: ev_c2(colh, g))

    for c in range(NCH):
        P.dma("gpsimd", accA_d[128 * c:128 * (c + 1), :], BB[:, 2048 * c:2048 * (c + 1)],
              reads=[rBB[c][g] for g in range(NG)],
              writes=([rDRAM["accA_d"]] if c == NCH - 1 else []))


    for g in range(NG):
        regen_bcast_from_cols(R1NC, rKV["r1nc"], g, 1, 0)
        for hf in range(2):
            bres = [rBB[c][g] for c in range(4 * hf, 4 * hf + 4)]
            P.op("vector", lambda E, hf=hf, g=g: E.tensor_tensor(
                span3(BB, 4 * hf, 4, g), span3(XN, 4 * hf, 4, g), rbc(RBS[1], 4), AL.mult),
                reads=[rRB[1]] + [rXN[c][g] for c in range(4 * hf, 4 * hf + 4)],
                writes=bres)
            P.op("vector", lambda E, hf=hf, g=g: E.tensor_tensor(
                span3(BB, 4 * hf, 4, g), span3(BB, 4 * hf, 4, g),
                vbc(vcols("l1w", 4 * hf, 4), 4, G), AL.mult),
                reads=[rCONST], writes=bres)
            P.op("vector", lambda E, hf=hf, g=g: E.tensor_tensor(
                span3(BB, 4 * hf, 4, g), span3(BB, 4 * hf, 4, g),
                vbc(vcols("l1b", 4 * hf, 4), 4, G), AL.add),
                reads=[rCONST], writes=bres)

    for g in range(NG):
        def ev_s(colh, g=g):
            for f in range(4 * colh, 4 * colh + 4):
                st = SC2[:, 512 * (f % 3):512 * (f % 3) + 512]
                P.op("scalar", lambda E, f=f, st=st: E.activation(st, bank(f), AF.Sigmoid),
                     reads=[rBANK[f]], writes=[rSC2[f % 3]])
                P.dma("gpsimd", s_d[128 * f:128 * (f + 1), G * g:G * (g + 1)], st,
                      reads=[rSC2[f % 3]],
                      writes=([rDRAM["s_d"]] if f == NCH - 1 and g == NG - 1 else []))
        grp3p(NCH,
              lambda colh, ki: wload_presplit_h(wsw2p_d, 128 * ki, 0, colh, ki,
                                                rDRAM["wsw2h"]),
              None,
              lambda ki, g=g: split_act(chk(BB, ki, g), rBB[ki][g], ki % 2) + (rXSP[ki % 2],),
              ev_s)

    def ghalf(c, gg):
        return SCR[:, 1024 * c + 512 * gg:1024 * c + 512 * gg + 512]

    def rghalf(c, gg):
        return rSCR[2 * c + gg]

    for h2 in range(2):
        for wi, wdram in ((0, wr_d), (1, wvv_d)):
            for gg in range(2):
                g = 2 * h2 + gg

                def ev_rv(colh, wi=wi, gg=gg):
                    for f in range(4 * colh, 4 * colh + 4):
                        if wi == 0:
                            P.op("scalar", lambda E, f=f, gg=gg: E.activation(
                                ghalf(f, gg), bank(f), AF.Sigmoid),
                                reads=[rBANK[f]], writes=[rghalf(f, gg)])
                        else:
                            P.op("vector", lambda E, f=f, gg=gg: E.tensor_tensor(
                                ghalf(f, gg), ghalf(f, gg), bank(f), AL.mult),
                                reads=[rBANK[f]], writes=[rghalf(f, gg)])
                grp3p(NCH,
                      lambda colh, ki, wdram=wdram: wdma_raw_h(wdram, 128 * ki, 0, colh, ki),
                      wproc_raw_h,
                      lambda ki, g=g: split_act(chk(BB, ki, g), rBB[ki][g], ki % 2)
                      + (rXSP[ki % 2],),
                      ev_rv)
        for gg in range(2):
            g = 2 * h2 + gg

            def ev_wo(colh, g=g):
                for f in range(4 * colh, 4 * colh + 4):
                    P.op("vector", lambda E, f=f, g=g: E.tensor_tensor(
                        chk(XN, f, g), chk(XN, f, g), bank(f), AL.add),
                        reads=[rBANK[f]], writes=[rXN[f][g]])
            grp3p(NCH,
                  lambda colh, ki: wdma_raw_h(wo_d, 128 * ki, 0, colh, ki),
                  wproc_raw_h,
                  lambda ki, gg=gg: split_act(ghalf(ki, gg), rghalf(ki, gg), ki % 2)
                  + (rXSP[ki % 2],),
                  ev_wo)

    for g in range(NG):
        pump_job(4)
        for c in range(NCH):
            P.op("tensor", lambda E, c=c, g=g: E.matmul(
                PS[0:1, 0:512], ONESC[:, :], chk(XN, c, g),
                start=(c == 0), stop=(c == NCH - 1)),
                reads=[rXN[c][g], rCONST],
                writes=([rBANK[0]] if c in (0, NCH - 1) else []))
        for hf in range(2):
            P.op("scalar", lambda E, hf=hf, g=g: E.activation(
                SC2[:, :].rearrange("p (c t) -> p c t", t=G)[:, 0:4, :],
                span3(XN, 4 * hf, 4, g), AF.Square),
                reads=[rXN[c][g] for c in range(4 * hf, 4 * hf + 4)],
                writes=rSC2[0:4])
            for c in range(4 * hf, 4 * hf + 4):
                P.op("tensor", lambda E, c=c: E.matmul(
                    PS[0:1, 512:1024], ONESC[:, :], SC2[:, 512 * (c % 4):512 * (c % 4 + 1)],
                    start=(c == 0), stop=(c == NCH - 1)),
                    reads=[rSC2[c % 4], rCONST],
                    writes=([rBANK[1]] if c in (0, NCH - 1) else []))
        P.op("vector", lambda E: E.tensor_scalar(RXs[0], PS[0:1, 0:512], 1.0 / C, None, AL.mult),
             reads=[rBANK[0]], writes=[rROWX[0]])
        P.op("vector", lambda E: E.tensor_scalar(RXs[1], PS[0:1, 512:1024], 1.0 / C, None, AL.mult),
             reads=[rBANK[1]], writes=[rROWX[1]])
        P.op("vector", lambda E: E.tensor_tensor(RXs[2], RXs[0], RXs[0], AL.mult),
             reads=[rROWX[0]], writes=[rROWX[2]])
        P.op("vector", lambda E: E.tensor_tensor(RXs[1], RXs[1], RXs[2], AL.subtract),
             reads=[rROWX[2]], writes=[rROWX[1]])
        P.op("vector", lambda E: E.tensor_scalar(RXs[1], RXs[1], EPS_LN, None, AL.add),
             writes=[rROWX[1]])
        P.op("vector", lambda E: E.reciprocal(RXs[1], RXs[1]), writes=[rROWX[1]])
        P.op("scalar", lambda E: E.activation(RXs[1], RXs[1], AF.Sqrt), writes=[rROWX[1]])
        P.op("vector", lambda E: E.tensor_tensor(RXs[2], RXs[0], RXs[1], AL.mult),
             reads=[rROWX[0], rROWX[1]], writes=[rROWX[2]])
        P.op("vector", lambda E: E.tensor_scalar(RXs[2], RXs[2], -1.0, None, AL.mult),
             writes=[rROWX[2]])
        P.op("tensor", lambda E: E.matmul(PS[:, 0:512], ONESR[0:1, :], RXs[1],
                                          start=True, stop=True),
             reads=[rROWX[1], rCONST], writes=[rBANK[0]])
        P.op("vector", lambda E: E.tensor_copy(RBS[0], PS[:, 0:512]),
             reads=[rBANK[0]], writes=[rRB[0]])
        P.op("tensor", lambda E: E.matmul(PS[:, 512:1024], ONESR[0:1, :], RXs[2],
                                          start=True, stop=True),
             reads=[rROWX[2], rCONST], writes=[rBANK[1]])
        P.op("vector", lambda E: E.tensor_copy(RBS[1], PS[:, 512:1024]),
             reads=[rBANK[1]], writes=[rRB[1]])
        for hf in range(2):
            xres = [rXN[c][g] for c in range(4 * hf, 4 * hf + 4)]
            bres = [rBB[c][g] for c in range(4 * hf, 4 * hf + 4)]
            P.op("vector", lambda E, hf=hf, g=g: E.tensor_tensor(
                span3(BB, 4 * hf, 4, g), span3(XN, 4 * hf, 4, g), rbc(RBS[0], 4), AL.mult),
                reads=[rRB[0]] + xres, writes=bres)
            P.op("vector", lambda E, hf=hf, g=g: E.tensor_tensor(
                span3(BB, 4 * hf, 4, g), span3(BB, 4 * hf, 4, g), rbc(RBS[1], 4), AL.add),
                reads=[rRB[1]], writes=bres)
            P.op("vector", lambda E, hf=hf, g=g: E.tensor_tensor(
                span3(BB, 4 * hf, 4, g), span3(BB, 4 * hf, 4, g),
                vbc(vcols("l2w", 4 * hf, 4), 4, G), AL.mult),
                reads=[rCONST], writes=bres)
            P.op("vector", lambda E, hf=hf, g=g: E.tensor_tensor(
                span3(BB, 4 * hf, 4, g), span3(BB, 4 * hf, 4, g),
                vbc(vcols("l2b", 4 * hf, 4), 4, G), AL.add),
                reads=[rCONST], writes=bres)


    P.dma("gpsimd", WSKT[:, :].rearrange("p (u j) -> p u j", j=8),
          wsk_d[:, :].rearrange("(u p) j -> p u j", p=128), writes=[rKV["wskt"]])
    for tc in range(16):
        g = tc // 4
        pump_job(2)
        for c in range(NCH):
            P.op("tensor", lambda E, c=c, tc=tc: E.matmul(
                PS[:, 0:8], BB[:, 2048 * c + 128 * tc:2048 * c + 128 * tc + 128],
                WSKT[:, 8 * c:8 * c + 8],
                start=(c == 0), stop=(c == NCH - 1)),
                reads=[rBB[c][g], rKV["wskt"]],
                writes=([rBANK[0]] if c in (0, NCH - 1) else []))
        P.op("vector", lambda E, tc=tc: E.tensor_copy(ROUT[:, 8 * tc:8 * tc + 8], PS[:, 0:8]),
             reads=[rBANK[0]], writes=([rKV["rout"]] if tc == 15 else []))
    RV = ROUT.rearrange("p (u j) -> p u j", j=8)
    SGv = SIGS.rearrange("p (u j) -> p u j", j=4)

    def rv(j):
        return RV[:, :, j:j + 1].squeeze(2)

    def sg(j):
        return SGv[:, :, j:j + 1].squeeze(2)

    rT = rKV["tmp"]
    P.op("scalar", lambda E: E.activation(SGv, RV[:, :, 0:4], AF.Sigmoid),
         reads=[rKV["rout"]], writes=[rT, rKV["kvsb"]])
    for e in range(3):
        P.op("vector", lambda E, e=e: E.tensor_tensor(TMP1, sg(e), sg(3), AL.mult),
             reads=[rT], writes=[rT])
        P.op("vector", lambda E, e=e: E.tensor_scalar(TMP1, TMP1, CAPS[:, e:e + 1], None, AL.mult),
             reads=[rKV["caps"]], writes=[rT])
        P.op("vector", lambda E, e=e: E.tensor_tensor(BIDS[e], TMP1, rv(4 + e), AL.add),
             reads=[rKV["rout"]], writes=[rT])
    P.op("vector", lambda E: E.tensor_tensor(TMP1, BIDS[0], BIDS[1], AL.is_ge), writes=[rT])
    P.op("vector", lambda E: E.tensor_tensor(TMP2, BIDS[0], BIDS[2], AL.is_ge), writes=[rT])
    P.op("vector", lambda E: E.tensor_tensor(SELS[0], TMP1, TMP2, AL.mult), writes=[rT])
    P.op("vector", lambda E: E.tensor_tensor(TMP1, BIDS[1], BIDS[2], AL.is_ge), writes=[rT])
    P.op("vector", lambda E: E.tensor_scalar(TMP2, SELS[0], -1.0, 1.0, AL.mult, AL.add),
         writes=[rT])
    P.op("vector", lambda E: E.tensor_tensor(SELS[1], TMP2, TMP1, AL.mult), writes=[rT])
    P.op("vector", lambda E: E.tensor_scalar(TMP1, TMP1, -1.0, 1.0, AL.mult, AL.add),
         writes=[rT])
    P.op("vector", lambda E: E.tensor_tensor(SELS[2], TMP2, TMP1, AL.mult), writes=[rT])
    P.op("vector", lambda E: E.tensor_tensor(WCONF, SELS[0], sg(0), AL.mult), writes=[rT])
    P.op("vector", lambda E: E.tensor_tensor(TMP1, SELS[1], sg(1), AL.mult), writes=[rT])
    P.op("vector", lambda E: E.tensor_tensor(WCONF, WCONF, TMP1, AL.add), writes=[rT])
    P.op("vector", lambda E: E.tensor_tensor(TMP1, SELS[2], sg(2), AL.mult), writes=[rT])
    P.op("vector", lambda E: E.tensor_tensor(WCONF, WCONF, TMP1, AL.add), writes=[rT])
    P.op("vector", lambda E: E.tensor_scalar(TMP1, WCONF, 1e-6, None, AL.add), writes=[rT])
    P.op("vector", lambda E: E.reciprocal(TMP1, TMP1), writes=[rT])
    P.op("vector", lambda E: E.tensor_tensor(SCL, WCONF, TMP1, AL.mult), writes=[rT])
    for e in range(3):
        P.op("vector", lambda E, e=e: E.tensor_tensor(MEC[:, 16 * e:16 * e + 16], SELS[e],
                                                      SCL, AL.mult),
             reads=[rT], writes=([rKV["mec"]] if e == 2 else []))

    pump_rest()

    def wset_hi(s):
        return WSTR[:, 1024 * s:1024 * s + 512]

    def wset_lo(s):
        return WSTR[:, 1024 * s + 512:1024 * s + 1024]

    def xsp_pair(pr):
        return (XSP[:, 1024 * pr:1024 * pr + 512],
                XSP[:, 1024 * pr + 512:1024 * pr + 1024])

    def split_act(src_ap, src_res, pr):
        XH, XL = xsp_pair(pr)
        P.op("scalar", lambda E, o=XH, i=src_ap: E.activation(o, i, AF.Copy),
             reads=[src_res], writes=[rXSP[pr]])
        P.op("vector", lambda E, o=XL, i=src_ap, h=XH: E.tensor_tensor(o, i, h, AL.subtract),
             reads=[src_res], writes=[rXSP[pr]])
        return XH, XL

    HRF = [SCR[:, 512 * hh:512 * (hh + 1)] for hh in range(8)]
    for e in range(2):
        for g in range(NG):
            regen_bcast_from_cols(MEC[:, 16 * e:16 * e + 16], rKV["mec"], g, 1, 0)
            for q4 in range(4):
                def ev_k(colh):
                    for j in range(4):
                        hh = 4 * colh + j
                        a = SC2[:, 512 * (2 * (j % 2)):512 * (2 * (j % 2)) + 512]
                        bsc = SC2[:, 512 * (2 * (j % 2) + 1):512 * (2 * (j % 2) + 1) + 512]
                        ra, rb_ = rSC2[2 * (j % 2)], rSC2[2 * (j % 2) + 1]
                        P.op("scalar", lambda E, hh=hh, a=a: E.activation(a, bank(hh), AF.Relu),
                             reads=[rBANK[hh]], writes=[ra])
                        P.op("vector", lambda E, a=a, bsc=bsc: E.tensor_tensor(
                            bsc, a, a, AL.mult),
                            reads=[ra], writes=[rb_])
                        P.op("vector", lambda E, hh=hh, bsc=bsc: E.tensor_tensor(
                            HRF[hh], bsc, RBS[1], AL.mult),
                            reads=[rb_, rRB[1]], writes=[rSCR[hh]])
                grp3p(NCH,
                      lambda colh, ki, e=e, q4=q4: wdma_exp(
                          wk_d[e], 128 * ki, 1024 * q4, colh, ki),
                      wproc_exp,
                      lambda ki, g=g: split_act(chk(BB, ki, g), rBB[ki][g], ki % 2)
                      + (rXSP[ki % 2],),
                      ev_k)

                def ev_v(colh, g=g):
                    for j in range(4):
                        co = 4 * colh + j
                        P.op("vector", lambda E, co=co, g=g: E.tensor_tensor(
                            chk(XN, co, g), chk(XN, co, g), bank(co), AL.add),
                            reads=[rBANK[co]], writes=[rXN[co][g]])
                grp3p(NCH,
                      lambda colh, ki, e=e, q4=q4: wdma_exp(
                          wv_d[e], 128 * (8 * q4 + ki), 0, colh, ki),
                      wproc_exp,
                      lambda ki: split_act(HRF[ki], rSCR[ki], ki % 2) + (rXSP[ki % 2],),
                      ev_v)
    for h2 in range(2):
        for gg in range(2):
            g = 2 * h2 + gg

            def ev_w1(colh, g=g, gg=gg):
                for f in range(4 * colh, 4 * colh + 4):
                    st = SC2[:, 512 * (f % 3):512 * (f % 3) + 512]
                    P.dma("gpsimd", st, s_d[128 * f:128 * (f + 1), G * g:G * (g + 1)],
                          reads=[rDRAM["s_d"]], writes=[rSC2[f % 3]])
                    P.op("vector", lambda E, f=f, gg=gg, st=st: E.tensor_tensor(
                        ghalf(f, gg), bank(f), st, AL.mult),
                        reads=[rBANK[f], rSC2[f % 3]], writes=[rghalf(f, gg)])
            grp3p(NCH,
                  lambda colh, ki: wdma_raw_h(w1_d, 128 * ki, 0, colh, ki),
                  wproc_raw_h,
                  lambda ki, g=g: split_act(chk(BB, ki, g), rBB[ki][g], ki % 2)
                  + (rXSP[ki % 2],),
                  ev_w1)
        for gg in range(2):
            g = 2 * h2 + gg
            regen_bcast_from_cols(MEC[:, 32:48], rKV["mec"], g, 1, 0)

            def ev_w3(colh, g=g):
                for f in range(4 * colh, 4 * colh + 4):
                    st = SC2[:, 512 * (f % 3):512 * (f % 3) + 512]
                    P.op("vector", lambda E, f=f, st=st: E.tensor_tensor(
                        st, bank(f), RBS[1], AL.mult),
                        reads=[rBANK[f], rRB[1]], writes=[rSC2[f % 3]])
                    P.op("vector", lambda E, f=f, g=g, st=st: E.tensor_tensor(
                        chk(XN, f, g), chk(XN, f, g), st, AL.add),
                        reads=[rSC2[f % 3]], writes=[rXN[f][g]])
            grp3p(NCH,
                  lambda colh, ki: wdma_raw_h(w3_d, 128 * ki, 0, colh, ki),
                  wproc_raw_h,
                  lambda ki, gg=gg: split_act(ghalf(ki, gg), rghalf(ki, gg), ki % 2)
                  + (rXSP[ki % 2],),
                  ev_w3)

    if "xpost2" in tap_d:
        for c in range(NCH):
            P.dma("gpsimd", tap_d["xpost2"][128 * c:128 * (c + 1), :],
                  XN[:, 2048 * c:2048 * (c + 1)],
                  reads=[rXN[c][g] for g in range(NG)])
    for c in range(NCH):
        t_a = P.dma("gpsimd", SC2[:, :], accA_d[128 * c:128 * (c + 1), :],
                    reads=[rDRAM["accA_d"]], writes=rSC2[0:4])
        t_x = P.dma("gpsimd", SCR[:, 0:2048], x_d[128 * c:128 * (c + 1), :], writes=rSCR[0:4])
        P.op("vector", lambda E, c=c: E.tensor_tensor(
            XN[:, 2048 * c:2048 * (c + 1)], XN[:, 2048 * c:2048 * (c + 1)], SC2[:, :], AL.add),
            reads=rSC2[0:4], deps=[t_a], writes=[rXN[c][g] for g in range(NG)])
        P.op("vector", lambda E, c=c: E.tensor_tensor(
            XN[:, 2048 * c:2048 * (c + 1)], XN[:, 2048 * c:2048 * (c + 1)],
            SCR[:, 0:2048], AL.add),
            reads=rSCR[0:4], deps=[t_x], writes=[rXN[c][g] for g in range(NG)])
        P.dma("gpsimd", out_d[128 * c:128 * (c + 1), :], XN[:, 2048 * c:2048 * (c + 1)],
              reads=[rXN[c][g] for g in range(NG)])

    def tap_fm(name, buf, rres):
        if name not in tap_d:
            return
        for c in range(NCH):
            P.dma("gpsimd", tap_d[name][128 * c:128 * (c + 1), :],
                  buf[:, 2048 * c:2048 * (c + 1)],
                  reads=[rres[c][g] for g in range(NG)])

    tap_fm("xn", XN, rXN)
    tap_fm("y", BB, rBB)
    tap_fm("x2", XN, rXN)
    tap_fm("h", BB, rBB)
    if "kv" in tap_d:
        P.dma("gpsimd", tap_d["kv"][:, :], KV_SB, reads=[rKV["kvsb"]])
    if "ksum2" in tap_d:
        P.dma("gpsimd", tap_d["ksum2"][:, :], KSUM2[:, :], reads=[rKV["ksum2"]])

    for e, s in (("tensor", "ct"), ("vector", "cv"), ("scalar", "cs"), ("sync", "sq")):
        pass
    for s in ("ct", "cv", "cs", "sq", "aq", "sq0", "sq1", "sqw", "jq"):
        if P.cnt[s]:
            P._wait("gpsimd", (s, P.cnt[s]))
    P._wait("gpsimd", ("gq", P.cnt["gq"]))

    sem_handles = {}
    for sname in ("sq", "ct", "cv", "cs", "gq", "cc", "aq", "sq0", "sq1", "sqw", "jq"):
        sem_handles[sname] = nc.semaphore(sname).__enter__()
    with nc.Block() as block:
        def mk(name):
            def run(Eobj):
                for op in P.ops[name]:
                    if op[0] == "w":
                        Eobj.wait_ge(sem_handles[op[1]], op[2])
                    else:
                        _, fn, sem, inc = op
                        fn(Eobj).then_inc(sem_handles[sem], inc)
            return run
        block.sync(mk("sync"))
        block.tensor(mk("tensor"))
        block.vector(mk("vector"))
        block.scalar(mk("scalar"))
        block.gpsimd(mk("gpsimd"))
    return nc


def prep_inputs(inp, core):
    b, q = core // 4, core % 4
    x = np.asarray(inp["L0_attn_x"])
    xs = np.ascontiguousarray(x[b, 2048 * q:2048 * (q + 1), :].T)
    qkv = np.asarray(inp["L0_attn_qkv_w"])
    d = {
        "x": xs,
        "wq": np.ascontiguousarray(qkv[0:1024].T),
        "wkv": np.ascontiguousarray(qkv[1024:3072].T),
        "wproj": np.ascontiguousarray(np.asarray(inp["L0_attn_proj_w"]).T),
        "wc1": np.ascontiguousarray(np.asarray(inp["L0_attn_conv1_w"]).T),
        "wc2": np.ascontiguousarray(np.asarray(inp["L0_attn_conv2_w"]).T),
        "wr": np.ascontiguousarray(np.asarray(inp["L0_ffn_Wr"])),
        "wvv": np.ascontiguousarray(np.asarray(inp["L0_ffn_Wv"])),
        "wo": np.ascontiguousarray(np.asarray(inp["L0_ffn_Wo"])),
        "wst": np.ascontiguousarray(np.asarray(inp["L0_ffn_Ws"]).T),
        "w2": np.ascontiguousarray(np.asarray(inp["L0_ffn_W2"])),
        "w1": np.ascontiguousarray(np.asarray(inp["L0_ffn_W1"])),
        "w3": np.ascontiguousarray(np.asarray(inp["L0_ffn_W3"])),
        "wk0": np.ascontiguousarray(np.asarray(inp["L0_ffn_K_rwkv"])[0]),
        "wk1": np.ascontiguousarray(np.asarray(inp["L0_ffn_K_rwkv"])[1]),
        "wve0": np.ascontiguousarray(np.asarray(inp["L0_ffn_V_rwkv"])[0]),
        "wve1": np.ascontiguousarray(np.asarray(inp["L0_ffn_V_rwkv"])[1]),
    }
    wsk = np.zeros((1024, 8), np.float32)
    wsk[:, 0:2] = np.asarray(inp["L0_ffn_conf_rwkv"]).T
    wsk[:, 2] = np.asarray(inp["L0_ffn_conf_trans"])
    wsk[:, 3] = np.asarray(inp["L0_ffn_w_diff"])
    wsk[:, 4:7] = np.asarray(inp["L0_ffn_W_aff"])
    d["wsk"] = np.ascontiguousarray(wsk)
    vecs = np.zeros((128, 80), np.float32)
    vsrc = {
        "s1": inp["L0_attn_scale1"], "s2": inp["L0_attn_scale2"], "gamma": inp["L0_attn_gamma"],
        "pb": inp["L0_attn_proj_b"], "cb2": inp["L0_attn_conv2_b"],
        "l1w": inp["L0_ffn_ln1_w"], "l1b": inp["L0_ffn_ln1_b"],
        "l2w": inp["L0_ffn_ln2_w"], "l2b": inp["L0_ffn_ln2_b"],
    }
    VNAMES = ["s1", "s2", "gamma", "pb", "cb2", "l1w", "l1b", "l2w", "l2b", "pad"]
    for i, n in enumerate(VNAMES):
        if n == "pad":
            continue
        vecs[:, 8 * i:8 * i + 8] = np.asarray(vsrc[n]).reshape(8, 128).T
    d["vecs"] = vecs
    d["cb1"] = np.ascontiguousarray(np.asarray(inp["L0_attn_conv1_b"]).reshape(16, 128).T)
    caps = np.zeros((1, 4), np.float32)
    caps[0, 0:3] = np.asarray(inp["L0_ffn_capital_shares"])
    d["caps"] = caps
    d["onesc"] = np.ones((128, 1), np.float32)
    d["onesr"] = np.ones((1, 128), np.float32)
    d["one1"] = np.ones((1, 1), np.float32)
    sel2a = np.zeros((1, 128), np.float32); sel2a[0, 0:64] = 1.0
    sel2b = np.zeros((1, 128), np.float32); sel2b[0, 64:128] = 1.0
    d["sel2a"] = sel2a
    d["sel2b"] = sel2b
    d["i128"] = np.eye(128, dtype=np.float32)
    return d


def kernel(**inputs):
    nc = build_nc(taps=DEBUG_TAPS)
    in_maps = [prep_inputs(inputs, i) for i in range(8)]
    res = run_bass_kernel_spmd(nc, in_maps, core_ids=list(range(8)))
    x = np.asarray(inputs["L0_attn_x"])
    y = np.zeros_like(x)
    for i in range(8):
        b, q = i // 4, i % 4
        y[b, 2048 * q:2048 * (q + 1), :] = res.results[i]["out"].T
    if DEBUG_TAPS:
        return y, res
    return y


# version 13
# speedup vs baseline: 1.0138x; 1.0109x over previous
import numpy as np
import concourse.bass as bass
import concourse.mybir as mybir
from concourse.bass_utils import run_bass_kernel_spmd

F32 = mybir.dt.float32
F32R = mybir.dt.float32r
AL = mybir.AluOpType
AF = mybir.ActivationFunctionType

C = 1024
T = 2048
NCH = 8
G = 512
NG = T // G
SQRTC = float(np.sqrt(1024.0))
EPS_RMS = 1e-6
EPS_LN = 1e-5

ENGS = ("sync", "tensor", "vector", "scalar", "gpsimd")
SEM_OF = dict(sync="sq", tensor="ct", vector="cv", scalar="cs", gpsimd="gq")

DEBUG_TAPS = ()


class Res:
    __slots__ = ("w", "rd")

    def __init__(self):
        self.w = None
        self.rd = []


class Prog:
    def __init__(self):
        self.ops = {e: [] for e in ENGS}
        self.cnt = dict(sq=0, ct=0, cv=0, cs=0, gq=0, cc=0, aq=0,
                        sq0=0, sq1=0, sqw=0, jq=0)
        self.lastdma = {}
        self.lastw = {e: {} for e in ENGS}

    def _wait(self, eng, tok):
        if tok is None:
            return
        s, v = tok
        if v <= self.lastw[eng].get(s, 0):
            return
        self.lastw[eng][s] = v
        self.ops[eng].append(("w", s, v))

    def op(self, eng, fn, reads=(), writes=(), deps=(), sem=None, inc=1):
        if sem is None and inc == 16:
            sem = SEM_OF[eng]
        if inc == 16 and sem in ("gq", "sqw"):
            if sem in self.lastdma:
                deps = list(deps) + [self.lastdma[sem]]
        want = {}
        for r in reads:
            if r.w is not None:
                want[r.w[0]] = max(want.get(r.w[0], 0), r.w[1])
        for r in writes:
            if r.w is not None and not r.rd:
                want[r.w[0]] = max(want.get(r.w[0], 0), r.w[1])
            for t in r.rd:
                want[t[0]] = max(want.get(t[0], 0), t[1])
        for d in deps:
            if d is not None:
                want[d[0]] = max(want.get(d[0], 0), d[1])
        for s_, v_ in want.items():
            self._wait(eng, (s_, v_))
        if sem is None:
            sem = SEM_OF[eng]
        self.cnt[sem] += inc
        tok = (sem, self.cnt[sem])
        self.ops[eng].append(("o", fn, sem, inc))
        for r in reads:
            r.rd.append(tok)
        for r in writes:
            r.w = tok
            r.rd = []
        if inc == 16 and sem in ("gq", "sqw"):
            self.lastdma[sem] = tok
        return tok

    def dma(self, eng, out, in_, reads=(), writes=(), deps=()):
        return self.op(eng, lambda E, o=out, i=in_: E.dma_start(out=o, in_=i),
                       reads=reads, writes=writes, deps=deps, inc=16)


def build_nc(taps=()):
    nc = bass.Bass("TRN2", target_bir_lowering=False)
    P = Prog()

    dI = lambda n, s: nc.dram_tensor(n, s, F32, kind="ExternalInput").ap()
    x_d = dI("x", [C, T])
    wq_d = dI("wq", [C, C])
    wkv_d = dI("wkv", [C, 2 * C])
    wproj_d = dI("wproj", [C, C])
    wc1_d = dI("wc1", [C, 2048])
    wc2_d = dI("wc2", [2048, C])
    wr_d = dI("wr", [C, C])
    wvv_d = dI("wvv", [C, C])
    wo_d = dI("wo", [C, C])
    wst_d = dI("wst", [C, C])
    w2_d = dI("w2", [C, C])
    w1_d = dI("w1", [C, C])
    w3_d = dI("w3", [C, C])
    wk_d = [dI("wk0", [C, 4096]), dI("wk1", [C, 4096])]
    wv_d = [dI("wve0", [4096, C]), dI("wve1", [4096, C])]
    wsk_d = dI("wsk", [C, 8])
    vecs_d = dI("vecs", [128, 80])
    cb1_d = dI("cb1", [128, 16])
    caps_d = dI("caps", [1, 4])
    onesc_d = dI("onesc", [128, 1])
    onesr_d = dI("onesr", [1, 128])
    one1_d = dI("one1", [1, 1])
    sel2a_d = dI("sel2a", [1, 128])
    sel2b_d = dI("sel2b", [1, 128])
    i128_d = dI("i128", [128, 128])

    out_d = nc.dram_tensor("out", [C, T], F32, kind="ExternalOutput").ap()
    tap_d = {}
    for tname, shape in taps:
        tap_d[tname] = nc.dram_tensor("tap_" + tname, shape, F32, kind="ExternalOutput").ap()

    cc_in = nc.dram_tensor("cc_in", [130, 512], F32, kind="Internal").ap()
    cc_out = nc.dram_tensor("cc_out", [130, 512], F32, kind="Internal").ap()
    accA_d = nc.dram_tensor("accA_d", [C, T], F32, kind="Internal").ap()
    s_d = nc.dram_tensor("s_d", [C, T], F32, kind="Internal").ap()
    wsw2p_d = nc.dram_tensor("wsw2p", [C, 2 * C], F32R, kind="Internal").ap()

    A = lambda n, s: nc.alloc_sbuf_tensor(n, s, F32).ap()
    XN = A("XN", [128, 16384])
    BB = A("BB", [128, 16384])
    SCR = A("SCR", [128, 8192])
    WST = A("WST", [128, 1024])
    SC2 = A("SC2", [128, 2048])
    RB = A("RB", [128, 1024])
    KVB = A("KVB", [128, 1780])
    ROWX = A("ROWX", [1, 1536])
    AR = lambda n, s: nc.alloc_sbuf_tensor(n, s, F32R).ap()
    XSP = AR("XSP", [128, 2048])
    WSTR = AR("WSTR", [128, 2048])
    VEC = A("VEC", [128, 80])
    CB1 = A("CB1", [128, 16])
    CAPS = A("CAPS", [128, 4])
    ONESC = A("ONESC", [128, 1])
    ONESR = A("ONESR", [1, 128])
    ONE1 = A("ONE1", [1, 1])
    SELT2 = A("SELT2", [66, 128])
    I128 = A("I128", [128, 128])
    BCOL = A("BCOL", [128, 40])
    WSKT = A("WSKT", [128, 64])
    PS = nc.alloc_psum_tensor("PS", [128, 4096]).ap()

    KV_PHIK = KVB[:, 0:512]
    KV_V = KVB[:, 512:1024]
    KV_SB = KVB[:, 1024:1536]
    KSUM2 = KVB[:, 1536:1552]
    R1C = KVB[:, 1552:1568]
    R1SQC = KVB[:, 1568:1584]
    R1NC = KVB[:, 1584:1600]
    MEC = KVB[:, 1600:1648]
    ROUT = KVB[:, 1648:1776]
    CAPS_STG = KVB[0:1, 1776:1780]
    SIGS = KVB[:, 1024:1088]
    BIDS = [KVB[:, 1088 + 16 * e:1104 + 16 * e] for e in range(3)]
    TMP1 = KVB[:, 1136:1152]
    TMP2 = KVB[:, 1152:1168]
    SELS = [KVB[:, 1168 + 16 * e:1184 + 16 * e] for e in range(3)]
    WCONF = KVB[:, 1216:1232]
    SCL = KVB[:, 1232:1248]

    rXN = [[Res() for _ in range(NG)] for _ in range(NCH)]
    rBB = [[Res() for _ in range(NG)] for _ in range(NCH)]
    rSCR = [Res() for _ in range(16)]
    rWST = [Res() for _ in range(2)]
    rSC2 = [Res() for _ in range(4)]
    rRB = [Res() for _ in range(2)]
    rKV = dict(phik=Res(), v=Res(), kvsb=Res(), ksum2=Res(), r1c=Res(), r1sqc=Res(),
               r1nc=Res(), mec=Res(), rout=Res(), caps=Res(), tmp=Res(), wskt=Res())
    rROWX = [Res() for _ in range(3)]
    rXSP = [Res(), Res()]
    rWSRh = [Res(), Res()]
    rWSRl = [Res(), Res()]
    rBANK = [Res() for _ in range(8)]
    rCONST = Res()
    rBCOL = [Res() for _ in range(5)]
    rDRAM = {n: Res() for n in ("cc_in", "cc_out", "accA_d", "s_d", "wsw2h",
                                "khi0", "klo0", "vhi0", "vlo0", "khi1", "klo1", "vhi1", "vlo1")}

    def bank(b):
        return PS[:, 512 * b:512 * (b + 1)]

    VNAMES = ["s1", "s2", "gamma", "pb", "cb2", "l1w", "l1b", "l2w", "l2b", "pad"]
    VCOL = {n: 8 * i for i, n in enumerate(VNAMES)}

    def vcols(n, c0=0, n_=8):
        return VEC[:, VCOL[n] + c0:VCOL[n] + c0 + n_]

    def vcol(n, c):
        return VEC[:, VCOL[n] + c:VCOL[n] + c + 1]

    def chk(buf, c, g):
        return buf[:, 2048 * c + G * g: 2048 * c + G * g + G]

    def span3(buf, c0, n, g):
        return buf[:, :].rearrange("p (c t) -> p c t", t=2048)[:, c0:c0 + n, G * g:G * (g + 1)]

    def vbc(vap, n, w):
        return vap.unsqueeze(2).to_broadcast([128, n, w])

    def rbc(slot, n):
        return slot.unsqueeze(1).to_broadcast([128, n, G])

    RBS = [RB[:, 512 * i:512 * (i + 1)] for i in range(2)]
    RXs = [ROWX[0:1, 512 * i:512 * (i + 1)] for i in range(3)]

    ld = []
    for dst, src in ((VEC[:, :], vecs_d[:, :]), (CB1[:, :], cb1_d[:, :]),
                     (ONESC[:, :], onesc_d[:, :]), (ONESR[:, :], onesr_d[:, :]),
                     (ONE1[:, :], one1_d[:, :]),
                     (SELT2[0:1, :], sel2a_d[:, :]), (SELT2[1:2, :], sel2b_d[:, :]),
                     (SELT2[32:33, :], sel2a_d[:, :]), (SELT2[33:34, :], sel2b_d[:, :]),
                     (SELT2[64:65, :], sel2a_d[:, :]), (SELT2[65:66, :], sel2b_d[:, :]),
                     (I128[:, :], i128_d[:, :]), (CAPS_STG, caps_d[:, :])):
        ld.append(P.dma("gpsimd", dst, src))
    rCONST.w = ld[-1]
    for c in range(NCH):
        P.dma("gpsimd", XN[:, 2048 * c:2048 * (c + 1)], x_d[128 * c:128 * (c + 1), :],
              writes=[rXN[c][g] for g in range(NG)])
    P.op("tensor", lambda E: E.matmul(PS[:, 0:4], ONESR[0:1, :], CAPS_STG,
                                      start=True, stop=True),
         reads=[rCONST], writes=[rBANK[0]], deps=ld)
    P.op("vector", lambda E: E.tensor_copy(CAPS[:, :], PS[:, 0:4]),
         reads=[rBANK[0]], writes=[rKV["caps"]])

    for g in range(NG):
        for c in range(NCH):
            P.op("tensor", lambda E, c=c, g=g: E.matmul(
                PS[0:1, 0:512], ONESC[:, :], chk(XN, c, g),
                start=(c == 0), stop=(c == NCH - 1)),
                reads=[rXN[c][g], rCONST],
                writes=([rBANK[0]] if c in (0, NCH - 1) else []))
        for hf in range(2):
            P.op("scalar", lambda E, hf=hf, g=g: E.activation(
                SC2[:, :].rearrange("p (c t) -> p c t", t=G)[:, 0:4, :],
                span3(XN, 4 * hf, 4, g), AF.Square),
                reads=[rXN[c][g] for c in range(4 * hf, 4 * hf + 4)],
                writes=rSC2[0:4])
            for c in range(4 * hf, 4 * hf + 4):
                P.op("tensor", lambda E, c=c, g=g: E.matmul(
                    PS[0:1, 512:1024], ONESC[:, :], SC2[:, 512 * (c % 4):512 * (c % 4 + 1)],
                    start=(c == 0), stop=(c == NCH - 1)),
                    reads=[rSC2[c % 4], rCONST],
                    writes=([rBANK[1]] if c in (0, NCH - 1) else []))
        P.op("vector", lambda E: E.tensor_scalar(RXs[0], PS[0:1, 0:512], 1.0 / C, None, AL.mult),
             reads=[rBANK[0]], writes=[rROWX[0]])
        P.op("vector", lambda E: E.tensor_scalar(RXs[1], PS[0:1, 512:1024], 1.0 / C, None, AL.mult),
             reads=[rBANK[1]], writes=[rROWX[1]])
        P.op("vector", lambda E: E.tensor_tensor(RXs[2], RXs[0], RXs[0], AL.mult),
             reads=[rROWX[0]], writes=[rROWX[2]])
        P.op("tensor", lambda E: E.matmul(bank(2), ONESR[0:1, :], RXs[0], start=True, stop=True),
             reads=[rROWX[0], rCONST], writes=[rBANK[2]])
        P.op("vector", lambda E: E.tensor_copy(RBS[0], bank(2)), reads=[rBANK[2]], writes=[rRB[0]])
        P.op("vector", lambda E: E.tensor_tensor(RXs[1], RXs[1], RXs[2], AL.subtract),
             reads=[rROWX[2]], writes=[rROWX[1]])
        P.op("vector", lambda E: E.tensor_scalar(RXs[2], RXs[1], EPS_LN, None, AL.add),
             reads=[rROWX[1]], writes=[rROWX[2]])
        P.op("vector", lambda E: E.reciprocal(RXs[0], RXs[2]),
             reads=[rROWX[2]], writes=[rROWX[0]])
        P.op("scalar", lambda E: E.activation(RXs[2], RXs[0], AF.Sqrt),
             reads=[rROWX[0]], writes=[rROWX[2]])
        P.op("vector", lambda E: E.tensor_tensor(RXs[0], RXs[1], RXs[0], AL.mult),
             reads=[rROWX[1]], writes=[rROWX[0]])
        kq = ((SQRTC + EPS_RMS) / SQRTC) ** 2
        P.op("vector", lambda E: E.reciprocal(RXs[1], RXs[0]),
             reads=[rROWX[0]], writes=[rROWX[1]])
        t_r1row = P.op("scalar", lambda E: E.activation(RXs[1], RXs[1], AF.Sqrt, scale=kq),
                       reads=[], writes=[rROWX[1]])
        for u in range(4):
            P.op("tensor", lambda E, u=u: E.matmul(
                PS[0:128, 2048 + u:2048 + u + 1], RXs[1][0:1, 128 * u:128 * (u + 1)],
                ONE1[0:1, :], start=(u == 0), stop=(u == 3)),
                reads=[rROWX[1], rCONST],
                writes=([rBANK[4]] if u in (0, 3) else []))
        P.op("vector", lambda E, g=g: E.tensor_copy(R1C[:, 4 * g:4 * g + 4], PS[:, 2048:2052]),
             reads=[rBANK[4]], writes=([rKV["r1c"]] if g == NG - 1 else []))
        P.op("vector", lambda E: E.tensor_scalar(RXs[0], RXs[0], EPS_LN, None, AL.add),
             writes=[rROWX[0]])
        P.op("vector", lambda E: E.reciprocal(RXs[0], RXs[0]), writes=[rROWX[0]])
        P.op("scalar", lambda E: E.activation(RXs[0], RXs[0], AF.Sqrt), writes=[rROWX[0]])
        for u in range(4):
            P.op("tensor", lambda E, u=u: E.matmul(
                PS[0:128, 2560 + u:2560 + u + 1], RXs[0][0:1, 128 * u:128 * (u + 1)],
                ONE1[0:1, :], start=(u == 0), stop=(u == 3)),
                reads=[rROWX[0], rCONST],
                writes=([rBANK[5]] if u in (0, 3) else []))
        P.op("vector", lambda E, g=g: E.tensor_copy(R1NC[:, 4 * g:4 * g + 4], PS[:, 2560:2564]),
             reads=[rBANK[5]], writes=([rKV["r1nc"]] if g == NG - 1 else []))
        P.op("tensor", lambda E: E.matmul(bank(3), ONESR[0:1, :], RXs[2], start=True, stop=True),
             reads=[rROWX[2], rCONST], writes=[rBANK[3]])
        P.op("vector", lambda E: E.tensor_copy(RBS[1], bank(3)), reads=[rBANK[3]], writes=[rRB[1]])
        for hf in range(2):
            cres = [rXN[c][g] for c in range(4 * hf, 4 * hf + 4)]
            P.op("vector", lambda E, hf=hf, g=g: E.tensor_tensor(
                span3(XN, 4 * hf, 4, g), span3(XN, 4 * hf, 4, g), rbc(RBS[0], 4), AL.subtract),
                reads=[rRB[0]], writes=cres)
            P.op("vector", lambda E, hf=hf, g=g: E.tensor_tensor(
                span3(XN, 4 * hf, 4, g), span3(XN, 4 * hf, 4, g), rbc(RBS[1], 4), AL.mult),
                reads=[rRB[1]], writes=cres)
    P.op("vector", lambda E: E.tensor_tensor(R1SQC[:, :], R1C[:, :], R1C[:, :], AL.mult),
         reads=[rKV["r1c"]], writes=[rKV["r1sqc"]])

    WKH = SCR[:, 0:4096]
    WVH = SCR[:, 4096:8192]
    kv_mm_tok = []
    for p in range(2):
        t_wk = []
        for half2 in range(2):
            t_wk.append(P.op("sync", lambda E, half2=half2, p=p: E.dma_start(
                out=WKH[:, 2048 * half2:2048 * (half2 + 1)].rearrange("p (u f) -> p u f", f=512),
                in_=wkv_d[:, 512 * p:512 * (p + 1)]
                .rearrange("(u p) f -> p u f", p=128)[:, 4 * half2:4 * half2 + 4, :]),
                writes=rSCR[4 * half2:4 * half2 + 4], inc=16, sem="sqw"))
            t_wk.append(P.op("sync", lambda E, half2=half2, p=p: E.dma_start(
                out=WVH[:, 2048 * half2:2048 * (half2 + 1)].rearrange("p (u f) -> p u f", f=512),
                in_=wkv_d[:, 1024 + 512 * p:1024 + 512 * (p + 1)]
                .rearrange("(u p) f -> p u f", p=128)[:, 4 * half2:4 * half2 + 4, :]),
                writes=rSCR[8 + 4 * half2:8 + 4 * half2 + 4], inc=16, sem="sqw"))
        t_sk = P.op("vector", lambda E: E.tensor_tensor(
            WKH.rearrange("p (c f) -> p c f", f=512), WKH.rearrange("p (c f) -> p c f", f=512),
            vbc(vcols("s1"), 8, 512), AL.mult),
            reads=[rCONST], writes=rSCR[0:4], deps=t_wk)
        t_sv = P.op("vector", lambda E: E.tensor_tensor(
            WVH.rearrange("p (c f) -> p c f", f=512), WVH.rearrange("p (c f) -> p c f", f=512),
            vbc(vcols("s1"), 8, 512), AL.mult),
            reads=[rCONST], writes=rSCR[8:12], deps=t_wk)
        for tc in range(16):
            g, tsub = tc // 4, tc % 4
            xres = [rXN[c][g] for c in range(NCH)]
            for c in range(NCH):
                P.op("tensor", lambda E, c=c, tc=tc: E.matmul(
                    bank(0),
                    XN[:, 2048 * c + 128 * tc:2048 * c + 128 * tc + 128],
                    WKH[:, 512 * c:512 * (c + 1)],
                    start=(c == 0), stop=(c == NCH - 1)),
                    reads=[rXN[c][g], rSCR[c]],
                    writes=([rBANK[0]] if c in (0, NCH - 1) else []))
            P.op("scalar", lambda E, tc=tc: E.activation(
                SC2[:, 0:512], bank(0), AF.Square, scale=R1C[:, tc:tc + 1]),
                reads=[rBANK[0], rKV["r1c"]], writes=[rSC2[0]])
            P.op("scalar", lambda E: E.activation(KV_PHIK, SC2[:, 0:512], AF.Exp, scale=-0.5),
                 reads=[rSC2[0]], writes=[rKV["phik"]])
            for c in range(NCH):
                P.op("tensor", lambda E, c=c, tc=tc: E.matmul(
                    bank(1),
                    XN[:, 2048 * c + 128 * tc:2048 * c + 128 * tc + 128],
                    WVH[:, 512 * c:512 * (c + 1)],
                    start=(c == 0), stop=(c == NCH - 1)),
                    reads=[rXN[c][g], rSCR[8 + c]],
                    writes=([rBANK[1]] if c in (0, NCH - 1) else []))
            P.op("scalar", lambda E, tc=tc: E.activation(
                KV_V, bank(1), AF.Copy, scale=R1C[:, tc:tc + 1]),
                reads=[rBANK[1], rKV["r1c"]], writes=[rKV["v"]])
            for hl in range(8):
                P.op("tensor", lambda E, hl=hl, tc=tc: E.matmul(
                    PS[64 * (hl % 2):64 * (hl % 2) + 64,
                       512 * (4 + hl // 2):512 * (4 + hl // 2) + 64],
                    KV_PHIK[:, 64 * hl:64 * hl + 64],
                    KV_V[:, 64 * hl:64 * hl + 64],
                    start=(tc == 0), stop=(tc == 15)),
                    reads=[rKV["phik"], rKV["v"]],
                    writes=([rBANK[4 + hl // 2]] if tc in (0, 15) else []))
            P.op("tensor", lambda E, tc=tc: E.matmul(
                PS[0:1, 1024:1536], ONESC[:, :], KV_PHIK,
                start=(tc == 0), stop=(tc == 15)),
                reads=[rKV["phik"], rCONST],
                writes=([rBANK[2]] if tc in (0, 15) else []))
        for j in range(4):
            P.op("vector", lambda E, j=j, p=p: E.tensor_copy(
                KV_SB[:, 256 * p + 64 * j:256 * p + 64 * j + 64],
                PS[:, 512 * (4 + j):512 * (4 + j) + 64]),
                reads=[rBANK[4 + j]], writes=([rKV["kvsb"]] if p == 1 and j == 3 else []))
        P.op("vector", lambda E, p=p: E.tensor_copy(RXs[p], PS[0:1, 1024:1536]),
             reads=[rBANK[2]], writes=[rROWX[p]])
    t1 = P.dma("gpsimd", cc_in[0:128, :], KV_SB, reads=[rKV["kvsb"]], writes=[rDRAM["cc_in"]])
    t2 = P.dma("gpsimd", cc_in[128:129, :], RXs[0], reads=[rROWX[0]])
    t3 = P.dma("gpsimd", cc_in[129:130, :], RXs[1], reads=[rROWX[1]])
    t_cc = P.op("gpsimd", lambda E: E.collective_compute(
        "AllReduce", AL.add,
        replica_groups=[[0, 1, 2, 3], [4, 5, 6, 7]],
        ins=[cc_in[:, :]], outs=[cc_out[:, :]]),
        deps=[t1, t2, t3], sem="cc", inc=1)
    t_kvret = P.dma("gpsimd", KV_SB, cc_out[0:128, :], writes=[rKV["kvsb"]], deps=[t_cc])
    t_ks0 = P.dma("gpsimd", RXs[0], cc_out[128:129, :], writes=[rROWX[0]], deps=[t_cc])
    t_ks1 = P.dma("gpsimd", RXs[1], cc_out[129:130, :], writes=[rROWX[1]], deps=[t_cc])
    def wset_hi(s):
        return WSTR[:, 1024 * s:1024 * s + 512]

    def wset_lo(s):
        return WSTR[:, 1024 * s + 512:1024 * s + 1024]

    def xsp_pair(pr):
        return (XSP[:, 1024 * pr:1024 * pr + 512],
                XSP[:, 1024 * pr + 512:1024 * pr + 1024])

    def split_act(src_ap, src_res, pr, scale_col=None):
        XH, XL = xsp_pair(pr)
        if scale_col is None:
            P.op("scalar", lambda E, o=XH, i=src_ap: E.activation(o, i, AF.Copy),
                 reads=[src_res], writes=[rXSP[pr]])
            P.op("vector", lambda E, o=XL, i=src_ap, h=XH: E.tensor_tensor(o, i, h, AL.subtract),
                 reads=[src_res], writes=[rXSP[pr]])
        else:
            tb = SC2[:, 1536:2048]
            P.op("vector", lambda E, i=src_ap, sc=scale_col: E.tensor_scalar(
                tb, i, sc, None, AL.mult),
                reads=[src_res, rCONST], writes=[rSC2[3]])
            P.op("scalar", lambda E, o=XH: E.activation(o, tb, AF.Copy),
                 reads=[rSC2[3]], writes=[rXSP[pr]])
            P.op("vector", lambda E, o=XL, h=XH: E.tensor_tensor(o, tb, h, AL.subtract),
                 reads=[rSC2[3]], writes=[rXSP[pr]])
        return XH, XL

    def wdma_raw_h(wdram, r0, cols0, colh, ki):
        s = colh
        stg = WST[:, 512 * s:512 * s + 512]
        P.op("sync", lambda E, w=wdram, r0=r0, c0=cols0 + 512 * colh, stg=stg: E.dma_start(
            out=stg, in_=w[r0:r0 + 128, c0:c0 + 512]),
            writes=[rWST[s]], inc=16, sem="sqw")

    def wproc_raw_h(colh, ki):
        s = colh
        stg = WST[:, 512 * s:512 * s + 512]
        P.op("scalar", lambda E, s=s, stg=stg: E.activation(wset_hi(s), stg, AF.Copy),
             reads=[rWST[s]], writes=[rWSRh[s]])
        P.op("vector", lambda E, s=s, stg=stg: E.tensor_tensor(
            wset_lo(s), stg, wset_hi(s), AL.subtract),
            reads=[rWST[s], rWSRh[s]], writes=[rWSRl[s]])

    EXPSTG = [WST[:, 0:512], WST[:, 512:1024],
              SCR[:, 4096:4608], SCR[:, 4608:5120]]
    EXPRES_IDX = [("wst", 0), ("wst", 1), ("scr", 8), ("scr", 9)]

    def _expres(st):
        kind, i = EXPRES_IDX[st]
        return rWST[i] if kind == "wst" else rSCR[i]

    def wdma_exp(wdram, r0, cols0, colh, ki):
        st = 2 * (ki % 2) + colh
        P.op("sync", lambda E, w=wdram, r0=r0, c0=cols0 + 512 * colh, st=st: E.dma_start(
            out=EXPSTG[st], in_=w[r0:r0 + 128, c0:c0 + 512]),
            writes=[_expres(st)], inc=16, sem="sqw")

    def wproc_exp(colh, ki):
        st = 2 * (ki % 2) + colh
        s = colh
        P.op("scalar", lambda E, s=s, st=st: E.activation(wset_hi(s), EXPSTG[st], AF.Copy),
             reads=[_expres(st)], writes=[rWSRh[s]])
        P.op("vector", lambda E, s=s, st=st: E.tensor_tensor(
            wset_lo(s), EXPSTG[st], wset_hi(s), AL.subtract),
            reads=[_expres(st), rWSRh[s]], writes=[rWSRl[s]])

    def wload_presplit_h(pk_d, r0, cols0, colh, ki, rdep):
        s = colh
        pc0 = 2 * (cols0 + 512 * colh)
        P.op("sync", lambda E, d=pk_d, r0=r0, pc0=pc0, s=s: E.dma_start(
            out=WSTR[:, 1024 * s:1024 * (s + 1)], in_=d[r0:r0 + 128, pc0:pc0 + 1024]),
            reads=[rdep], writes=[rWSRh[s], rWSRl[s]], inc=16, sem="sq%d" % s)

    def mm3h(ki, nk, colh, XH, XL, rx):
        s = colh
        for j in range(4):
            b = 4 * colh + j
            for p_, (lhs, rhs) in enumerate((
                    (wset_hi(s)[:, 128 * j:128 * (j + 1)], XH),
                    (wset_lo(s)[:, 128 * j:128 * (j + 1)], XH),
                    (wset_hi(s)[:, 128 * j:128 * (j + 1)], XL))):
                P.op("tensor", lambda E, b=b, lhs=lhs, rhs=rhs, ki=ki, nk=nk,
                     p_=p_: E.matmul(
                         bank(b), lhs, rhs,
                         start=(ki == 0 and p_ == 0),
                         stop=(ki == nk - 1 and p_ == 2)),
                     reads=[rWSRh[s], rWSRl[s], rx],
                     writes=([rBANK[b]]
                             if (ki == 0 and p_ == 0) or (ki == nk - 1 and p_ == 2)
                             else []))

    def grp3p(nk, wdma_fn, wproc_fn, act_fn, evacq_fn, defer=False):
        wdma_fn(0, 0)
        wdma_fn(1, 0)
        if wproc_fn is not None:
            wproc_fn(0, 0)
        cur = act_fn(0)
        for ki in range(nk):
            if wproc_fn is not None:
                wproc_fn(1, ki)
            nxt = act_fn(ki + 1) if ki + 1 < nk else None
            pump_job(1)
            mm3h(ki, nk, 0, *cur)
            if ki + 1 < nk:
                wdma_fn(0, ki + 1)
            if ki == nk - 1 and not defer:
                evacq_fn(0)
            if wproc_fn is not None and ki + 1 < nk:
                wproc_fn(0, ki + 1)
            pump_job(1)
            mm3h(ki, nk, 1, *cur)
            if ki + 1 < nk:
                wdma_fn(1, ki + 1)
            cur = nxt
            if ki == nk - 1 and not defer:
                evacq_fn(1)
        if defer:
            evacq_fn(0)
            evacq_fn(1)

    job_items = []
    JSTG = KVB[:, 0:512]
    JHI = KVB[:, 512:1024]
    rJOB = Res()
    jst = {"done": 0, "in": 0}

    def pump_job(n=2):
        for _ in range(n):
            if jst["done"] < jst["in"]:
                src, hi, lo, dres = job_items[jst["done"]]
                t_hi = P.op("scalar", lambda E: E.activation(JHI.bitcast(F32R), JSTG, AF.Copy),
                            reads=[rJOB])
                P.op("gpsimd", lambda E, hi=hi: E.dma_start(out=hi, in_=JHI.bitcast(F32R)),
                     deps=[t_hi], reads=[rJOB], inc=16, sem="jq")
                t_lo = P.op("vector", lambda E: E.tensor_tensor(
                    JSTG.bitcast(F32R), JSTG, JHI.bitcast(F32R), AL.subtract),
                    deps=[t_hi], reads=[rJOB], writes=[rJOB])
                t_st = P.op("gpsimd", lambda E, lo=lo: E.dma_start(out=lo, in_=JSTG.bitcast(F32R)),
                            deps=[t_lo], reads=[rJOB], inc=16, sem="jq")
                if dres is not None:
                    dres.w = t_st
                jst["done"] += 1
            if jst["in"] < len(job_items):
                src, hi, lo, dres = job_items[jst["in"]]
                w = [rJOB] + ([rKV["phik"], rKV["v"]] if jst["in"] == 0 else [])
                P.op("gpsimd", lambda E, src=src: E.dma_start(out=JSTG, in_=src),
                     writes=w, inc=16, sem="jq")
                jst["in"] += 1

    def pump_rest():
        while jst["done"] < len(job_items):
            pump_job(2)

    def regen_bcast_from_cols(cols_ap, cols_res, g, rb_i, pbank):
        for u in range(4):
            P.op("tensor", lambda E, u=u, g=g, cols_ap=cols_ap, pbank=pbank: E.matmul(
                PS[0:1, 512 * pbank + 128 * u:512 * pbank + 128 * (u + 1)],
                cols_ap[:, 4 * g + u:4 * g + u + 1], I128[:, :],
                start=(u == 0), stop=(u == 3)),
                reads=[cols_res, rCONST],
                writes=([rBANK[pbank]] if u in (0, 3) else []))
        t_row = P.op("vector", lambda E, pbank=pbank: E.tensor_copy(
            RXs[2], PS[0:1, 512 * pbank:512 * pbank + 512]),
            reads=[rBANK[pbank]], writes=[rROWX[2]])
        P.op("tensor", lambda E, pbank=pbank: E.matmul(
            PS[:, 512 * pbank:512 * pbank + 512], ONESR[0:1, :], RXs[2],
            start=True, stop=True),
            reads=[rROWX[2], rCONST], writes=[rBANK[pbank]])
        return P.op("vector", lambda E, rb_i=rb_i, pbank=pbank: E.tensor_copy(
            RBS[rb_i], PS[:, 512 * pbank:512 * pbank + 512]),
            reads=[rBANK[pbank]], writes=[rRB[rb_i]])

    for fh in range(2):
        for k in range(NCH):
            t_a = P.op("sync", lambda E, k=k: E.dma_start(
                out=WST[:, 0:1024], in_=wst_d[128 * k:128 * (k + 1), :]),
                writes=[rWST[0], rWST[1]], inc=16, sem="sqw")
            t_b = P.op("sync", lambda E, k=k: E.dma_start(
                out=SCR[:, 0:1024], in_=w2_d[128 * k:128 * (k + 1), :]),
                writes=rSCR[0:2], inc=16, sem="sqw")
            for c in range(NCH):
                P.op("tensor", lambda E, c=c, fh=fh, k=k: E.matmul(
                    bank(c), WST[:, 128 * c:128 * (c + 1)],
                    SCR[:, 512 * fh:512 * fh + 512],
                    start=(k == 0), stop=(k == NCH - 1)),
                    reads=[rWST[0], rWST[1], rSCR[fh]],
                    writes=([rBANK[c]] if k in (0, NCH - 1) else []))
        for c in range(NCH):
            sa = SC2[:, 512 * (2 * (c % 2)):512 * (2 * (c % 2)) + 512]
            sb = SC2[:, 512 * (2 * (c % 2) + 1):512 * (2 * (c % 2) + 1) + 512]
            ra, rb_ = rSC2[2 * (c % 2)], rSC2[2 * (c % 2) + 1]
            P.op("vector", lambda E, c=c, sa=sa: E.tensor_copy(sa, bank(c)),
                 reads=[rBANK[c]], writes=[ra])
            P.op("scalar", lambda E, sa=sa, sb=sb: E.activation(sb.bitcast(F32R), sa, AF.Copy),
                 reads=[ra], writes=[rb_])
            P.dma("gpsimd", wsw2p_d[128 * c:128 * (c + 1), 1024 * fh:1024 * fh + 512],
                  sb.bitcast(F32R), reads=[rb_])
            P.op("vector", lambda E, sa=sa, sb=sb: E.tensor_tensor(
                sa.bitcast(F32R), sa, sb.bitcast(F32R), AL.subtract),
                reads=[rb_], writes=[ra])
            P.dma("gpsimd", wsw2p_d[128 * c:128 * (c + 1), 1024 * fh + 512:1024 * fh + 1024],
                  sa.bitcast(F32R), reads=[ra],
                  writes=([rDRAM["wsw2h"]] if fh == 1 and c == NCH - 1 else []))


    for g in range(NG):
        regen_bcast_from_cols(R1SQC, rKV["r1sqc"], g, 0, 0)

        def ev_q(colh, g=g):
            for f in range(4 * colh, 4 * colh + 4):
                st = SC2[:, 512 * (f % 3):512 * (f % 3) + 512]
                P.op("scalar", lambda E, f=f, st=st: E.activation(st, bank(f), AF.Square),
                     reads=[rBANK[f]], writes=[rSC2[f % 3]])
                P.op("vector", lambda E, st=st: E.tensor_tensor(st, st, RBS[0], AL.mult),
                     reads=[rRB[0]], writes=[rSC2[f % 3]])
                P.op("scalar", lambda E, f=f, g=g, st=st: E.activation(
                    chk(BB, f, g), st, AF.Exp, scale=-0.5),
                    reads=[rSC2[f % 3]], writes=[rBB[f][g]])
        grp3p(NCH,
              lambda colh, ki: wdma_raw_h(wq_d, 128 * ki, 0, colh, ki),
              wproc_raw_h,
              lambda ki, g=g: split_act(chk(XN, ki, g), rXN[ki][g], ki % 2,
                                        scale_col=vcol("s1", ki)) + (rXSP[ki % 2],),
              ev_q)

    for par in range(2):
        for j in range(8):
            h = 2 * j + par
            p_, hl = h // 8, h % 8
            P.op("tensor", lambda E, par=par, j=j, hl=hl, p_=p_: E.matmul(
                PS[64 * par:64 * par + 64, 1536 + 2 * j + par:1536 + 2 * j + par + 1],
                RXs[p_][0:1, 64 * hl:64 * hl + 64], ONE1[0:1, :],
                start=(j == 0), stop=(j == 7)),
                reads=[rROWX[p_], rCONST],
                deps=[t_ks0, t_ks1],
                writes=([rBANK[3]] if j in (0, 7) else []))
    P.op("gpsimd", lambda E: E.iota(KSUM2[:, :], [[0, 16]], base=0, channel_multiplier=0,
                                      allow_small_or_imprecise_dtypes=True),
         writes=[rKV["ksum2"]], sem="cc", inc=1)
    for h in range(16):
        P.op("vector", lambda E, h=h, pb=64 * (h % 2): E.tensor_copy(
            KSUM2[pb:pb + 64, h:h + 1], PS[pb:pb + 64, 1536 + h:1537 + h]),
            reads=[rBANK[3]], writes=[rKV["ksum2"]])

    for g in range(NG):
        SBK = [1, 1, 1, 5, 5, 5, 6, 6]
        PBS = [0, 32, 64, 0, 32, 64, 0, 32]
        for cq in range(NCH):
            sb = SBK[cq]
            pb = PBS[cq]
            P.op("tensor", lambda E, cq=cq, g=g, sb=sb, pb=pb: E.matmul(
                PS[pb:pb + 2, 512 * sb:512 * sb + 512], KSUM2[:, 2 * cq:2 * cq + 2],
                chk(BB, cq, g), start=True, stop=True),
                reads=[rKV["ksum2"], rBB[cq][g]], writes=[rBANK[sb]])
        ZB = [SC2[:, 512:1024], SC2[:, 1024:1536], SC2[:, 1536:2048]]
        for i, sb in enumerate((1, 5, 6)):
            P.op("vector", lambda E, i=i, sb=sb: E.tensor_scalar(
                ZB[i], PS[:, 512 * sb:512 * sb + 512], 1e-6, None, AL.add),
                reads=[rBANK[sb]], writes=[rSC2[1 + i]])
            P.op("vector", lambda E, i=i: E.reciprocal(ZB[i], ZB[i]),
                 writes=[rSC2[1 + i]])
        for cq in range(NCH):
            nb = 0 if cq % 2 == 0 else 2
            zb = 3 if cq % 2 == 0 else 4
            zi = cq // 3
            pb = PBS[cq]
            P.op("tensor", lambda E, cq=cq, g=g, nb=nb: E.matmul(
                PS[0:64, 512 * nb:512 * nb + 512], KV_SB[0:64, 64 * cq:64 * cq + 64],
                chk(BB, cq, g)[0:64, :], start=True, stop=True),
                reads=[rKV["kvsb"], rBB[cq][g]], writes=[rBANK[nb]])
            P.op("tensor", lambda E, cq=cq, g=g, nb=nb: E.matmul(
                PS[64:128, 512 * nb:512 * nb + 512], KV_SB[64:128, 64 * cq:64 * cq + 64],
                chk(BB, cq, g)[64:128, :], start=True, stop=True),
                reads=[rKV["kvsb"], rBB[cq][g]], writes=[rBANK[nb]])
            P.op("tensor", lambda E, zb=zb, zi=zi, pb=pb: E.matmul(
                PS[:, 512 * zb:512 * zb + 512], SELT2[pb:pb + 2, :], ZB[zi][pb:pb + 2, :],
                start=True, stop=True),
                reads=[rSC2[1 + zi], rCONST], writes=[rBANK[zb]])
            P.op("vector", lambda E, p=cq % 2, zb=zb: E.tensor_copy(
                RBS[p], PS[:, 512 * zb:512 * zb + 512]),
                reads=[rBANK[zb]], writes=[rRB[cq % 2]])
            P.op("vector", lambda E, cq=cq, g=g, nb=nb, p=cq % 2: E.tensor_tensor(
                chk(BB, cq, g), PS[:, 512 * nb:512 * nb + 512], RBS[p], AL.mult),
                reads=[rBANK[nb], rRB[cq % 2]], writes=[rBB[cq][g]])

    P.op("vector", lambda E: E.tensor_tensor(BCOL[:, 24:32], vcols("gamma"), vcols("pb"), AL.mult),
         reads=[rCONST], writes=[rBCOL[3]])
    for g in range(NG):
        def ev_proj(hf, g=g):
            P.op("vector", lambda E, hf=hf: E.tensor_tensor(
                SC2[:, :].rearrange("p (c t) -> p c t", t=G)[:, 0:4, :],
                PS[:, 2048 * hf:2048 * (hf + 1)].rearrange("p (c t) -> p c t", t=G)[:, 0:4, :],
                vbc(vcols("gamma", 4 * hf, 4), 4, G), AL.mult),
                reads=[rBANK[4 * hf + i] for i in range(4)] + [rCONST], writes=rSC2[0:4])
            P.op("vector", lambda E, hf=hf: E.tensor_tensor(
                SC2[:, :].rearrange("p (c t) -> p c t", t=G)[:, 0:4, :],
                SC2[:, :].rearrange("p (c t) -> p c t", t=G)[:, 0:4, :],
                vbc(BCOL[:, 24 + 4 * hf:24 + 4 * hf + 4], 4, G), AL.add),
                reads=[rBCOL[3]], writes=rSC2[0:4])
            for c in range(4 * hf, 4 * hf + 4):
                P.op("vector", lambda E, c=c, g=g: E.tensor_tensor(
                    chk(BB, c, g), chk(XN, c, g), SC2[:, 512 * (c % 4):512 * (c % 4) + 512],
                    AL.add),
                    reads=[rXN[c][g], rSC2[c % 4]], writes=[rBB[c][g]])
        grp3p(NCH,
              lambda colh, ki: wdma_raw_h(wproj_d, 128 * ki, 0, colh, ki),
              wproc_raw_h,
              lambda ki, g=g: split_act(chk(BB, ki, g), rBB[ki][g], ki % 2) + (rXSP[ki % 2],),
              ev_proj, defer=True)


    P.op("vector", lambda E: E.tensor_tensor(BCOL[:, 32:40], vcols("gamma"), vcols("cb2"), AL.mult),
         reads=[rCONST], writes=[rBCOL[4]])

    def ev_c2(hf, g):
        P.op("vector", lambda E, hf=hf: E.tensor_tensor(
            SC2[:, :].rearrange("p (c t) -> p c t", t=G)[:, 0:4, :],
            PS[:, 2048 * hf:2048 * (hf + 1)].rearrange("p (c t) -> p c t", t=G)[:, 0:4, :],
            vbc(vcols("gamma", 4 * hf, 4), 4, G), AL.mult),
            reads=[rBANK[4 * hf + i] for i in range(4)] + [rCONST], writes=rSC2[0:4])
        P.op("vector", lambda E, hf=hf: E.tensor_tensor(
            SC2[:, :].rearrange("p (c t) -> p c t", t=G)[:, 0:4, :],
            SC2[:, :].rearrange("p (c t) -> p c t", t=G)[:, 0:4, :],
            vbc(BCOL[:, 32 + 4 * hf:32 + 4 * hf + 4], 4, G), AL.add),
            reads=[rBCOL[4]], writes=rSC2[0:4])
        for c in range(4 * hf, 4 * hf + 4):
            P.op("vector", lambda E, c=c, g=g: E.tensor_tensor(
                chk(BB, c, g), chk(BB, c, g), SC2[:, 512 * (c % 4):512 * (c % 4) + 512],
                AL.add),
                reads=[rSC2[c % 4]], writes=[rBB[c][g]])

    for g in range(NG):
        for hf in range(2):
            P.op("scalar", lambda E, hf=hf, g=g: E.activation(
                SC2[:, :].rearrange("p (c t) -> p c t", t=G)[:, 0:4, :],
                span3(BB, 4 * hf, 4, g), AF.Square),
                reads=[rBB[c][g] for c in range(4 * hf, 4 * hf + 4)],
                writes=rSC2[0:4])
            for c in range(4 * hf, 4 * hf + 4):
                P.op("tensor", lambda E, c=c: E.matmul(
                    PS[0:1, 0:512], ONESC[:, :], SC2[:, 512 * (c % 4):512 * (c % 4 + 1)],
                    start=(c == 0), stop=(c == NCH - 1)),
                    reads=[rSC2[c % 4], rCONST],
                    writes=([rBANK[0]] if c in (0, NCH - 1) else []))
        kq2 = (SQRTC + EPS_RMS) ** 2
        P.op("vector", lambda E: E.reciprocal(RXs[0], PS[0:1, 0:512]),
             reads=[rBANK[0]], writes=[rROWX[0]])
        P.op("scalar", lambda E: E.activation(RXs[0], RXs[0], AF.Sqrt, scale=kq2),
             writes=[rROWX[0]])
        P.op("tensor", lambda E: E.matmul(PS[:, 0:512], ONESR[0:1, :], RXs[0],
                                          start=True, stop=True),
             reads=[rROWX[0], rCONST], writes=[rBANK[0]])
        P.op("vector", lambda E: E.tensor_copy(RBS[0], PS[:, 0:512]),
             reads=[rBANK[0]], writes=[rRB[0]])
        for hhalf in range(2):
            def ev_c1(colh, hhalf=hhalf, g=g):
                for f in range(4 * colh, 4 * colh + 4):
                    h = 8 * hhalf + f
                    st = SC2[:, 512 * (f % 3):512 * (f % 3) + 512]
                    P.op("vector", lambda E, f=f, st=st: E.tensor_tensor(
                        st, bank(f), RBS[0], AL.mult),
                        reads=[rBANK[f], rRB[0]], writes=[rSC2[f % 3]])
                    P.op("scalar", lambda E, h=h, st=st: E.activation(
                        SCR[:, 512 * h:512 * (h + 1)], st, AF.Gelu, bias=CB1[:, h:h + 1]),
                        reads=[rSC2[f % 3], rCONST], writes=[rSCR[h]])
            grp3p(NCH,
                  lambda colh, ki, hhalf=hhalf: wdma_raw_h(
                      wc1_d, 128 * ki, 1024 * hhalf, colh, ki),
                  wproc_raw_h,
                  lambda ki, g=g: split_act(chk(BB, ki, g), rBB[ki][g], ki % 2,
                                            scale_col=vcol("s2", ki)) + (rXSP[ki % 2],),
                  ev_c1)
        grp3p(16,
              lambda colh, h: wdma_raw_h(wc2_d, 128 * h, 0, colh, h),
              wproc_raw_h,
              lambda h: split_act(SCR[:, 512 * h:512 * (h + 1)], rSCR[h], h % 2)
              + (rXSP[h % 2],),
              lambda colh: ev_c2(colh, g))

    for c in range(NCH):
        P.dma("gpsimd", accA_d[128 * c:128 * (c + 1), :], BB[:, 2048 * c:2048 * (c + 1)],
              reads=[rBB[c][g] for g in range(NG)],
              writes=([rDRAM["accA_d"]] if c == NCH - 1 else []))


    for g in range(NG):
        regen_bcast_from_cols(R1NC, rKV["r1nc"], g, 1, 0)
        for hf in range(2):
            bres = [rBB[c][g] for c in range(4 * hf, 4 * hf + 4)]
            P.op("vector", lambda E, hf=hf, g=g: E.tensor_tensor(
                span3(BB, 4 * hf, 4, g), span3(XN, 4 * hf, 4, g), rbc(RBS[1], 4), AL.mult),
                reads=[rRB[1]] + [rXN[c][g] for c in range(4 * hf, 4 * hf + 4)],
                writes=bres)
            P.op("vector", lambda E, hf=hf, g=g: E.tensor_tensor(
                span3(BB, 4 * hf, 4, g), span3(BB, 4 * hf, 4, g),
                vbc(vcols("l1w", 4 * hf, 4), 4, G), AL.mult),
                reads=[rCONST], writes=bres)
            P.op("vector", lambda E, hf=hf, g=g: E.tensor_tensor(
                span3(BB, 4 * hf, 4, g), span3(BB, 4 * hf, 4, g),
                vbc(vcols("l1b", 4 * hf, 4), 4, G), AL.add),
                reads=[rCONST], writes=bres)

    for g in range(NG):
        def ev_s(colh, g=g):
            for f in range(4 * colh, 4 * colh + 4):
                st = SC2[:, 512 * (f % 3):512 * (f % 3) + 512]
                P.op("scalar", lambda E, f=f, st=st: E.activation(st, bank(f), AF.Sigmoid),
                     reads=[rBANK[f]], writes=[rSC2[f % 3]])
                P.dma("gpsimd", s_d[128 * f:128 * (f + 1), G * g:G * (g + 1)], st,
                      reads=[rSC2[f % 3]],
                      writes=([rDRAM["s_d"]] if f == NCH - 1 and g == NG - 1 else []))
        grp3p(NCH,
              lambda colh, ki: wload_presplit_h(wsw2p_d, 128 * ki, 0, colh, ki,
                                                rDRAM["wsw2h"]),
              None,
              lambda ki, g=g: split_act(chk(BB, ki, g), rBB[ki][g], ki % 2) + (rXSP[ki % 2],),
              ev_s)

    def ghalf(c, gg):
        return SCR[:, 1024 * c + 512 * gg:1024 * c + 512 * gg + 512]

    def rghalf(c, gg):
        return rSCR[2 * c + gg]

    for h2 in range(2):
        for wi, wdram in ((0, wr_d), (1, wvv_d)):
            for gg in range(2):
                g = 2 * h2 + gg

                def ev_rv(colh, wi=wi, gg=gg):
                    for f in range(4 * colh, 4 * colh + 4):
                        if wi == 0:
                            P.op("scalar", lambda E, f=f, gg=gg: E.activation(
                                ghalf(f, gg), bank(f), AF.Sigmoid),
                                reads=[rBANK[f]], writes=[rghalf(f, gg)])
                        else:
                            P.op("vector", lambda E, f=f, gg=gg: E.tensor_tensor(
                                ghalf(f, gg), ghalf(f, gg), bank(f), AL.mult),
                                reads=[rBANK[f]], writes=[rghalf(f, gg)])
                grp3p(NCH,
                      lambda colh, ki, wdram=wdram: wdma_raw_h(wdram, 128 * ki, 0, colh, ki),
                      wproc_raw_h,
                      lambda ki, g=g: split_act(chk(BB, ki, g), rBB[ki][g], ki % 2)
                      + (rXSP[ki % 2],),
                      ev_rv)
        for gg in range(2):
            g = 2 * h2 + gg

            def ev_wo(colh, g=g):
                for f in range(4 * colh, 4 * colh + 4):
                    P.op("vector", lambda E, f=f, g=g: E.tensor_tensor(
                        chk(XN, f, g), chk(XN, f, g), bank(f), AL.add),
                        reads=[rBANK[f]], writes=[rXN[f][g]])
            grp3p(NCH,
                  lambda colh, ki: wdma_raw_h(wo_d, 128 * ki, 0, colh, ki),
                  wproc_raw_h,
                  lambda ki, gg=gg: split_act(ghalf(ki, gg), rghalf(ki, gg), ki % 2)
                  + (rXSP[ki % 2],),
                  ev_wo)

    for g in range(NG):
        pump_job(4)
        for c in range(NCH):
            P.op("tensor", lambda E, c=c, g=g: E.matmul(
                PS[0:1, 0:512], ONESC[:, :], chk(XN, c, g),
                start=(c == 0), stop=(c == NCH - 1)),
                reads=[rXN[c][g], rCONST],
                writes=([rBANK[0]] if c in (0, NCH - 1) else []))
        for hf in range(2):
            P.op("scalar", lambda E, hf=hf, g=g: E.activation(
                SC2[:, :].rearrange("p (c t) -> p c t", t=G)[:, 0:4, :],
                span3(XN, 4 * hf, 4, g), AF.Square),
                reads=[rXN[c][g] for c in range(4 * hf, 4 * hf + 4)],
                writes=rSC2[0:4])
            for c in range(4 * hf, 4 * hf + 4):
                P.op("tensor", lambda E, c=c: E.matmul(
                    PS[0:1, 512:1024], ONESC[:, :], SC2[:, 512 * (c % 4):512 * (c % 4 + 1)],
                    start=(c == 0), stop=(c == NCH - 1)),
                    reads=[rSC2[c % 4], rCONST],
                    writes=([rBANK[1]] if c in (0, NCH - 1) else []))
        P.op("vector", lambda E: E.tensor_scalar(RXs[0], PS[0:1, 0:512], 1.0 / C, None, AL.mult),
             reads=[rBANK[0]], writes=[rROWX[0]])
        P.op("vector", lambda E: E.tensor_scalar(RXs[1], PS[0:1, 512:1024], 1.0 / C, None, AL.mult),
             reads=[rBANK[1]], writes=[rROWX[1]])
        P.op("vector", lambda E: E.tensor_tensor(RXs[2], RXs[0], RXs[0], AL.mult),
             reads=[rROWX[0]], writes=[rROWX[2]])
        P.op("vector", lambda E: E.tensor_tensor(RXs[1], RXs[1], RXs[2], AL.subtract),
             reads=[rROWX[2]], writes=[rROWX[1]])
        P.op("vector", lambda E: E.tensor_scalar(RXs[1], RXs[1], EPS_LN, None, AL.add),
             writes=[rROWX[1]])
        P.op("vector", lambda E: E.reciprocal(RXs[1], RXs[1]), writes=[rROWX[1]])
        P.op("scalar", lambda E: E.activation(RXs[1], RXs[1], AF.Sqrt), writes=[rROWX[1]])
        P.op("vector", lambda E: E.tensor_tensor(RXs[2], RXs[0], RXs[1], AL.mult),
             reads=[rROWX[0], rROWX[1]], writes=[rROWX[2]])
        P.op("vector", lambda E: E.tensor_scalar(RXs[2], RXs[2], -1.0, None, AL.mult),
             writes=[rROWX[2]])
        P.op("tensor", lambda E: E.matmul(PS[:, 0:512], ONESR[0:1, :], RXs[1],
                                          start=True, stop=True),
             reads=[rROWX[1], rCONST], writes=[rBANK[0]])
        P.op("vector", lambda E: E.tensor_copy(RBS[0], PS[:, 0:512]),
             reads=[rBANK[0]], writes=[rRB[0]])
        P.op("tensor", lambda E: E.matmul(PS[:, 512:1024], ONESR[0:1, :], RXs[2],
                                          start=True, stop=True),
             reads=[rROWX[2], rCONST], writes=[rBANK[1]])
        P.op("vector", lambda E: E.tensor_copy(RBS[1], PS[:, 512:1024]),
             reads=[rBANK[1]], writes=[rRB[1]])
        for hf in range(2):
            xres = [rXN[c][g] for c in range(4 * hf, 4 * hf + 4)]
            bres = [rBB[c][g] for c in range(4 * hf, 4 * hf + 4)]
            P.op("vector", lambda E, hf=hf, g=g: E.tensor_tensor(
                span3(BB, 4 * hf, 4, g), span3(XN, 4 * hf, 4, g), rbc(RBS[0], 4), AL.mult),
                reads=[rRB[0]] + xres, writes=bres)
            P.op("vector", lambda E, hf=hf, g=g: E.tensor_tensor(
                span3(BB, 4 * hf, 4, g), span3(BB, 4 * hf, 4, g), rbc(RBS[1], 4), AL.add),
                reads=[rRB[1]], writes=bres)
            P.op("vector", lambda E, hf=hf, g=g: E.tensor_tensor(
                span3(BB, 4 * hf, 4, g), span3(BB, 4 * hf, 4, g),
                vbc(vcols("l2w", 4 * hf, 4), 4, G), AL.mult),
                reads=[rCONST], writes=bres)
            P.op("vector", lambda E, hf=hf, g=g: E.tensor_tensor(
                span3(BB, 4 * hf, 4, g), span3(BB, 4 * hf, 4, g),
                vbc(vcols("l2b", 4 * hf, 4), 4, G), AL.add),
                reads=[rCONST], writes=bres)


    P.dma("gpsimd", WSKT[:, :].rearrange("p (u j) -> p u j", j=8),
          wsk_d[:, :].rearrange("(u p) j -> p u j", p=128), writes=[rKV["wskt"]])
    for tc in range(16):
        g = tc // 4
        pump_job(2)
        for c in range(NCH):
            P.op("tensor", lambda E, c=c, tc=tc: E.matmul(
                PS[:, 0:8], BB[:, 2048 * c + 128 * tc:2048 * c + 128 * tc + 128],
                WSKT[:, 8 * c:8 * c + 8],
                start=(c == 0), stop=(c == NCH - 1)),
                reads=[rBB[c][g], rKV["wskt"]],
                writes=([rBANK[0]] if c in (0, NCH - 1) else []))
        P.op("vector", lambda E, tc=tc: E.tensor_copy(ROUT[:, 8 * tc:8 * tc + 8], PS[:, 0:8]),
             reads=[rBANK[0]], writes=([rKV["rout"]] if tc == 15 else []))
    RV = ROUT.rearrange("p (u j) -> p u j", j=8)
    SGv = SIGS.rearrange("p (u j) -> p u j", j=4)

    def rv(j):
        return RV[:, :, j:j + 1].squeeze(2)

    def sg(j):
        return SGv[:, :, j:j + 1].squeeze(2)

    rT = rKV["tmp"]
    P.op("scalar", lambda E: E.activation(SGv, RV[:, :, 0:4], AF.Sigmoid),
         reads=[rKV["rout"]], writes=[rT, rKV["kvsb"]])
    for e in range(3):
        P.op("vector", lambda E, e=e: E.tensor_tensor(TMP1, sg(e), sg(3), AL.mult),
             reads=[rT], writes=[rT])
        P.op("vector", lambda E, e=e: E.tensor_scalar(TMP1, TMP1, CAPS[:, e:e + 1], None, AL.mult),
             reads=[rKV["caps"]], writes=[rT])
        P.op("vector", lambda E, e=e: E.tensor_tensor(BIDS[e], TMP1, rv(4 + e), AL.add),
             reads=[rKV["rout"]], writes=[rT])
    P.op("vector", lambda E: E.tensor_tensor(TMP1, BIDS[0], BIDS[1], AL.is_ge), writes=[rT])
    P.op("vector", lambda E: E.tensor_tensor(TMP2, BIDS[0], BIDS[2], AL.is_ge), writes=[rT])
    P.op("vector", lambda E: E.tensor_tensor(SELS[0], TMP1, TMP2, AL.mult), writes=[rT])
    P.op("vector", lambda E: E.tensor_tensor(TMP1, BIDS[1], BIDS[2], AL.is_ge), writes=[rT])
    P.op("vector", lambda E: E.tensor_scalar(TMP2, SELS[0], -1.0, 1.0, AL.mult, AL.add),
         writes=[rT])
    P.op("vector", lambda E: E.tensor_tensor(SELS[1], TMP2, TMP1, AL.mult), writes=[rT])
    P.op("vector", lambda E: E.tensor_scalar(TMP1, TMP1, -1.0, 1.0, AL.mult, AL.add),
         writes=[rT])
    P.op("vector", lambda E: E.tensor_tensor(SELS[2], TMP2, TMP1, AL.mult), writes=[rT])
    P.op("vector", lambda E: E.tensor_tensor(WCONF, SELS[0], sg(0), AL.mult), writes=[rT])
    P.op("vector", lambda E: E.tensor_tensor(TMP1, SELS[1], sg(1), AL.mult), writes=[rT])
    P.op("vector", lambda E: E.tensor_tensor(WCONF, WCONF, TMP1, AL.add), writes=[rT])
    P.op("vector", lambda E: E.tensor_tensor(TMP1, SELS[2], sg(2), AL.mult), writes=[rT])
    P.op("vector", lambda E: E.tensor_tensor(WCONF, WCONF, TMP1, AL.add), writes=[rT])
    P.op("vector", lambda E: E.tensor_scalar(TMP1, WCONF, 1e-6, None, AL.add), writes=[rT])
    P.op("vector", lambda E: E.reciprocal(TMP1, TMP1), writes=[rT])
    P.op("vector", lambda E: E.tensor_tensor(SCL, WCONF, TMP1, AL.mult), writes=[rT])
    for e in range(3):
        P.op("vector", lambda E, e=e: E.tensor_tensor(MEC[:, 16 * e:16 * e + 16], SELS[e],
                                                      SCL, AL.mult),
             reads=[rT], writes=([rKV["mec"]] if e == 2 else []))

    pump_rest()

    def wset_hi(s):
        return WSTR[:, 1024 * s:1024 * s + 512]

    def wset_lo(s):
        return WSTR[:, 1024 * s + 512:1024 * s + 1024]

    def xsp_pair(pr):
        return (XSP[:, 1024 * pr:1024 * pr + 512],
                XSP[:, 1024 * pr + 512:1024 * pr + 1024])

    def split_act(src_ap, src_res, pr):
        XH, XL = xsp_pair(pr)
        P.op("scalar", lambda E, o=XH, i=src_ap: E.activation(o, i, AF.Copy),
             reads=[src_res], writes=[rXSP[pr]])
        P.op("vector", lambda E, o=XL, i=src_ap, h=XH: E.tensor_tensor(o, i, h, AL.subtract),
             reads=[src_res], writes=[rXSP[pr]])
        return XH, XL

    HRF = [SCR[:, 512 * hh:512 * (hh + 1)] for hh in range(8)]
    for e in range(2):
        for g in range(NG):
            regen_bcast_from_cols(MEC[:, 16 * e:16 * e + 16], rKV["mec"], g, 1, 0)
            for q4 in range(4):
                def ev_k(colh):
                    for j in range(4):
                        hh = 4 * colh + j
                        a = SC2[:, 512 * (2 * (j % 2)):512 * (2 * (j % 2)) + 512]
                        bsc = SC2[:, 512 * (2 * (j % 2) + 1):512 * (2 * (j % 2) + 1) + 512]
                        ra, rb_ = rSC2[2 * (j % 2)], rSC2[2 * (j % 2) + 1]
                        P.op("scalar", lambda E, hh=hh, a=a: E.activation(a, bank(hh), AF.Relu),
                             reads=[rBANK[hh]], writes=[ra])
                        P.op("vector", lambda E, a=a, bsc=bsc: E.tensor_tensor(
                            bsc, a, a, AL.mult),
                            reads=[ra], writes=[rb_])
                        P.op("vector", lambda E, hh=hh, bsc=bsc: E.tensor_tensor(
                            HRF[hh], bsc, RBS[1], AL.mult),
                            reads=[rb_, rRB[1]], writes=[rSCR[hh]])
                grp3p(NCH,
                      lambda colh, ki, e=e, q4=q4: wdma_exp(
                          wk_d[e], 128 * ki, 1024 * q4, colh, ki),
                      wproc_exp,
                      lambda ki, g=g: split_act(chk(BB, ki, g), rBB[ki][g], ki % 2)
                      + (rXSP[ki % 2],),
                      ev_k)

                def ev_v(colh, g=g):
                    for j in range(4):
                        co = 4 * colh + j
                        P.op("vector", lambda E, co=co, g=g: E.tensor_tensor(
                            chk(XN, co, g), chk(XN, co, g), bank(co), AL.add),
                            reads=[rBANK[co]], writes=[rXN[co][g]])
                grp3p(NCH,
                      lambda colh, ki, e=e, q4=q4: wdma_exp(
                          wv_d[e], 128 * (8 * q4 + ki), 0, colh, ki),
                      wproc_exp,
                      lambda ki: split_act(HRF[ki], rSCR[ki], ki % 2) + (rXSP[ki % 2],),
                      ev_v)
    for h2 in range(2):
        for gg in range(2):
            g = 2 * h2 + gg

            def ev_w1(colh, g=g, gg=gg):
                for f in range(4 * colh, 4 * colh + 4):
                    st = SC2[:, 512 * (f % 3):512 * (f % 3) + 512]
                    P.dma("gpsimd", st, s_d[128 * f:128 * (f + 1), G * g:G * (g + 1)],
                          reads=[rDRAM["s_d"]], writes=[rSC2[f % 3]])
                    P.op("vector", lambda E, f=f, gg=gg, st=st: E.tensor_tensor(
                        ghalf(f, gg), bank(f), st, AL.mult),
                        reads=[rBANK[f], rSC2[f % 3]], writes=[rghalf(f, gg)])
            grp3p(NCH,
                  lambda colh, ki: wdma_raw_h(w1_d, 128 * ki, 0, colh, ki),
                  wproc_raw_h,
                  lambda ki, g=g: split_act(chk(BB, ki, g), rBB[ki][g], ki % 2)
                  + (rXSP[ki % 2],),
                  ev_w1)
        for gg in range(2):
            g = 2 * h2 + gg
            regen_bcast_from_cols(MEC[:, 32:48], rKV["mec"], g, 1, 0)

            def ev_w3(colh, g=g):
                for f in range(4 * colh, 4 * colh + 4):
                    st = SC2[:, 512 * (f % 3):512 * (f % 3) + 512]
                    P.op("vector", lambda E, f=f, st=st: E.tensor_tensor(
                        st, bank(f), RBS[1], AL.mult),
                        reads=[rBANK[f], rRB[1]], writes=[rSC2[f % 3]])
                    P.op("vector", lambda E, f=f, g=g, st=st: E.tensor_tensor(
                        chk(XN, f, g), chk(XN, f, g), st, AL.add),
                        reads=[rSC2[f % 3]], writes=[rXN[f][g]])
            grp3p(NCH,
                  lambda colh, ki: wdma_raw_h(w3_d, 128 * ki, 0, colh, ki),
                  wproc_raw_h,
                  lambda ki, gg=gg: split_act(ghalf(ki, gg), rghalf(ki, gg), ki % 2)
                  + (rXSP[ki % 2],),
                  ev_w3)

    if "xpost2" in tap_d:
        for c in range(NCH):
            P.dma("gpsimd", tap_d["xpost2"][128 * c:128 * (c + 1), :],
                  XN[:, 2048 * c:2048 * (c + 1)],
                  reads=[rXN[c][g] for g in range(NG)])
    def p14_bufs(c):
        p = c % 2
        return (SCR[:, 4096 + 2048 * p:4096 + 2048 * p + 2048], rSCR[8 + 4 * p:8 + 4 * p + 4],
                SCR[:, 2048 * p:2048 * p + 2048], rSCR[4 * p:4 * p + 4])

    def p14_load(c):
        bufA, rA, bufX, rX = p14_bufs(c)
        P.op("gpsimd", lambda E, c=c, bufA=bufA: E.dma_start(
            out=bufA, in_=accA_d[128 * c:128 * (c + 1), :]),
            reads=[rDRAM["accA_d"]], writes=list(rA), inc=16, sem="gq")
        P.op("sync", lambda E, c=c, bufX=bufX: E.dma_start(
            out=bufX, in_=x_d[128 * c:128 * (c + 1), :]),
            writes=list(rX), inc=16, sem="sqw")

    p14_load(0)
    for c in range(NCH):
        if c + 1 < NCH:
            p14_load(c + 1)
        bufA, rA, bufX, rX = p14_bufs(c)
        P.op("vector", lambda E, c=c, bufA=bufA: E.tensor_tensor(
            XN[:, 2048 * c:2048 * (c + 1)], XN[:, 2048 * c:2048 * (c + 1)], bufA, AL.add),
            reads=list(rA), writes=[rXN[c][g] for g in range(NG)])
        P.op("vector", lambda E, c=c, bufX=bufX: E.tensor_tensor(
            XN[:, 2048 * c:2048 * (c + 1)], XN[:, 2048 * c:2048 * (c + 1)], bufX, AL.add),
            reads=list(rX), writes=[rXN[c][g] for g in range(NG)])
        eng, sm = ("gpsimd", "gq") if c % 2 == 0 else ("sync", "sqw")
        P.op(eng, lambda E, c=c: E.dma_start(
            out=out_d[128 * c:128 * (c + 1), :], in_=XN[:, 2048 * c:2048 * (c + 1)]),
            reads=[rXN[c][g] for g in range(NG)], inc=16, sem=sm)

    def tap_fm(name, buf, rres):
        if name not in tap_d:
            return
        for c in range(NCH):
            P.dma("gpsimd", tap_d[name][128 * c:128 * (c + 1), :],
                  buf[:, 2048 * c:2048 * (c + 1)],
                  reads=[rres[c][g] for g in range(NG)])

    tap_fm("xn", XN, rXN)
    tap_fm("y", BB, rBB)
    tap_fm("x2", XN, rXN)
    tap_fm("h", BB, rBB)
    if "kv" in tap_d:
        P.dma("gpsimd", tap_d["kv"][:, :], KV_SB, reads=[rKV["kvsb"]])
    if "ksum2" in tap_d:
        P.dma("gpsimd", tap_d["ksum2"][:, :], KSUM2[:, :], reads=[rKV["ksum2"]])

    for e, s in (("tensor", "ct"), ("vector", "cv"), ("scalar", "cs"), ("sync", "sq")):
        pass
    for s in ("ct", "cv", "cs", "sq", "aq", "sq0", "sq1", "sqw", "jq"):
        if P.cnt[s]:
            P._wait("gpsimd", (s, P.cnt[s]))
    P._wait("gpsimd", ("gq", P.cnt["gq"]))

    sem_handles = {}
    for sname in ("sq", "ct", "cv", "cs", "gq", "cc", "aq", "sq0", "sq1", "sqw", "jq"):
        sem_handles[sname] = nc.semaphore(sname).__enter__()
    with nc.Block() as block:
        def mk(name):
            def run(Eobj):
                for op in P.ops[name]:
                    if op[0] == "w":
                        Eobj.wait_ge(sem_handles[op[1]], op[2])
                    else:
                        _, fn, sem, inc = op
                        fn(Eobj).then_inc(sem_handles[sem], inc)
            return run
        block.sync(mk("sync"))
        block.tensor(mk("tensor"))
        block.vector(mk("vector"))
        block.scalar(mk("scalar"))
        block.gpsimd(mk("gpsimd"))
    return nc


def prep_inputs(inp, core):
    b, q = core // 4, core % 4
    x = np.asarray(inp["L0_attn_x"])
    xs = np.ascontiguousarray(x[b, 2048 * q:2048 * (q + 1), :].T)
    qkv = np.asarray(inp["L0_attn_qkv_w"])
    d = {
        "x": xs,
        "wq": np.ascontiguousarray(qkv[0:1024].T),
        "wkv": np.ascontiguousarray(qkv[1024:3072].T),
        "wproj": np.ascontiguousarray(np.asarray(inp["L0_attn_proj_w"]).T),
        "wc1": np.ascontiguousarray(np.asarray(inp["L0_attn_conv1_w"]).T),
        "wc2": np.ascontiguousarray(np.asarray(inp["L0_attn_conv2_w"]).T),
        "wr": np.ascontiguousarray(np.asarray(inp["L0_ffn_Wr"])),
        "wvv": np.ascontiguousarray(np.asarray(inp["L0_ffn_Wv"])),
        "wo": np.ascontiguousarray(np.asarray(inp["L0_ffn_Wo"])),
        "wst": np.ascontiguousarray(np.asarray(inp["L0_ffn_Ws"]).T),
        "w2": np.ascontiguousarray(np.asarray(inp["L0_ffn_W2"])),
        "w1": np.ascontiguousarray(np.asarray(inp["L0_ffn_W1"])),
        "w3": np.ascontiguousarray(np.asarray(inp["L0_ffn_W3"])),
        "wk0": np.ascontiguousarray(np.asarray(inp["L0_ffn_K_rwkv"])[0]),
        "wk1": np.ascontiguousarray(np.asarray(inp["L0_ffn_K_rwkv"])[1]),
        "wve0": np.ascontiguousarray(np.asarray(inp["L0_ffn_V_rwkv"])[0]),
        "wve1": np.ascontiguousarray(np.asarray(inp["L0_ffn_V_rwkv"])[1]),
    }
    wsk = np.zeros((1024, 8), np.float32)
    wsk[:, 0:2] = np.asarray(inp["L0_ffn_conf_rwkv"]).T
    wsk[:, 2] = np.asarray(inp["L0_ffn_conf_trans"])
    wsk[:, 3] = np.asarray(inp["L0_ffn_w_diff"])
    wsk[:, 4:7] = np.asarray(inp["L0_ffn_W_aff"])
    d["wsk"] = np.ascontiguousarray(wsk)
    vecs = np.zeros((128, 80), np.float32)
    vsrc = {
        "s1": inp["L0_attn_scale1"], "s2": inp["L0_attn_scale2"], "gamma": inp["L0_attn_gamma"],
        "pb": inp["L0_attn_proj_b"], "cb2": inp["L0_attn_conv2_b"],
        "l1w": inp["L0_ffn_ln1_w"], "l1b": inp["L0_ffn_ln1_b"],
        "l2w": inp["L0_ffn_ln2_w"], "l2b": inp["L0_ffn_ln2_b"],
    }
    VNAMES = ["s1", "s2", "gamma", "pb", "cb2", "l1w", "l1b", "l2w", "l2b", "pad"]
    for i, n in enumerate(VNAMES):
        if n == "pad":
            continue
        vecs[:, 8 * i:8 * i + 8] = np.asarray(vsrc[n]).reshape(8, 128).T
    d["vecs"] = vecs
    d["cb1"] = np.ascontiguousarray(np.asarray(inp["L0_attn_conv1_b"]).reshape(16, 128).T)
    caps = np.zeros((1, 4), np.float32)
    caps[0, 0:3] = np.asarray(inp["L0_ffn_capital_shares"])
    d["caps"] = caps
    d["onesc"] = np.ones((128, 1), np.float32)
    d["onesr"] = np.ones((1, 128), np.float32)
    d["one1"] = np.ones((1, 1), np.float32)
    sel2a = np.zeros((1, 128), np.float32); sel2a[0, 0:64] = 1.0
    sel2b = np.zeros((1, 128), np.float32); sel2b[0, 64:128] = 1.0
    d["sel2a"] = sel2a
    d["sel2b"] = sel2b
    d["i128"] = np.eye(128, dtype=np.float32)
    return d


def kernel(**inputs):
    nc = build_nc(taps=DEBUG_TAPS)
    in_maps = [prep_inputs(inputs, i) for i in range(8)]
    res = run_bass_kernel_spmd(nc, in_maps, core_ids=list(range(8)))
    x = np.asarray(inputs["L0_attn_x"])
    y = np.zeros_like(x)
    for i in range(8):
        b, q = i // 4, i % 4
        y[b, 2048 * q:2048 * (q + 1), :] = res.results[i]["out"].T
    if DEBUG_TAPS:
        return y, res
    return y
